# version 10
# baseline (speedup 1.0000x reference)
import numpy as np
import concourse.bass as bass
import concourse.mybir as mybir
from concourse.bass_utils import run_bass_kernel_spmd
from contextlib import ExitStack

F32 = mybir.dt.float32
BF16 = mybir.dt.bfloat16
AF = mybir.ActivationFunctionType
ALU = mybir.AluOpType
AX = mybir.AxisListType

P = 128
NT = 512
C = 8
D = 1024
H2 = 2048
H4 = 4096
NE = 8
EPS = 1e-5
QUEUES = ("pe", "act", "dve", "pool", "sync")


class Step:
    __slots__ = ("q", "emit", "deps", "inc", "cum", "needs", "tag")

    def __init__(self, q, emit, inc, tag):
        self.q = q
        self.emit = emit
        self.deps = []
        self.inc = inc
        self.cum = None
        self.needs = False
        self.tag = tag


class Buf:
    __slots__ = ("ap", "w", "rs", "name")

    def __init__(self, ap, name=""):
        self.ap = ap
        self.w = None
        self.rs = []
        self.name = name


class Sched:
    def __init__(self):
        self.plan = {q: [] for q in QUEUES}

    def add(self, q, emit, reads=(), writes=(), extra=(), inc=1, dma=False, tag=""):
        st = Step(q, emit, 16 if dma else inc, tag)
        deps = []
        for d in extra:
            if d is not None:
                deps.append(d)
        for b in reads:
            if b.w is not None:
                deps.append(b.w)
        for b in writes:
            if b.w is not None:
                deps.append(b.w)
            deps.extend(b.rs)
        seen = set()
        for d in deps:
            if id(d) not in seen:
                seen.add(id(d))
                st.deps.append(d)
                d.needs = True
        if dma:
            st.needs = True
        for b in reads:
            b.rs.append(st)
        for b in writes:
            b.w = st
            b.rs = []
        self.plan[q].append(st)
        return st

    def finalize(self):
        for q in QUEUES:
            cum = 0
            for st in self.plan[q]:
                if st.needs:
                    cum += st.inc
                    st.cum = cum
                else:
                    st.cum = None


def build(n_p=8, n_m=8, n_e=NE, dbg=False):
    nc = bass.Bass("TRN2", target_bir_lowering=False, debug=False)
    S = Sched()

    xfm = nc.dram_tensor("xfm", [16, P, C * NT], F32, kind="ExternalInput").ap()
    maskd = nc.dram_tensor("mask", [P, 1], F32, kind="ExternalInput").ap()
    wD = {}
    for nm in ("fr_w", "dp_w", "bp_w", "cp_w"):
        wD[nm] = nc.dram_tensor(nm, [D, D], F32, kind="ExternalInput").ap()
    wD["m1_w"] = nc.dram_tensor("m1_w", [D, H2], F32, kind="ExternalInput").ap()
    wD["m2_w"] = nc.dram_tensor("m2_w", [H2, D], F32, kind="ExternalInput").ap()
    wD["gate_w"] = nc.dram_tensor("gate_w", [H2, D], F32, kind="ExternalInput").ap()
    wD["ffn1_w"] = nc.dram_tensor("ffn1_w", [D, H4], F32, kind="ExternalInput").ap()
    wD["ffn2_w"] = nc.dram_tensor("ffn2_w", [H4, D], F32, kind="ExternalInput").ap()
    wD["W1"] = nc.dram_tensor("W1", [NE, D, H4], F32, kind="ExternalInput").ap()
    wD["W2"] = nc.dram_tensor("W2", [NE, H4, D], F32, kind="ExternalInput").ap()
    wD["Wr"] = nc.dram_tensor("Wr", [D, NE], F32, kind="ExternalInput").ap()
    pD = {}
    for nm, sz in (("norm_g", D), ("norm_b", D), ("fr_b", D), ("dp_b", D),
                   ("bp_b", D), ("cp_b", D), ("skip", D), ("m1_b", H2),
                   ("m2_b", D), ("gate_b", D), ("ffn_g", D), ("ffn_bn", D),
                   ("ffn1_b", H4), ("ffn2_b", D)):
        pD[nm] = nc.dram_tensor(nm, [sz], F32, kind="ExternalInput").ap()
    pD["b1"] = nc.dram_tensor("b1", [NE, H4], F32, kind="ExternalInput").ap()
    pD["b2"] = nc.dram_tensor("b2", [NE, D], F32, kind="ExternalInput").ap()
    x2 = nc.dram_tensor("x2", [8, P, 4 * D], F32, kind="ExternalOutput").ap()
    dbg_out = {}
    if dbg:
        for nm in ("d_xn", "d_ssd", "d_mam", "d_y", "d_x1"):
            dbg_out[nm] = nc.dram_tensor(nm, [P, C * NT], F32, kind="ExternalOutput").ap()

    scr = {}
    for nm, mg in (("fr_w", 2), ("dp_w", 2), ("bp_w", 2), ("cp_w", 2),
                   ("m1_w", 4), ("ffn1_w", 8)):
        scr[nm] = nc.dram_tensor("s_" + nm, [mg, P, 8 * 512], BF16, kind="Internal").ap()
    for nm, nq in (("m2_w", 4), ("gate_w", 4), ("ffn2_w", 8)):
        scr[nm] = nc.dram_tensor("s_" + nm, [nq, P, 4 * D], BF16, kind="Internal").ap()
    scr["W1"] = nc.dram_tensor("s_W1", [NE, 8, P, 8 * 512], BF16, kind="Internal").ap()
    scr["W2"] = nc.dram_tensor("s_W2", [NE, 2, P, 32 * 512], BF16, kind="Internal").ap()
    sbuf_ctx = ExitStack()

    def sb(name, shape, dt):
        return sbuf_ctx.enter_context(nc.sbuf_tensor(name, shape, dt))

    fx0 = sb("fx0", [P, C * NT], F32)
    fxn = sb("fxn", [P, C * NT], F32)
    fscr = sb("fscr", [P, C * NT], F32)
    fa = sb("fa", [P, C * NT], F32)
    fst = sb("fst", [P, C * NT], F32)
    bxn = sb("bxn", [P, C * NT], BF16)
    blow = sb("blow", [P, C * NT], BF16)
    bhigh = sb("bhigh", [P, C * NT], BF16)
    bssd = sb("bssd", [P, C * NT], BF16)
    bmam = sb("bmam", [P, C * NT], BF16)
    hbuf = sb("hbuf", [P, 32 * NT], BF16)
    wsA = [sb(f"wsA{i}", [P, 8 * 512], BF16) for i in range(4)]
    pv = {}
    for nm in ("norm_g", "norm_b", "fr_b", "dp_b", "bp_b", "cp_b", "skip",
               "m2_b", "gate_b", "ffn_g", "ffn_bn", "ffn2_b"):
        pv[nm] = sb("p_" + nm, [P, C], F32)
    pv["m1_b"] = sb("p_m1_b", [P, 16], F32)
    pv["ffn1_b"] = sb("p_ffn1_b", [P, 32], F32)
    pv["b1"] = sb("p_b1", [P, NE * 32], F32)
    b2bf = sb("b2bf", [NE, D], BF16)
    wrbf = sb("wrbf", [P, C * NE], BF16)
    masks = sb("masks", [P, 1], F32)
    epst = sb("epst", [P, 1], F32)
    ident = sb("ident", [P, P], F32)
    ones_f = sb("ones_f", [P, 1], F32)
    ones_row = sb("ones_row", [1, P], BF16)
    nones_row = sb("nones_row", [1, P], BF16)
    RW = sb("RW", [P, 6 * 512], F32)
    RWB = sb("RWB", [P, 4 * 512], BF16)
    carP = sb("carP", [P, C], F32)
    carM = sb("carM", [P, C], F32)
    minit = sb("minit", [P, C], F32)
    comb = sb("comb", [P, 4 * NE], F32)
    combT = sb("combT", [NE, 4 * P], BF16)
    tkt = sb("tkt", [P, 8], F32)
    eqt = sb("eqt", [P, 16], F32)
    mskt = sb("mskt", [P, 8], F32)
    acc = fst

    psum_ctx = ExitStack()
    pbank = [psum_ctx.enter_context(nc.psum_tensor(f"pb{i}", [P, NT], F32)) for i in range(8)]

    B = lambda t, nm: Buf(t, nm)
    bufs = {}
    for nm, t in [("fx0", fx0), ("fxn", fxn), ("fscr", fscr), ("fa", fa),
                  ("fst", fst), ("bxn", bxn),
                  ("blow", blow), ("bhigh", bhigh),
                  ("bssd", bssd), ("bmam", bmam), ("hbuf", hbuf),
                  ("RW", RW), ("RWB", RWB),
                  ("carP", carP), ("carM", carM), ("minit", minit),
                  ("comb", comb), ("combT", combT), ("tkt", tkt), ("eqt", eqt),
                  ("mskt", mskt), ("b2bf", b2bf),
                  ("masks", masks), ("ident", ident)]:
        bufs[nm] = B(t, nm)
    bufs["acc"] = bufs["fst"]
    bufs["bmam0"] = Buf(bmam, "bmam0")
    bufs["bmam1"] = Buf(bmam, "bmam1")
    pB = [B(pbank[i], f"pb{i}") for i in range(8)]
    wsB = [B(t, f"wsA{i}") for i, t in enumerate(wsA)]
    pvB = {nm: B(t, "p_" + nm) for nm, t in pv.items()}
    scrB = {nm: B(a, "s_" + nm) for nm, a in scr.items()}
    x2B = B(x2, "x2")

    ring = {"ws": 0, "wv2": 0}
    wv2B = [(bhigh, [bufs["bhigh"]]),
            (bmam, [bufs["bmam"], bufs["bmam0"], bufs["bmam1"]])]

    def ring_next(kind, bl):
        i = ring[kind] % len(bl)
        ring[kind] += 1
        return bl[i]

    def dma(q, out_ap, in_ap, reads=(), writes=(), extra=(), tag="", slow=False):
        def em(e):
            return e.dma_start(out=out_ap, in_=in_ap,
                               **({"allow_slow_non_contiguous": True} if slow else {}))
        return S.add(q, em, reads, writes, extra, dma=True, tag=tag)

    def act(fn, reads=(), writes=(), extra=(), tag=""):
        return S.add("act", fn, reads, writes, extra, tag=tag)

    def dve(fn, reads=(), writes=(), extra=(), tag=""):
        return S.add("dve", fn, reads, writes, extra, tag=tag)

    def pe(fn, reads=(), writes=(), extra=(), tag=""):
        return S.add("pe", fn, reads, writes, extra, tag=tag)

    def chunk(t, c, n=NT):
        return t[:, c * n:(c + 1) * n]

    for nm in pv:
        src = pD[nm]
        w = pv[nm].shape[1]
        if nm == "b1":
            ap = src.rearrange("e (mk p) -> p e mk", p=P)
            dst = pv[nm][:, :].rearrange("p (e mk) -> p e mk", e=NE)
        else:
            ap = src.rearrange("(c p) -> p c", p=P)
            dst = pv[nm][:, :]
        dma("sync", dst, ap, writes=[pvB[nm]], tag=f"ld_{nm}", slow=True)
    dma("pool", b2bf[:, :], pD["b2"], writes=[bufs["b2bf"]], tag="ld_b2")
    dma("sync", masks[:, :], maskd, writes=[bufs["masks"]], tag="ld_mask")
    dma("pool", wrbf[:, :].rearrange("p (k e) -> p k e", k=C),
        wD["Wr"].rearrange("(k p) e -> p k e", p=P),
        writes=[Buf(wrbf, "wrbf")], tag="ld_wr", slow=True)
    wrB = Buf(wrbf, "wrbf")
    wrB.w = S.plan["pool"][-1]
    for nm in ("fr_b", "dp_b", "gate_b"):
        def em(v, t=pv[nm]):
            return v.tensor_scalar(t[:, :], t[:, :], 0.5, None, ALU.mult)
        dve(em, reads=[pvB[nm]], writes=[pvB[nm]], tag=f"half_{nm}")
    S.add("pool", lambda g: g.memset(ident[:, :], 0.0), writes=[bufs["ident"]], tag="id0")
    S.add("pool", lambda g: g.affine_select(
        out=ident[:, :], in_=ident[:, :], compare_op=ALU.not_equal, fill=1.0,
        base=0, pattern=[[-1, P]], channel_multiplier=1),
        reads=[bufs["ident"]], writes=[bufs["ident"]], tag="id1")
    onesFB = Buf(ones_f, "onesf")
    dve(lambda v: v.memset(ones_f[:, :], 1.0), writes=[onesFB], tag="ones_f")
    onesRB = Buf(ones_row, "onesr")
    dve(lambda v: v.memset(ones_row[:, :], 1.0), writes=[onesRB], tag="ones_r")
    dve(lambda v: v.memset(nones_row[:, :], -1.0), writes=[onesRB], tag="nones_r")
    epsB = Buf(epst, "eps")
    dve(lambda v: v.memset(epst[:, :], EPS), writes=[epsB], tag="eps")
    carPB = bufs["carP"]
    dve(lambda v: v.memset(carP[:, :], 0.0), writes=[carPB], tag="carP0")
    dve(lambda v: v.memset(carM[:, :], 0.0), writes=[bufs["carM"]], tag="carM0")

    converted = set()

    def stream_mg_slab(nm, mg, nk=8):
        sbuf = ring_next("ws", wsB)
        if nm.startswith("W1_"):
            e = int(nm[3:])
            sc, scb, wsrc = scr["W1"][e, mg], scrB["W1"], wD["W1"][e]
        else:
            sc, scb, wsrc = scr[nm][mg], scrB[nm], wD[nm]
        key = ("mg", nm, mg)
        if key not in converted:
            converted.add(key)
            fsrc = wsrc[:, mg * 512:(mg + 1) * 512].rearrange("(k p) f -> p k f", p=P)
            dma("pool", sbuf.ap[:, 0:nk * 512].rearrange("p (k f) -> p k f", k=nk),
                fsrc, writes=[sbuf], tag=f"cv_{nm}_{mg}")
            dma("sync", sc[:, 0:nk * 512], sbuf.ap[:, 0:nk * 512],
                reads=[sbuf], writes=[scb], tag=f"wb_{nm}_{mg}")
        else:
            dma("sync", sbuf.ap[:, 0:nk * 512], sc[:, 0:nk * 512],
                reads=[scb], writes=[sbuf], tag=f"st_{nm}_{mg}")
        return sbuf

    def stream_kfat(nm, kq):
        sbuf = ring_next("ws", wsB)
        key = ("ks", nm, kq)
        if key not in converted:
            converted.add(key)
            fsrc = wD[nm][kq * 512:(kq + 1) * 512, :].rearrange("(k p) f -> p k f", p=P)
            dma("pool", sbuf.ap[:, :].rearrange("p (k f) -> p k f", k=4),
                fsrc, writes=[sbuf], tag=f"cv_{nm}_{kq}")
            dma("sync", scr[nm][kq], sbuf.ap[:, :],
                reads=[sbuf], writes=[scrB[nm]], tag=f"wb_{nm}_{kq}")
        else:
            dma("sync", sbuf.ap[:, :], scr[nm][kq],
                reads=[scrB[nm]], writes=[sbuf], tag=f"st_{nm}_{kq}")
        return sbuf

    def stream_w2half(e, fh, hq):
        tens, wbl = wv2B[ring["wv2"] % 2]
        ring["wv2"] += 1
        key = ("w2", e, fh, hq)
        if key not in converted:
            converted.add(key)
            fsrc = wD["W2"][e][hq * 1024:(hq + 1) * 1024, fh * 512:(fh + 1) * 512] \
                .rearrange("(k p) f -> p k f", p=P)
            dma("pool", tens[:, 0:4096].rearrange("p (k f) -> p k f", k=8),
                fsrc, writes=wbl, tag=f"cv_W2_{e}_{fh}_{hq}")
            dma("sync", scr["W2"][e, fh][:, hq * 4096:(hq + 1) * 4096], tens[:, 0:4096],
                reads=[wbl[0]], writes=[scrB["W2"]], tag=f"wb_W2_{e}_{fh}_{hq}")
        else:
            dma("sync", tens[:, 0:4096], scr["W2"][e, fh][:, hq * 4096:(hq + 1) * 4096],
                reads=[scrB["W2"]], writes=wbl, tag=f"st_W2_{e}_{fh}_{hq}")
        return wbl

    def mm_group(pbuf, pairs, reads, extra=(), tag=""):
        def em(t):
            last = None
            n = len(pairs)
            for i, (l, r) in enumerate(pairs):
                last = t.matmul(pbuf.ap[:, :], l, r,
                                start=(i == 0), stop=(i == n - 1))
            return last
        return pe(em, reads=reads, writes=[pbuf], extra=extra, tag=tag)

    rot_idx = [0]
    mid_idx = [0]

    def rot_bank():
        i = 6 + (rot_idx[0] % 2)
        rot_idx[0] += 1
        return pB[i]

    def mid_bank():
        i = 4 + (mid_idx[0] % 4)
        mid_idx[0] += 1
        return pB[i]

    def proj_1024(nm, rhs_bf, rhs_buf, epi, tag, bank=None):
        for mg in range(2):
            slab = stream_mg_slab(nm, mg)

            def em(t, slab=slab):
                last = None
                for j in range(4):
                    for k in range(C):
                        last = t.matmul(pbank[4 + j][:, :],
                                        slab.ap[:, k * 512 + j * P:k * 512 + j * P + P],
                                        chunk(rhs_bf, k),
                                        start=(k == 0), stop=(k == C - 1),
                                        skip_group_check=True)
                return last
            g = pe(em, reads=[slab, rhs_buf], writes=[pB[4], pB[5], pB[6], pB[7]],
                   tag=f"{tag}_mg{mg}")
            for j in range(4):
                epi(mg * 4 + j, pB[4 + j], g)

    def ln_stats(src_buf, src_t, sq_buf, sq_t, tag):
        g1 = pe(lambda t: [t.matmul(pbank[6][0:1, :], ones_f[:, 0:1], chunk(src_t, k),
                                    start=(k == 0), stop=(k == C - 1)) for k in range(C)][-1],
                reads=[src_buf, onesFB], writes=[pB[6]], tag=f"{tag}_sum")
        g2 = pe(lambda t: [t.matmul(pbank[7][0:1, :], ones_f[:, 0:1], chunk(sq_t, k),
                                    start=(k == 0), stop=(k == C - 1)) for k in range(C)][-1],
                reads=[sq_buf, onesFB], writes=[pB[7]], tag=f"{tag}_sq")
        return g1, g2

    RWb, RWBb = bufs["RW"], bufs["RWB"]

    def Rr(j):
        return RW[0:1, j * 512:(j + 1) * 512]

    def Br(j):
        return RWB[0:1, j * 512:(j + 1) * 512]

    def rmul(dj, aj, bj, tag):
        dve(lambda v: v.tensor_tensor(Rr(dj), Rr(aj), Rr(bj), ALU.mult),
            reads=[RWb], writes=[RWb], tag=tag)

    def rneg(j, tag):
        dve(lambda v: v.tensor_scalar(Rr(j), Rr(j), -1.0, None, ALU.mult),
            reads=[RWb], writes=[RWb], tag=tag)

    def rsqrt_row(j, tag, src_j=None):
        sj = j if src_j is None else src_j
        act(lambda s: s.activation(Rr(j), Rr(sj), AF.Sqrt, bias=epst[0:1, 0:1]),
            reads=[RWb, epsB], writes=[RWb], tag=f"{tag}_sd")
        dve(lambda v: v.reciprocal(Rr(j), Rr(j)),
            reads=[RWb], writes=[RWb], tag=f"{tag}_rc")

    def hi_lo(src_j, hi_j, lo_j, scr_j, tag):
        dve(lambda v: v.tensor_copy(Br(hi_j), Rr(src_j)),
            reads=[RWb], writes=[RWBb], tag=f"{tag}_hi")
        dve(lambda v: v.tensor_copy(Rr(scr_j), Br(hi_j)),
            reads=[RWBb], writes=[RWb], tag=f"{tag}_hf")
        dve(lambda v: v.tensor_tensor(Rr(scr_j), Rr(src_j), Rr(scr_j), ALU.subtract),
            reads=[RWb], writes=[RWb], tag=f"{tag}_lf")
        dve(lambda v: v.tensor_copy(Br(lo_j), Rr(scr_j)),
            reads=[RWb], writes=[RWBb], tag=f"{tag}_lo")

    def bcast(hi_j, lo_j, pbuf, tag, neg=False):
        orow = nones_row if neg else ones_row
        def em(t):
            t.matmul(pbuf.ap[:, :], orow[0:1, :], Br(hi_j),
                     start=True, stop=False, skip_group_check=True)
            return t.matmul(pbuf.ap[:, :], orow[0:1, :], Br(lo_j),
                            start=False, stop=True, skip_group_check=True)
        return pe(em, reads=[RWBb, onesRB], writes=[pbuf], tag=tag)

    def ln_rows_mv(g1, g2, tag):
        dve(lambda v: v.tensor_scalar(Rr(0), pbank[6][0:1, :], 1.0 / D, None, ALU.mult),
            reads=[pB[6]], writes=[RWb], extra=[g1], tag=f"{tag}_m")
        dve(lambda v: v.tensor_scalar(Rr(1), pbank[7][0:1, :], 1.0 / D, None, ALU.mult),
            reads=[pB[7]], writes=[RWb], extra=[g2], tag=f"{tag}_ss")
        rmul(4, 0, 0, f"{tag}_m2")
        dve(lambda v: v.tensor_tensor(Rr(1), Rr(1), Rr(4), ALU.subtract),
            reads=[RWb], writes=[RWb], tag=f"{tag}_v")

    def ln12(x0b, tag):
        act(lambda s: s.activation(fxn[:, :], x0b.ap[:, :], AF.Square),
            reads=[x0b], writes=[bufs["fxn"]], tag=f"{tag}_sq")
        g1, g2 = ln_stats(x0b, fx0, bufs["fxn"], fxn, tag)
        ln_rows_mv(g1, g2, tag)
        rsqrt_row(2, f"{tag}_r1", src_j=1)
        rmul(4, 2, 2, f"{tag}_rq")
        rmul(4, 1, 4, f"{tag}_v2")
        rsqrt_row(4, f"{tag}_r2")
        rmul(3, 0, 2, f"{tag}_s2a")
        rmul(5, 2, 4, f"{tag}_rr")
        rmul(0, 0, 5, f"{tag}_mm")
        hi_lo(5, 0, 1, 4, f"{tag}_hrr")
        hi_lo(0, 2, 3, 4, f"{tag}_hmm")

    def ln_apply_xn(x0b, g_nm, b_nm, tag):
        p_rr = pB[4 + (rot_idx[0] % 2)]; rot_idx[0] += 1
        p_mm = pB[4 + (rot_idx[0] % 2)]; rot_idx[0] += 1
        bc1 = bcast(0, 1, p_rr, f"{tag}_bc1")
        bc2 = bcast(2, 3, p_mm, f"{tag}_bc2", neg=True)
        for c in range(C):
            dve(lambda v, c=c: v.tensor_tensor(chunk(fxn, c), chunk(x0b.ap, c),
                                               p_rr.ap[:, :], ALU.mult),
                reads=[x0b, p_rr], writes=[bufs["fxn"]], extra=[bc1], tag=f"{tag}_u{c}")
            dve(lambda v, c=c: v.tensor_tensor(chunk(fxn, c), chunk(fxn, c),
                                               p_mm.ap[:, :], ALU.add),
                reads=[bufs["fxn"], p_mm], writes=[bufs["fxn"]], extra=[bc2], tag=f"{tag}_a{c}")
            dve(lambda v, c=c: v.tensor_scalar(chunk(fxn, c), chunk(fxn, c),
                                               pv[g_nm][:, c:c + 1], pv[b_nm][:, c:c + 1],
                                               ALU.mult, ALU.add),
                reads=[bufs["fxn"], pvB[g_nm], pvB[b_nm]], writes=[bufs["fxn"]], tag=f"{tag}_g{c}")
        dve(lambda v: v.tensor_copy(bxn[:, :], fxn[:, :]),
            reads=[bufs["fxn"]], writes=[bufs["bxn"]], tag=f"{tag}_bf")

    def front(tile_idx, tag):
        x0b = bufs["fx0"]
        dma("sync", fx0[:, :], xfm[tile_idx], writes=[x0b], tag=f"{tag}_x0")
        ln12(x0b, f"{tag}_ln")
        ln_apply_xn(x0b, "norm_g", "norm_b", f"{tag}_xn")
        return x0b

    def fr_low_high(with_high, tag):
        xnbfb = bufs["bxn"]

        def epi(m, pb_, g):
            act(lambda s, m=m, pb_=pb_: s.activation(
                chunk(fscr, m), pb_.ap[:, :], AF.Tanh,
                bias=pv["fr_b"][:, m:m + 1], scale=0.5),
                reads=[pb_, pvB["fr_b"]], writes=[bufs["fscr"]], extra=[g], tag=f"{tag}_e{m}")
        proj_1024("fr_w", bxn, xnbfb, epi, tag)
        dve(lambda v: v.tensor_scalar(fscr[:, :], fscr[:, :], 0.5, 0.5, ALU.mult, ALU.add),
            reads=[bufs["fscr"]], writes=[bufs["fscr"]], tag=f"{tag}_lg")
        dve(lambda v: v.tensor_tensor(blow[:, :], fxn[:, :], fscr[:, :], ALU.mult),
            reads=[bufs["fxn"], bufs["fscr"]], writes=[bufs["blow"]], tag=f"{tag}_low")
        if with_high:
            dve(lambda v: v.tensor_scalar(fscr[:, :], fscr[:, :], -1.0, 1.0, ALU.mult, ALU.add),
                reads=[bufs["fscr"]], writes=[bufs["fscr"]], tag=f"{tag}_ilg")
            dve(lambda v: v.tensor_tensor(bhigh[:, :], fxn[:, :], fscr[:, :], ALU.mult),
                reads=[bufs["fxn"], bufs["fscr"]], writes=[bufs["bhigh"]], tag=f"{tag}_high")

    def dp_bp_scan(init_buf, init_zero, car_buf, tag):
        lowb = bufs["blow"]

        def epi_a(m, pb_, g):
            act(lambda s, m=m, pb_=pb_: s.activation(
                chunk(fa, m), pb_.ap[:, :], AF.Tanh,
                bias=pv["dp_b"][:, m:m + 1], scale=0.5),
                reads=[pb_, pvB["dp_b"]], writes=[bufs["fa"]], extra=[g], tag=f"{tag}_ea{m}")
        proj_1024("dp_w", blow, lowb, epi_a, f"{tag}_dp")
        dve(lambda v: v.tensor_scalar(fa[:, :], fa[:, :], 0.5, 0.5, ALU.mult, ALU.add),
            reads=[bufs["fa"]], writes=[bufs["fa"]], tag=f"{tag}_a")

        def epi_b(m, pb_, g):
            act(lambda s, m=m, pb_=pb_: s.activation(
                chunk(fscr, m), pb_.ap[:, :], AF.Tanh,
                bias=pv["bp_b"][:, m:m + 1]),
                reads=[pb_, pvB["bp_b"]], writes=[bufs["fscr"]], extra=[g], tag=f"{tag}_eb{m}")
        proj_1024("bp_w", blow, lowb, epi_b, f"{tag}_bp")
        dve(lambda v: v.tensor_tensor(fscr[:, :], fscr[:, :], fxn[:, :], ALU.mult),
            reads=[bufs["fscr"], bufs["fxn"]], writes=[bufs["fscr"]], tag=f"{tag}_b")
        for c in range(C):
            if init_zero:
                ini = 0.0
                rd = [bufs["fa"], bufs["fscr"]]
            else:
                ini = init_buf.ap[:, c:c + 1]
                rd = [bufs["fa"], bufs["fscr"], init_buf]
            dve(lambda v, c=c, ini=ini: v.tensor_tensor_scan(
                chunk(fst, c), chunk(fa, c), chunk(fscr, c), ini, ALU.mult, ALU.add),
                reads=rd, writes=[bufs["fst"]], tag=f"{tag}_sc{c}")
        dve(lambda v: v.tensor_copy(
            car_buf.ap[:, :], fst[:, :].rearrange("p (c t) -> p c t", c=C)[:, :, NT - 1]),
            reads=[bufs["fst"]], writes=[car_buf], tag=f"{tag}_car")

    for k in range(n_p):
        tag = f"P{k}"
        front(k, tag)
        fr_low_high(False, f"{tag}_fr")
        dp_bp_scan(carPB, k == 0, carPB, tag)

    dve(lambda v: v.tensor_scalar(minit[:, :], carP[:, :], masks[:, 0:1], None, ALU.mult),
        reads=[carPB, bufs["masks"]], writes=[bufs["minit"]], tag="minit")

    if n_m > 0:
        front(8, "M0")
    for k in range(n_m):
        tag = f"M{k}"
        x0b = bufs["fx0"]
        fr_low_high(True, f"{tag}_fr")
        init_b = bufs["minit"] if k == 0 else bufs["carM"]
        dp_bp_scan(init_b, False, bufs["carM"], tag)
        if dbg and k == 0:
            dma("act", dbg_out["d_xn"], fxn[:, :], reads=[bufs["fxn"]], tag="dbg_xn")

        def epi_c(m, pb_, g):
            act(lambda s, m=m, pb_=pb_: s.activation(
                chunk(fscr, m), pb_.ap[:, :], AF.Tanh, bias=pv["cp_b"][:, m:m + 1]),
                reads=[pb_, pvB["cp_b"]], writes=[bufs["fscr"]], extra=[g], tag=f"{tag}_ec{m}")
        proj_1024("cp_w", bhigh, bufs["bhigh"], epi_c, f"{tag}_cp", bank=mid_bank)
        dve(lambda v: v.tensor_tensor(fst[:, :], fscr[:, :], fst[:, :], ALU.mult),
            reads=[bufs["fscr"], bufs["fst"]], writes=[bufs["fst"]], tag=f"{tag}_s1")
        for c in range(C):
            dve(lambda v, c=c: v.scalar_tensor_tensor(
                chunk(fst, c), chunk(fxn, c), pv["skip"][:, c:c + 1], chunk(fst, c),
                ALU.mult, ALU.add),
                reads=[bufs["fxn"], pvB["skip"], bufs["fst"]], writes=[bufs["fst"]],
                tag=f"{tag}_ssd{c}")
        dve(lambda v: v.tensor_copy(bssd[:, :], fst[:, :]),
            reads=[bufs["fst"]], writes=[bufs["bssd"]], tag=f"{tag}_ssdbf")
        if dbg and k == 0:
            dma("act", dbg_out["d_ssd"], fst[:, :], reads=[bufs["fst"]], tag="dbg_ssd")

        for mg in range(4):
            slab = stream_mg_slab("m1_w", mg)

            def em(t, slab=slab):
                last = None
                for j in range(4):
                    for kk in range(C):
                        last = t.matmul(pbank[4 + j][:, :],
                                        slab.ap[:, kk * 512 + j * P:kk * 512 + j * P + P],
                                        chunk(blow, kk),
                                        start=(kk == 0), stop=(kk == C - 1),
                                        skip_group_check=True)
                return last
            g = pe(em, reads=[slab, bufs["blow"]], writes=[pB[4], pB[5], pB[6], pB[7]],
                   tag=f"{tag}_m1_{mg}")
            for j in range(4):
                m = mg * 4 + j
                act(lambda s, m=m, j=j: s.activation(
                    hbuf[:, m * 512:(m + 1) * 512], pbank[4 + j][:, :], AF.Gelu,
                    bias=pv["m1_b"][:, m:m + 1]),
                    reads=[pB[4 + j], pvB["m1_b"]], writes=[bufs["hbuf"]], extra=[g],
                    tag=f"{tag}_m1e{m}")

        def half_m_kslab(nm, nk, rhs_fn, rhs_bufs, epi, tag2, rhs_bufs_fn=None):
            for mg in range(2):
                g = None
                for kq in range(nk // 4):
                    slab = stream_kfat(nm, kq)
                    rbufs = rhs_bufs_fn(kq) if rhs_bufs_fn else rhs_bufs

                    def em(t, kq=kq, mg=mg, slab=slab):
                        last = None
                        for ki in range(4):
                            kk = kq * 4 + ki
                            for j in range(4):
                                m = mg * 4 + j
                                last = t.matmul(pbank[j][:, :],
                                                slab.ap[:, ki * D + m * P:ki * D + (m + 1) * P],
                                                rhs_fn(kk), start=(kk == 0), stop=(kk == nk - 1),
                                                skip_group_check=True)
                        return last
                    g = pe(em, reads=[slab] + rbufs,
                           writes=[pB[0], pB[1], pB[2], pB[3]], tag=f"{tag2}_{mg}_{kq}")
                for j in range(4):
                    epi(mg * 4 + j, pB[j], g)

        def m2_epi(m, pb_, g):
            dve(lambda v, m=m, pb_=pb_: v.tensor_scalar(
                chunk(fa, m), pb_.ap[:, :], pv["m2_b"][:, m:m + 1], None, ALU.add),
                reads=[pb_, pvB["m2_b"]], writes=[bufs["fa"]], extra=[g], tag=f"{tag}_m2e{m}")
        half_m_kslab("m2_w", 16, lambda kk: hbuf[:, kk * 512:(kk + 1) * 512],
                     [bufs["hbuf"]], m2_epi, f"{tag}_m2")
        dve(lambda v: v.tensor_copy(bmam[:, 0:4 * NT], fa[:, 0:4 * NT]),
            reads=[bufs["fa"]], writes=[bufs["bmam0"]], tag=f"{tag}_mambf0")
        dve(lambda v: v.tensor_copy(bmam[:, 4 * NT:8 * NT], fa[:, 4 * NT:8 * NT]),
            reads=[bufs["fa"]], writes=[bufs["bmam1"]], tag=f"{tag}_mambf1")
        if dbg and k == 0:
            dma("act", dbg_out["d_mam"], fa[:, :], reads=[bufs["fa"]], tag="dbg_mam")

        def gate_rhs(kk):
            return chunk(bssd, kk) if kk < 8 else chunk(bmam, kk - 8)

        def gate_epi(m, pb_, g):
            act(lambda s, m=m, pb_=pb_: s.activation(
                chunk(fscr, m), pb_.ap[:, :], AF.Tanh,
                bias=pv["gate_b"][:, m:m + 1], scale=0.5),
                reads=[pb_, pvB["gate_b"]], writes=[bufs["fscr"]], extra=[g],
                tag=f"{tag}_ge{m}")
        half_m_kslab("gate_w", 16, gate_rhs, None, gate_epi, f"{tag}_gate",
                     rhs_bufs_fn=lambda kq: [bufs["bssd"]] if kq < 2 else
                     [bufs["bmam0"] if kq == 2 else bufs["bmam1"]])

        dve(lambda v: v.tensor_tensor(fst[:, :], fst[:, :], fa[:, :], ALU.subtract),
            reads=[bufs["fst"], bufs["fa"]], writes=[bufs["fst"]], tag=f"{tag}_d")
        dve(lambda v: v.tensor_tensor(fscr[:, :], fscr[:, :], fst[:, :], ALU.mult),
            reads=[bufs["fscr"], bufs["fst"]], writes=[bufs["fscr"]], tag=f"{tag}_e")
        dve(lambda v: v.tensor_tensor(fst[:, :], fst[:, :], fscr[:, :], ALU.add),
            reads=[bufs["fst"], bufs["fscr"]], writes=[bufs["fst"]], tag=f"{tag}_f")
        dve(lambda v: v.scalar_tensor_tensor(fst[:, :], fst[:, :], 0.5, fa[:, :],
                                             ALU.mult, ALU.add),
            reads=[bufs["fst"], bufs["fa"]], writes=[bufs["fst"]], tag=f"{tag}_u")
        hi_lo(2, 0, 1, 4, f"{tag}_hr1")
        hi_lo(3, 2, 3, 4, f"{tag}_hs2")
        bc1 = bcast(0, 1, pB[4], f"{tag}_br1")
        bc2 = bcast(2, 3, pB[5], f"{tag}_bs2", neg=True)
        for c in range(C):
            dve(lambda v, c=c: v.tensor_tensor(chunk(fxn, c), chunk(fx0, c),
                                               pbank[4][:, :], ALU.mult),
                reads=[x0b, pB[4]], writes=[bufs["fxn"]], extra=[bc1], tag=f"{tag}_y1{c}")
            dve(lambda v, c=c: v.tensor_tensor(chunk(fxn, c), chunk(fxn, c),
                                               pbank[5][:, :], ALU.add),
                reads=[bufs["fxn"], pB[5]], writes=[bufs["fxn"]], extra=[bc2], tag=f"{tag}_y2{c}")
        dve(lambda v: v.tensor_tensor(fxn[:, :], fxn[:, :], fst[:, :], ALU.add),
            reads=[bufs["fxn"], bufs["fst"]], writes=[bufs["fxn"]], tag=f"{tag}_y3")
        if dbg and k == 0:
            dma("act", dbg_out["d_y"], fxn[:, :], reads=[bufs["fxn"]], tag="dbg_y")

        act(lambda s: s.activation(fst[:, :], fxn[:, :], AF.Square),
            reads=[bufs["fxn"]], writes=[bufs["fst"]], tag=f"{tag}_ysq")
        g1, g2 = ln_stats(bufs["fxn"], fxn, bufs["fst"], fst, f"{tag}_l3")
        ln_rows_mv(g1, g2, f"{tag}_l3")
        rsqrt_row(5, f"{tag}_l3r", src_j=1)
        rmul(0, 0, 5, f"{tag}_l3sa")
        hi_lo(5, 0, 1, 1, f"{tag}_l3h1")
        hi_lo(0, 2, 3, 1, f"{tag}_l3h2")
        bc1 = bcast(0, 1, pB[4], f"{tag}_l3b1")
        bc2 = bcast(2, 3, pB[5], f"{tag}_l3b2", neg=True)
        for c in range(C):
            dve(lambda v, c=c: v.tensor_tensor(chunk(fst, c), chunk(fxn, c),
                                               pbank[4][:, :], ALU.mult),
                reads=[bufs["fxn"], pB[4]], writes=[bufs["fst"]], extra=[bc1], tag=f"{tag}_l3u{c}")
            dve(lambda v, c=c: v.tensor_tensor(chunk(fst, c), chunk(fst, c),
                                               pbank[5][:, :], ALU.add),
                reads=[bufs["fst"], pB[5]], writes=[bufs["fst"]], extra=[bc2], tag=f"{tag}_l3a{c}")
            dve(lambda v, c=c: v.tensor_scalar(chunk(bxn, c), chunk(fst, c),
                                               pv["ffn_g"][:, c:c + 1], pv["ffn_bn"][:, c:c + 1],
                                               ALU.mult, ALU.add),
                reads=[bufs["fst"], pvB["ffn_g"], pvB["ffn_bn"]], writes=[bufs["bxn"]],
                tag=f"{tag}_l3g{c}")

        for mg in range(8):
            slab = stream_mg_slab("ffn1_w", mg)

            def em(t, slab=slab):
                last = None
                for j in range(4):
                    for kk in range(C):
                        last = t.matmul(pbank[4 + j][:, :],
                                        slab.ap[:, kk * 512 + j * P:kk * 512 + j * P + P],
                                        chunk(bxn, kk),
                                        start=(kk == 0), stop=(kk == C - 1),
                                        skip_group_check=True)
                return last
            g = pe(em, reads=[slab, bufs["bxn"]], writes=[pB[4], pB[5], pB[6], pB[7]],
                   tag=f"{tag}_f1_{mg}")
            for j in range(4):
                m = mg * 4 + j
                act(lambda s, m=m, j=j: s.activation(
                    hbuf[:, m * 512:(m + 1) * 512], pbank[4 + j][:, :], AF.Gelu,
                    bias=pv["ffn1_b"][:, m:m + 1]),
                    reads=[pB[4 + j], pvB["ffn1_b"]], writes=[bufs["hbuf"]], extra=[g],
                    tag=f"{tag}_f1e{m}")

        def f2_epi(m, pb_, g):
            dve(lambda v, m=m, pb_=pb_: v.scalar_tensor_tensor(
                chunk(fa, m), pb_.ap[:, :], pv["ffn2_b"][:, m:m + 1], chunk(fxn, m),
                ALU.add, ALU.add),
                reads=[pb_, pvB["ffn2_b"], bufs["fxn"]], writes=[bufs["fa"]], extra=[g],
                tag=f"{tag}_f2e{m}")
        half_m_kslab("ffn2_w", 32, lambda kk: hbuf[:, kk * 512:(kk + 1) * 512],
                     [bufs["hbuf"]], f2_epi, f"{tag}_f2")
        dve(lambda v: v.tensor_tensor(fa[:, :], fa[:, :], fx0[:, :], ALU.add),
            reads=[bufs["fa"], x0b], writes=[bufs["fa"]], tag=f"{tag}_x1")
        if dbg and k == 0:
            dma("act", dbg_out["d_x1"], fa[:, :], reads=[bufs["fa"]], tag="dbg_x1")

        act(lambda s: s.activation(fst[:, :], fa[:, :], AF.Square),
            reads=[bufs["fa"]], writes=[bufs["fst"]], tag=f"{tag}_x1sq")
        g1, g2 = ln_stats(bufs["fa"], fa, bufs["fst"], fst, f"{tag}_l4")
        ln_rows_mv(g1, g2, f"{tag}_l4")
        rsqrt_row(5, f"{tag}_l4r", src_j=1)
        rmul(0, 0, 5, f"{tag}_l4sa")
        hi_lo(5, 0, 1, 1, f"{tag}_l4h1")
        hi_lo(0, 2, 3, 1, f"{tag}_l4h2")
        bc1 = bcast(0, 1, pB[4], f"{tag}_l4b1")
        bc2 = bcast(2, 3, pB[5], f"{tag}_l4b2", neg=True)
        for c in range(C):
            dve(lambda v, c=c: v.tensor_tensor(chunk(fst, c), chunk(fa, c),
                                               pbank[4][:, :], ALU.mult),
                reads=[bufs["fa"], pB[4]], writes=[bufs["fst"]], extra=[bc1], tag=f"{tag}_l4u{c}")
            dve(lambda v, c=c: v.tensor_tensor(chunk(blow, c), chunk(fst, c),
                                               pbank[5][:, :], ALU.add),
                reads=[bufs["fst"], pB[5]], writes=[bufs["blow"]], extra=[bc2], tag=f"{tag}_l4a{c}")

        for tc in range(4):
            def em_t(t, tc=tc):
                last = None
                for cc in range(C):
                    pbk = pbank[4 + cc // 4]
                    last = t.transpose(pbk[:, (cc % 4) * P:(cc % 4 + 1) * P],
                                       fa[:, cc * 512 + tc * P:cc * 512 + tc * P + P],
                                       ident[:, :])
                return last
            g = pe(em_t, reads=[bufs["fa"], bufs["ident"]], writes=[pB[4], pB[5]],
                   tag=f"{tag}_tp{tc}")
            act(lambda s, tc=tc: s.activation(acc[:, tc * D:tc * D + 512],
                                              pbank[4][:, :], AF.Copy),
                reads=[pB[4]], writes=[bufs["acc"]], extra=[g], tag=f"{tag}_tpa{tc}")
            act(lambda s, tc=tc: s.activation(acc[:, tc * D + 512:(tc + 1) * D],
                                              pbank[5][:, :], AF.Copy),
                reads=[pB[5]], writes=[bufs["acc"]], extra=[g], tag=f"{tag}_tpb{tc}")

        def em_lg(t):
            last = None
            for tc in range(4):
                for kk in range(C):
                    last = t.matmul(pbank[6][:, tc * 8:(tc + 1) * 8],
                                    blow[:, kk * 512 + tc * P:kk * 512 + tc * P + P],
                                    wrbf[:, kk * 8:(kk + 1) * 8],
                                    start=(kk == 0), stop=(kk == C - 1),
                                    skip_group_check=True)
            return last
        glg = pe(em_lg, reads=[bufs["blow"], wrB], writes=[pB[6]], tag=f"{tag}_lg")
        for tc in range(4):
            ls = pbank[6][:, tc * 8:(tc + 1) * 8]
            dve(lambda v, ls=ls: v.tensor_reduce(tkt[:, 0:1], ls, AX.X, ALU.max),
                reads=[pB[6]], writes=[bufs["tkt"]], extra=[glg], tag=f"{tag}_tk0{tc}")
            dve(lambda v, ls=ls: v.tensor_scalar(eqt[:, 0:8], ls, tkt[:, 0:1], None, ALU.is_equal),
                reads=[pB[6], bufs["tkt"]], writes=[bufs["eqt"]], tag=f"{tag}_tk1{tc}")
            dve(lambda v, ls=ls: v.scalar_tensor_tensor(mskt[:, :], eqt[:, 0:8], -1e9, ls,
                                                        ALU.mult, ALU.add),
                reads=[bufs["eqt"], pB[6]], writes=[bufs["mskt"]], tag=f"{tag}_tk2{tc}")
            dve(lambda v: v.tensor_reduce(tkt[:, 1:2], mskt[:, :], AX.X, ALU.max),
                reads=[bufs["mskt"]], writes=[bufs["tkt"]], tag=f"{tag}_tk3{tc}")
            dve(lambda v: v.tensor_tensor(tkt[:, 4:5], tkt[:, 0:1], tkt[:, 1:2], ALU.subtract),
                reads=[bufs["tkt"]], writes=[bufs["tkt"]], tag=f"{tag}_tk4{tc}")
            act(lambda s: s.activation(tkt[:, 5:6], tkt[:, 4:5], AF.Tanh, scale=0.5),
                reads=[bufs["tkt"]], writes=[bufs["tkt"]], tag=f"{tag}_tk5{tc}")
            dve(lambda v: v.tensor_scalar(tkt[:, 2:3], tkt[:, 5:6], 0.5, 0.5, ALU.mult, ALU.add),
                reads=[bufs["tkt"]], writes=[bufs["tkt"]], tag=f"{tag}_tk6{tc}")
            dve(lambda v: v.tensor_scalar(tkt[:, 3:4], tkt[:, 2:3], -1.0, 1.0, ALU.mult, ALU.add),
                reads=[bufs["tkt"]], writes=[bufs["tkt"]], tag=f"{tag}_tk7{tc}")
            dve(lambda v: v.tensor_scalar(eqt[:, 8:16], mskt[:, :], tkt[:, 1:2], None, ALU.is_equal),
                reads=[bufs["mskt"], bufs["tkt"]], writes=[bufs["eqt"]], tag=f"{tag}_tk8{tc}")
            dve(lambda v, tc=tc: v.tensor_scalar(comb[:, tc * 8:(tc + 1) * 8], eqt[:, 0:8],
                                                 tkt[:, 2:3], None, ALU.mult),
                reads=[bufs["eqt"], bufs["tkt"]], writes=[bufs["comb"]], tag=f"{tag}_tk9{tc}")
            dve(lambda v, tc=tc: v.scalar_tensor_tensor(
                comb[:, tc * 8:(tc + 1) * 8], eqt[:, 8:16], tkt[:, 3:4],
                comb[:, tc * 8:(tc + 1) * 8], ALU.mult, ALU.add),
                reads=[bufs["eqt"], bufs["tkt"], bufs["comb"]], writes=[bufs["comb"]],
                tag=f"{tag}_tkA{tc}")

        def em_ct(t):
            last = None
            for tc in range(4):
                last = t.transpose(pbank[7][0:NE, tc * P:(tc + 1) * P],
                                   comb[:, tc * 8:(tc + 1) * 8], ident[:, :])
            return last
        gct = pe(em_ct, reads=[bufs["comb"], bufs["ident"]], writes=[pB[7]], tag=f"{tag}_ct")
        act(lambda s: s.activation(combT[:, :], pbank[7][0:NE, :], AF.Copy),
            reads=[pB[7]], writes=[bufs["combT"]], extra=[gct], tag=f"{tag}_ctc")
        for tc in range(4):
            for fh in range(2):
                def em_b2(t, tc=tc, fh=fh):
                    return t.matmul(pbank[6][:, :], combT[0:NE, tc * P:(tc + 1) * P],
                                    b2bf[0:NE, fh * 512:(fh + 1) * 512],
                                    start=True, stop=True)
                g = pe(em_b2, reads=[bufs["combT"], bufs["b2bf"]], writes=[pB[6]],
                       tag=f"{tag}_b2_{tc}_{fh}")
                dve(lambda v, tc=tc, fh=fh: v.tensor_tensor(
                    acc[:, tc * D + fh * 512:tc * D + (fh + 1) * 512],
                    acc[:, tc * D + fh * 512:tc * D + (fh + 1) * 512],
                    pbank[6][:, :], ALU.add),
                    reads=[bufs["acc"], pB[6]], writes=[bufs["acc"]], extra=[g],
                    tag=f"{tag}_b2a_{tc}_{fh}")

        def emit_w2_quad(e, fh, hg2, half):
            hap = half[0].ap
            def em_w2(t, hg2=hg2, hap=hap):
                last = None
                for j in range(4):
                    mk = hg2 * 4 + j
                    for tc in range(4):
                        last = t.matmul(
                            pbank[tc][:, :],
                            hbuf[:, mk * 512 + tc * P:mk * 512 + tc * P + P],
                            hap[:, (mk % 8) * 512:(mk % 8 + 1) * 512],
                            start=(mk == 0), stop=(mk == 31),
                            skip_group_check=True)
                return last
            pe(em_w2, reads=[bufs["hbuf"]] + list(half),
               writes=[pB[0], pB[1], pB[2], pB[3]], tag=f"{tag}_w2_{e}_{fh}_{hg2}")

        for e in range(n_e):
            if e == 2 and k + 1 < n_m:
                front(8 + k + 1, f"M{k + 1}")
            for fh in range(2):
                halves = {}
                for hg in range(8):
                    if hg % 2 == 0:
                        halves[hg // 2] = stream_w2half(e, fh, hg // 2)
                    if fh == 0:
                        slab = stream_mg_slab(f"W1_{e}", hg)
                        for pr in range(2):
                            def em_w1(t, slab=slab, hg=hg, pr=pr):
                                last = None
                                for jj in range(2):
                                    j = pr * 2 + jj
                                    for kk in range(C):
                                        last = t.matmul(
                                            pbank[4 + jj][:, :],
                                            slab.ap[:, kk * 512 + j * P:kk * 512 + j * P + P],
                                            chunk(blow, kk),
                                            start=(kk == 0), stop=(kk == C - 1),
                                            skip_group_check=True)
                                return last
                            g = pe(em_w1, reads=[slab, bufs["blow"]],
                                   writes=[pB[4], pB[5]], tag=f"{tag}_w1_{e}_{hg}_{pr}")
                            for jj in range(2):
                                mk = hg * 4 + pr * 2 + jj
                                act(lambda s, mk=mk, jj=jj, e=e: s.activation(
                                    hbuf[:, mk * 512:(mk + 1) * 512], pbank[4 + jj][:, :],
                                    AF.Gelu,
                                    bias=pv["b1"][:, e * 32 + mk:e * 32 + mk + 1]),
                                    reads=[pB[4 + jj], pvB["b1"]], writes=[bufs["hbuf"]],
                                    extra=[g], tag=f"{tag}_w1e_{e}_{mk}")
                        if hg >= 1:
                            emit_w2_quad(e, fh, hg - 1, halves[(hg - 1) // 2])
                    else:
                        emit_w2_quad(e, fh, hg, halves[hg // 2])
                if fh == 0:
                    emit_w2_quad(e, fh, 7, halves[3])
                for tc in range(4):
                    dve(lambda v, tc=tc, fh=fh, e=e: v.scalar_tensor_tensor(
                        acc[:, tc * D + fh * 512:tc * D + (fh + 1) * 512],
                        pbank[tc][:, :], comb[:, tc * 8 + e:tc * 8 + e + 1],
                        acc[:, tc * D + fh * 512:tc * D + (fh + 1) * 512],
                        ALU.mult, ALU.add),
                        reads=[pB[tc], bufs["comb"], bufs["acc"]], writes=[bufs["acc"]],
                        tag=f"{tag}_ye_{e}_{fh}_{tc}")

        dma("act", x2[k], acc[:, :], reads=[bufs["acc"]], writes=[x2B], tag=f"{tag}_store")

    S.add("sync", lambda e: e.nop(), extra=[x2B.w] + x2B.rs, tag="drain")

    S.finalize()
    sem_ctx = ExitStack()
    sems = {q: sem_ctx.enter_context(nc.semaphore(f"sem_{q}")) for q in QUEUES}

    def runner(qname):
        def _(eng):
            clock = {q: 0 for q in QUEUES}
            for st in S.plan[qname]:
                needed = {}
                for d in st.deps:
                    if d.q == qname == "pe":
                        continue
                    if d.cum is None:
                        continue
                    if d.cum > needed.get(d.q, 0):
                        needed[d.q] = d.cum
                for qd, v in needed.items():
                    if clock[qd] < v:
                        eng.wait_ge(sems[qd], v)
                        clock[qd] = v
                last = st.emit(eng)
                if st.cum is not None:
                    if last is None:
                        raise RuntimeError(f"step {st.tag} returned no instruction")
                    last.then_inc(sems[qname], st.inc)
        return _

    with nc.Block() as block:
        block.tensor(runner("pe"))
        block.scalar(runner("act"))
        block.vector(runner("dve"))
        block.gpsimd(runner("pool"))
        block.sync(runner("sync"))

    sbuf_ctx.close()
    psum_ctx.close()
    sem_ctx.close()
    return nc


_IN_NAMES = {
    "fr_w": "L0_attn_fr_w", "dp_w": "L0_attn_dp_w", "bp_w": "L0_attn_bp_w",
    "cp_w": "L0_attn_cp_w", "m1_w": "L0_attn_m1_w", "m2_w": "L0_attn_m2_w",
    "gate_w": "L0_attn_gate_w", "ffn1_w": "L0_attn_ffn1_w", "ffn2_w": "L0_attn_ffn2_w",
    "W1": "L0_ffn_W1", "W2": "L0_ffn_W2", "Wr": "L0_ffn_Wr",
    "norm_g": "L0_attn_norm_g", "norm_b": "L0_attn_norm_b",
    "fr_b": "L0_attn_fr_b", "dp_b": "L0_attn_dp_b", "bp_b": "L0_attn_bp_b",
    "cp_b": "L0_attn_cp_b", "skip": "L0_attn_skip", "m1_b": "L0_attn_m1_b",
    "m2_b": "L0_attn_m2_b", "gate_b": "L0_attn_gate_b",
    "ffn_g": "L0_attn_ffn_g", "ffn_bn": "L0_attn_ffn_bn",
    "ffn1_b": "L0_attn_ffn1_b", "ffn2_b": "L0_attn_ffn2_b",
    "b1": "L0_ffn_b1", "b2": "L0_ffn_b2",
}


def kernel(**inputs):
    x = np.ascontiguousarray(np.asarray(inputs["L0_attn_x"], dtype=np.float32))
    w = {k: np.ascontiguousarray(np.asarray(inputs[v], dtype=np.float32))
         for k, v in _IN_NAMES.items()}
    nc = build()
    in_maps = []
    for c in range(8):
        b, h = c // 2, c % 2
        xb = x[b]
        pref = xb[0:4096].T
        own = xb[h * 4096:(h + 1) * 4096].T
        full = np.concatenate([pref, own], axis=1)
        tiled = full.reshape(8, P, 16, NT).transpose(2, 1, 0, 3).reshape(16, P, C * NT)
        m = dict(w)
        m["xfm"] = np.ascontiguousarray(tiled)
        m["mask"] = np.full((P, 1), float(h), np.float32)
        in_maps.append(m)
    res = run_bass_kernel_spmd(nc, in_maps, core_ids=list(range(8)))
    out = np.empty((4, 8192, 1024), np.float32)
    for c in range(8):
        b, h = c // 2, c % 2
        z = np.asarray(res.results[c]["x2"])
        rows = z.reshape(8, P, 4, D).transpose(0, 2, 1, 3).reshape(4096, D)
        out[b, h * 4096:(h + 1) * 4096, :] = rows
    return out


# version 11
# speedup vs baseline: 1.0950x; 1.0950x over previous
import numpy as np
import concourse.bass as bass
import concourse.mybir as mybir
from concourse.bass_utils import run_bass_kernel_spmd
from contextlib import ExitStack

F32 = mybir.dt.float32
BF16 = mybir.dt.bfloat16
AF = mybir.ActivationFunctionType
ALU = mybir.AluOpType
AX = mybir.AxisListType

P = 128
NT = 512
C = 8
D = 1024
H2 = 2048
H4 = 4096
NE = 8
EPS = 1e-5
QUEUES = ("pe", "act", "dve", "pool", "sync")


class Step:
    __slots__ = ("q", "emit", "deps", "inc", "cum", "needs", "tag")

    def __init__(self, q, emit, inc, tag):
        self.q = q
        self.emit = emit
        self.deps = []
        self.inc = inc
        self.cum = None
        self.needs = False
        self.tag = tag


class Buf:
    __slots__ = ("ap", "w", "rs", "name")

    def __init__(self, ap, name=""):
        self.ap = ap
        self.w = None
        self.rs = []
        self.name = name


class Sched:
    def __init__(self):
        self.plan = {q: [] for q in QUEUES}

    def add(self, q, emit, reads=(), writes=(), extra=(), inc=1, dma=False, tag=""):
        st = Step(q, emit, 16 if dma else inc, tag)
        deps = []
        for d in extra:
            if d is not None:
                deps.append(d)
        for b in reads:
            if b.w is not None:
                deps.append(b.w)
        for b in writes:
            if b.w is not None:
                deps.append(b.w)
            deps.extend(b.rs)
        seen = set()
        for d in deps:
            if id(d) not in seen:
                seen.add(id(d))
                st.deps.append(d)
                d.needs = True
        if dma:
            st.needs = True
        for b in reads:
            b.rs.append(st)
        for b in writes:
            b.w = st
            b.rs = []
        self.plan[q].append(st)
        return st

    def finalize(self):
        for q in QUEUES:
            cum = 0
            for st in self.plan[q]:
                if st.needs:
                    cum += st.inc
                    st.cum = cum
                else:
                    st.cum = None


def build(n_p=8, n_m=8, n_e=NE, dbg=False):
    nc = bass.Bass("TRN2", target_bir_lowering=False, debug=False)
    S = Sched()

    xfm = nc.dram_tensor("xfm", [16, P, C * NT], F32, kind="ExternalInput").ap()
    maskd = nc.dram_tensor("mask", [P, 1], F32, kind="ExternalInput").ap()
    wD = {}
    for nm in ("fr_w", "dp_w", "bp_w", "cp_w"):
        wD[nm] = nc.dram_tensor(nm, [D, D], F32, kind="ExternalInput").ap()
    wD["m1_w"] = nc.dram_tensor("m1_w", [D, H2], F32, kind="ExternalInput").ap()
    wD["m2_w"] = nc.dram_tensor("m2_w", [H2, D], F32, kind="ExternalInput").ap()
    wD["gate_w"] = nc.dram_tensor("gate_w", [H2, D], F32, kind="ExternalInput").ap()
    wD["ffn1_w"] = nc.dram_tensor("ffn1_w", [D, H4], F32, kind="ExternalInput").ap()
    wD["ffn2_w"] = nc.dram_tensor("ffn2_w", [H4, D], F32, kind="ExternalInput").ap()
    wD["W1"] = nc.dram_tensor("W1", [NE, D, H4], F32, kind="ExternalInput").ap()
    wD["W2"] = nc.dram_tensor("W2", [NE, H4, D], F32, kind="ExternalInput").ap()
    wD["Wr"] = nc.dram_tensor("Wr", [D, NE], F32, kind="ExternalInput").ap()
    pD = {}
    for nm, sz in (("norm_g", D), ("norm_b", D), ("fr_b", D), ("dp_b", D),
                   ("bp_b", D), ("cp_b", D), ("skip", D), ("m1_b", H2),
                   ("m2_b", D), ("gate_b", D), ("ffn_g", D), ("ffn_bn", D),
                   ("ffn1_b", H4), ("ffn2_b", D)):
        pD[nm] = nc.dram_tensor(nm, [sz], F32, kind="ExternalInput").ap()
    pD["b1"] = nc.dram_tensor("b1", [NE, H4], F32, kind="ExternalInput").ap()
    pD["b2"] = nc.dram_tensor("b2", [NE, D], F32, kind="ExternalInput").ap()
    x2 = nc.dram_tensor("x2", [8, P, 4 * D], F32, kind="ExternalOutput").ap()
    dbg_out = {}
    if dbg:
        for nm in ("d_xn", "d_ssd", "d_mam", "d_y", "d_x1"):
            dbg_out[nm] = nc.dram_tensor(nm, [P, C * NT], F32, kind="ExternalOutput").ap()

    scr = {}
    for nm, mg in (("fr_w", 2), ("dp_w", 2), ("bp_w", 2), ("cp_w", 2),
                   ("m1_w", 4), ("ffn1_w", 8)):
        scr[nm] = nc.dram_tensor("s_" + nm, [mg, P, 8 * 512], BF16, kind="Internal").ap()
    for nm, nq in (("m2_w", 4), ("gate_w", 4), ("ffn2_w", 8)):
        scr[nm] = nc.dram_tensor("s_" + nm, [nq, P, 4 * D], BF16, kind="Internal").ap()
    scr["W1"] = nc.dram_tensor("s_W1", [NE, 8, P, 8 * 512], BF16, kind="Internal").ap()
    scr["W2"] = nc.dram_tensor("s_W2", [NE, 2, P, 32 * 512], BF16, kind="Internal").ap()
    sbuf_ctx = ExitStack()

    def sb(name, shape, dt):
        return sbuf_ctx.enter_context(nc.sbuf_tensor(name, shape, dt))

    fx0 = sb("fx0", [P, C * NT], F32)
    fxn = sb("fxn", [P, C * NT], F32)
    fscr = sb("fscr", [P, C * NT], F32)
    fa = sb("fa", [P, C * NT], F32)
    fst = sb("fst", [P, C * NT], F32)
    bxn = sb("bxn", [P, C * NT], BF16)
    blow = sb("blow", [P, C * NT], BF16)
    bhigh = sb("bhigh", [P, C * NT], BF16)
    bssd = sb("bssd", [P, C * NT], BF16)
    bmam = sb("bmam", [P, C * NT], BF16)
    hbuf = sb("hbuf", [P, 32 * NT], BF16)
    wsA = [sb(f"wsA{i}", [P, 8 * 512], BF16) for i in range(4)]
    pv = {}
    for nm in ("norm_g", "norm_b", "fr_b", "dp_b", "bp_b", "cp_b", "skip",
               "m2_b", "gate_b", "ffn_g", "ffn_bn", "ffn2_b"):
        pv[nm] = sb("p_" + nm, [P, C], F32)
    pv["m1_b"] = sb("p_m1_b", [P, 16], F32)
    pv["ffn1_b"] = sb("p_ffn1_b", [P, 32], F32)
    pv["b1"] = sb("p_b1", [P, NE * 32], F32)
    b2bf = sb("b2bf", [NE, D], BF16)
    wrbf = sb("wrbf", [P, C * NE], BF16)
    masks = sb("masks", [P, 1], F32)
    epst = sb("epst", [P, 1], F32)
    ident = sb("ident", [P, P], F32)
    ones_f = sb("ones_f", [P, 1], F32)
    ones_row = sb("ones_row", [1, P], BF16)
    nones_row = sb("nones_row", [1, P], BF16)
    RW = sb("RW", [P, 6 * 512], F32)
    RWB = sb("RWB", [P, 4 * 512], BF16)
    carP = sb("carP", [P, C], F32)
    carM = sb("carM", [P, C], F32)
    minit = sb("minit", [P, C], F32)
    comb = sb("comb", [P, 4 * NE], F32)
    combT = sb("combT", [NE, 4 * P], BF16)
    tkt = sb("tkt", [P, 8], F32)
    eqt = sb("eqt", [P, 16], F32)
    mskt = sb("mskt", [P, 8], F32)
    acc = fst

    psum_ctx = ExitStack()
    pbank = [psum_ctx.enter_context(nc.psum_tensor(f"pb{i}", [P, NT], F32)) for i in range(8)]

    B = lambda t, nm: Buf(t, nm)
    bufs = {}
    for nm, t in [("fx0", fx0), ("fxn", fxn), ("fscr", fscr), ("fa", fa),
                  ("fst", fst), ("bxn", bxn),
                  ("blow", blow), ("bhigh", bhigh),
                  ("bssd", bssd), ("bmam", bmam), ("hbuf", hbuf),
                  ("RW", RW), ("RWB", RWB),
                  ("carP", carP), ("carM", carM), ("minit", minit),
                  ("comb", comb), ("combT", combT), ("tkt", tkt), ("eqt", eqt),
                  ("mskt", mskt), ("b2bf", b2bf),
                  ("masks", masks), ("ident", ident)]:
        bufs[nm] = B(t, nm)
    bufs["acc"] = bufs["fst"]
    bufs["bmam0"] = Buf(bmam, "bmam0")
    bufs["bmam1"] = Buf(bmam, "bmam1")
    pB = [B(pbank[i], f"pb{i}") for i in range(8)]
    wsB = [B(t, f"wsA{i}") for i, t in enumerate(wsA)]
    pvB = {nm: B(t, "p_" + nm) for nm, t in pv.items()}
    scrB = {nm: B(a, "s_" + nm) for nm, a in scr.items()}
    x2B = B(x2, "x2")

    ring = {"ws": 0, "wv2": 0}
    wv2B = [(bhigh, [bufs["bhigh"]]),
            (bmam, [bufs["bmam"], bufs["bmam0"], bufs["bmam1"]])]

    def ring_next(kind, bl):
        i = ring[kind] % len(bl)
        ring[kind] += 1
        return bl[i]

    def dma(q, out_ap, in_ap, reads=(), writes=(), extra=(), tag="", slow=False):
        def em(e):
            return e.dma_start(out=out_ap, in_=in_ap,
                               **({"allow_slow_non_contiguous": True} if slow else {}))
        return S.add(q, em, reads, writes, extra, dma=True, tag=tag)

    def act(fn, reads=(), writes=(), extra=(), tag=""):
        return S.add("act", fn, reads, writes, extra, tag=tag)

    def dve(fn, reads=(), writes=(), extra=(), tag=""):
        return S.add("dve", fn, reads, writes, extra, tag=tag)

    def pe(fn, reads=(), writes=(), extra=(), tag=""):
        return S.add("pe", fn, reads, writes, extra, tag=tag)

    def chunk(t, c, n=NT):
        return t[:, c * n:(c + 1) * n]

    for nm in pv:
        src = pD[nm]
        w = pv[nm].shape[1]
        if nm == "b1":
            ap = src.rearrange("e (mk p) -> p e mk", p=P)
            dst = pv[nm][:, :].rearrange("p (e mk) -> p e mk", e=NE)
        else:
            ap = src.rearrange("(c p) -> p c", p=P)
            dst = pv[nm][:, :]
        dma("sync", dst, ap, writes=[pvB[nm]], tag=f"ld_{nm}", slow=True)
    dma("pool", b2bf[:, :], pD["b2"], writes=[bufs["b2bf"]], tag="ld_b2")
    dma("sync", masks[:, :], maskd, writes=[bufs["masks"]], tag="ld_mask")
    dma("pool", wrbf[:, :].rearrange("p (k e) -> p k e", k=C),
        wD["Wr"].rearrange("(k p) e -> p k e", p=P),
        writes=[Buf(wrbf, "wrbf")], tag="ld_wr", slow=True)
    wrB = Buf(wrbf, "wrbf")
    wrB.w = S.plan["pool"][-1]
    for nm in ("fr_b", "dp_b", "gate_b"):
        def em(v, t=pv[nm]):
            return v.tensor_scalar(t[:, :], t[:, :], 0.5, None, ALU.mult)
        dve(em, reads=[pvB[nm]], writes=[pvB[nm]], tag=f"half_{nm}")
    S.add("pool", lambda g: g.memset(ident[:, :], 0.0), writes=[bufs["ident"]], tag="id0")
    S.add("pool", lambda g: g.affine_select(
        out=ident[:, :], in_=ident[:, :], compare_op=ALU.not_equal, fill=1.0,
        base=0, pattern=[[-1, P]], channel_multiplier=1),
        reads=[bufs["ident"]], writes=[bufs["ident"]], tag="id1")
    onesFB = Buf(ones_f, "onesf")
    dve(lambda v: v.memset(ones_f[:, :], 1.0), writes=[onesFB], tag="ones_f")
    onesRB = Buf(ones_row, "onesr")
    dve(lambda v: v.memset(ones_row[:, :], 1.0), writes=[onesRB], tag="ones_r")
    dve(lambda v: v.memset(nones_row[:, :], -1.0), writes=[onesRB], tag="nones_r")
    epsB = Buf(epst, "eps")
    dve(lambda v: v.memset(epst[:, :], EPS), writes=[epsB], tag="eps")
    carPB = bufs["carP"]
    dve(lambda v: v.memset(carP[:, :], 0.0), writes=[carPB], tag="carP0")
    dve(lambda v: v.memset(carM[:, :], 0.0), writes=[bufs["carM"]], tag="carM0")

    converted = set()

    def stream_mg_slab(nm, mg, nk=8):
        sbuf = ring_next("ws", wsB)
        if nm.startswith("W1_"):
            e = int(nm[3:])
            sc, scb, wsrc = scr["W1"][e, mg], scrB["W1"], wD["W1"][e]
        else:
            sc, scb, wsrc = scr[nm][mg], scrB[nm], wD[nm]
        key = ("mg", nm, mg)
        if key not in converted:
            converted.add(key)
            fsrc = wsrc[:, mg * 512:(mg + 1) * 512].rearrange("(k p) f -> p k f", p=P)
            dma("pool", sbuf.ap[:, 0:nk * 512].rearrange("p (k f) -> p k f", k=nk),
                fsrc, writes=[sbuf], tag=f"cv_{nm}_{mg}")
            dma("sync", sc[:, 0:nk * 512], sbuf.ap[:, 0:nk * 512],
                reads=[sbuf], writes=[scb], tag=f"wb_{nm}_{mg}")
        else:
            dma("sync", sbuf.ap[:, 0:nk * 512], sc[:, 0:nk * 512],
                reads=[scb], writes=[sbuf], tag=f"st_{nm}_{mg}")
        return sbuf

    def stream_kfat(nm, kq):
        sbuf = ring_next("ws", wsB)
        key = ("ks", nm, kq)
        if key not in converted:
            converted.add(key)
            fsrc = wD[nm][kq * 512:(kq + 1) * 512, :].rearrange("(k p) f -> p k f", p=P)
            dma("pool", sbuf.ap[:, :].rearrange("p (k f) -> p k f", k=4),
                fsrc, writes=[sbuf], tag=f"cv_{nm}_{kq}")
            dma("sync", scr[nm][kq], sbuf.ap[:, :],
                reads=[sbuf], writes=[scrB[nm]], tag=f"wb_{nm}_{kq}")
        else:
            dma("sync", sbuf.ap[:, :], scr[nm][kq],
                reads=[scrB[nm]], writes=[sbuf], tag=f"st_{nm}_{kq}")
        return sbuf

    def stream_w2half(e, fh, hq):
        tens, wbl = wv2B[ring["wv2"] % 2]
        ring["wv2"] += 1
        key = ("w2", e, fh, hq)
        if key not in converted:
            converted.add(key)
            fsrc = wD["W2"][e][hq * 1024:(hq + 1) * 1024, fh * 512:(fh + 1) * 512] \
                .rearrange("(k p) f -> p k f", p=P)
            dma("pool", tens[:, 0:4096].rearrange("p (k f) -> p k f", k=8),
                fsrc, writes=wbl, tag=f"cv_W2_{e}_{fh}_{hq}")
            dma("sync", scr["W2"][e, fh][:, hq * 4096:(hq + 1) * 4096], tens[:, 0:4096],
                reads=[wbl[0]], writes=[scrB["W2"]], tag=f"wb_W2_{e}_{fh}_{hq}")
        else:
            dma("sync", tens[:, 0:4096], scr["W2"][e, fh][:, hq * 4096:(hq + 1) * 4096],
                reads=[scrB["W2"]], writes=wbl, tag=f"st_W2_{e}_{fh}_{hq}")
        return wbl

    def mm_group(pbuf, pairs, reads, extra=(), tag=""):
        def em(t):
            last = None
            n = len(pairs)
            for i, (l, r) in enumerate(pairs):
                last = t.matmul(pbuf.ap[:, :], l, r,
                                start=(i == 0), stop=(i == n - 1))
            return last
        return pe(em, reads=reads, writes=[pbuf], extra=extra, tag=tag)

    rot_idx = [0]
    mid_idx = [0]

    def rot_bank():
        i = 6 + (rot_idx[0] % 2)
        rot_idx[0] += 1
        return pB[i]

    def mid_bank():
        i = 4 + (mid_idx[0] % 4)
        mid_idx[0] += 1
        return pB[i]

    quad_idx = [0]

    def next_quad():
        b0 = 4 if quad_idx[0] % 2 == 0 else 0
        quad_idx[0] += 1
        return b0

    def proj_1024(nm, rhs_bf, rhs_buf, epi, tag, bank=None):
        for mg in range(2):
            slab = stream_mg_slab(nm, mg)
            b0 = next_quad()

            def em(t, slab=slab, b0=b0):
                last = None
                for j in range(4):
                    for k in range(C):
                        last = t.matmul(pbank[b0 + j][:, :],
                                        slab.ap[:, k * 512 + j * P:k * 512 + j * P + P],
                                        chunk(rhs_bf, k),
                                        start=(k == 0), stop=(k == C - 1),
                                        skip_group_check=True)
                return last
            g = pe(em, reads=[slab, rhs_buf],
                   writes=[pB[b0], pB[b0 + 1], pB[b0 + 2], pB[b0 + 3]],
                   tag=f"{tag}_mg{mg}")
            for j in range(4):
                epi(mg * 4 + j, pB[b0 + j], g)

    def ln_stats(src_buf, src_t, sq_buf, sq_t, tag):
        g1 = pe(lambda t: [t.matmul(pbank[6][0:1, :], ones_f[:, 0:1], chunk(src_t, k),
                                    start=(k == 0), stop=(k == C - 1)) for k in range(C)][-1],
                reads=[src_buf, onesFB], writes=[pB[6]], tag=f"{tag}_sum")
        g2 = pe(lambda t: [t.matmul(pbank[7][0:1, :], ones_f[:, 0:1], chunk(sq_t, k),
                                    start=(k == 0), stop=(k == C - 1)) for k in range(C)][-1],
                reads=[sq_buf, onesFB], writes=[pB[7]], tag=f"{tag}_sq")
        return g1, g2

    RWb, RWBb = bufs["RW"], bufs["RWB"]

    def Rr(j):
        return RW[0:1, j * 512:(j + 1) * 512]

    def Br(j):
        return RWB[0:1, j * 512:(j + 1) * 512]

    def rmul(dj, aj, bj, tag):
        dve(lambda v: v.tensor_tensor(Rr(dj), Rr(aj), Rr(bj), ALU.mult),
            reads=[RWb], writes=[RWb], tag=tag)

    def rneg(j, tag):
        dve(lambda v: v.tensor_scalar(Rr(j), Rr(j), -1.0, None, ALU.mult),
            reads=[RWb], writes=[RWb], tag=tag)

    def rsqrt_row(j, tag, src_j=None):
        sj = j if src_j is None else src_j
        act(lambda s: s.activation(Rr(j), Rr(sj), AF.Sqrt, bias=epst[0:1, 0:1]),
            reads=[RWb, epsB], writes=[RWb], tag=f"{tag}_sd")
        dve(lambda v: v.reciprocal(Rr(j), Rr(j)),
            reads=[RWb], writes=[RWb], tag=f"{tag}_rc")

    def hi_lo(src_j, hi_j, lo_j, scr_j, tag):
        dve(lambda v: v.tensor_copy(Br(hi_j), Rr(src_j)),
            reads=[RWb], writes=[RWBb], tag=f"{tag}_hi")
        dve(lambda v: v.tensor_copy(Rr(scr_j), Br(hi_j)),
            reads=[RWBb], writes=[RWb], tag=f"{tag}_hf")
        dve(lambda v: v.tensor_tensor(Rr(scr_j), Rr(src_j), Rr(scr_j), ALU.subtract),
            reads=[RWb], writes=[RWb], tag=f"{tag}_lf")
        dve(lambda v: v.tensor_copy(Br(lo_j), Rr(scr_j)),
            reads=[RWb], writes=[RWBb], tag=f"{tag}_lo")

    def bcast(hi_j, lo_j, pbuf, tag, neg=False):
        orow = nones_row if neg else ones_row
        def em(t):
            t.matmul(pbuf.ap[:, :], orow[0:1, :], Br(hi_j),
                     start=True, stop=False, skip_group_check=True)
            return t.matmul(pbuf.ap[:, :], orow[0:1, :], Br(lo_j),
                            start=False, stop=True, skip_group_check=True)
        return pe(em, reads=[RWBb, onesRB], writes=[pbuf], tag=tag)

    def ln_rows_mv(g1, g2, tag):
        dve(lambda v: v.tensor_scalar(Rr(0), pbank[6][0:1, :], 1.0 / D, None, ALU.mult),
            reads=[pB[6]], writes=[RWb], extra=[g1], tag=f"{tag}_m")
        dve(lambda v: v.tensor_scalar(Rr(1), pbank[7][0:1, :], 1.0 / D, None, ALU.mult),
            reads=[pB[7]], writes=[RWb], extra=[g2], tag=f"{tag}_ss")
        rmul(4, 0, 0, f"{tag}_m2")
        dve(lambda v: v.tensor_tensor(Rr(1), Rr(1), Rr(4), ALU.subtract),
            reads=[RWb], writes=[RWb], tag=f"{tag}_v")

    def ln12(x0b, tag):
        act(lambda s: s.activation(fxn[:, :], x0b.ap[:, :], AF.Square),
            reads=[x0b], writes=[bufs["fxn"]], tag=f"{tag}_sq")
        g1, g2 = ln_stats(x0b, fx0, bufs["fxn"], fxn, tag)
        ln_rows_mv(g1, g2, tag)
        rsqrt_row(2, f"{tag}_r1", src_j=1)
        rmul(4, 2, 2, f"{tag}_rq")
        rmul(4, 1, 4, f"{tag}_v2")
        rsqrt_row(4, f"{tag}_r2")
        rmul(3, 0, 2, f"{tag}_s2a")
        rmul(5, 2, 4, f"{tag}_rr")
        rmul(0, 0, 5, f"{tag}_mm")
        hi_lo(5, 0, 1, 4, f"{tag}_hrr")
        hi_lo(0, 2, 3, 4, f"{tag}_hmm")

    def ln_apply_xn(x0b, g_nm, b_nm, tag):
        p_rr = pB[4 + (rot_idx[0] % 2)]; rot_idx[0] += 1
        p_mm = pB[4 + (rot_idx[0] % 2)]; rot_idx[0] += 1
        bc1 = bcast(0, 1, p_rr, f"{tag}_bc1")
        bc2 = bcast(2, 3, p_mm, f"{tag}_bc2", neg=True)
        for c in range(C):
            dve(lambda v, c=c: v.tensor_tensor(chunk(fxn, c), chunk(x0b.ap, c),
                                               p_rr.ap[:, :], ALU.mult),
                reads=[x0b, p_rr], writes=[bufs["fxn"]], extra=[bc1], tag=f"{tag}_u{c}")
            dve(lambda v, c=c: v.tensor_tensor(chunk(fxn, c), chunk(fxn, c),
                                               p_mm.ap[:, :], ALU.add),
                reads=[bufs["fxn"], p_mm], writes=[bufs["fxn"]], extra=[bc2], tag=f"{tag}_a{c}")
            dve(lambda v, c=c: v.tensor_scalar(chunk(fxn, c), chunk(fxn, c),
                                               pv[g_nm][:, c:c + 1], pv[b_nm][:, c:c + 1],
                                               ALU.mult, ALU.add),
                reads=[bufs["fxn"], pvB[g_nm], pvB[b_nm]], writes=[bufs["fxn"]], tag=f"{tag}_g{c}")
        dve(lambda v: v.tensor_copy(bxn[:, :], fxn[:, :]),
            reads=[bufs["fxn"]], writes=[bufs["bxn"]], tag=f"{tag}_bf")

    def front(tile_idx, tag):
        x0b = bufs["fx0"]
        dma("sync", fx0[:, :], xfm[tile_idx], writes=[x0b], tag=f"{tag}_x0")
        ln12(x0b, f"{tag}_ln")
        ln_apply_xn(x0b, "norm_g", "norm_b", f"{tag}_xn")
        return x0b

    def fr_low_high(with_high, tag):
        xnbfb = bufs["bxn"]

        def epi(m, pb_, g):
            act(lambda s, m=m, pb_=pb_: s.activation(
                chunk(fscr, m), pb_.ap[:, :], AF.Tanh,
                bias=pv["fr_b"][:, m:m + 1], scale=0.5),
                reads=[pb_, pvB["fr_b"]], writes=[bufs["fscr"]], extra=[g], tag=f"{tag}_e{m}")
        proj_1024("fr_w", bxn, xnbfb, epi, tag)
        dve(lambda v: v.tensor_scalar(fscr[:, :], fscr[:, :], 0.5, 0.5, ALU.mult, ALU.add),
            reads=[bufs["fscr"]], writes=[bufs["fscr"]], tag=f"{tag}_lg")
        dve(lambda v: v.tensor_tensor(blow[:, :], fxn[:, :], fscr[:, :], ALU.mult),
            reads=[bufs["fxn"], bufs["fscr"]], writes=[bufs["blow"]], tag=f"{tag}_low")
        if with_high:
            dve(lambda v: v.tensor_scalar(fscr[:, :], fscr[:, :], -1.0, 1.0, ALU.mult, ALU.add),
                reads=[bufs["fscr"]], writes=[bufs["fscr"]], tag=f"{tag}_ilg")
            dve(lambda v: v.tensor_tensor(bhigh[:, :], fxn[:, :], fscr[:, :], ALU.mult),
                reads=[bufs["fxn"], bufs["fscr"]], writes=[bufs["bhigh"]], tag=f"{tag}_high")

    def dp_bp_scan(init_buf, init_zero, car_buf, tag):
        lowb = bufs["blow"]

        def epi_a(m, pb_, g):
            act(lambda s, m=m, pb_=pb_: s.activation(
                chunk(fa, m), pb_.ap[:, :], AF.Tanh,
                bias=pv["dp_b"][:, m:m + 1], scale=0.5),
                reads=[pb_, pvB["dp_b"]], writes=[bufs["fa"]], extra=[g], tag=f"{tag}_ea{m}")
        proj_1024("dp_w", blow, lowb, epi_a, f"{tag}_dp")
        dve(lambda v: v.tensor_scalar(fa[:, :], fa[:, :], 0.5, 0.5, ALU.mult, ALU.add),
            reads=[bufs["fa"]], writes=[bufs["fa"]], tag=f"{tag}_a")

        def epi_b(m, pb_, g):
            act(lambda s, m=m, pb_=pb_: s.activation(
                chunk(fscr, m), pb_.ap[:, :], AF.Tanh,
                bias=pv["bp_b"][:, m:m + 1]),
                reads=[pb_, pvB["bp_b"]], writes=[bufs["fscr"]], extra=[g], tag=f"{tag}_eb{m}")
        proj_1024("bp_w", blow, lowb, epi_b, f"{tag}_bp")
        dve(lambda v: v.tensor_tensor(fscr[:, :], fscr[:, :], fxn[:, :], ALU.mult),
            reads=[bufs["fscr"], bufs["fxn"]], writes=[bufs["fscr"]], tag=f"{tag}_b")
        for c in range(C):
            if init_zero:
                ini = 0.0
                rd = [bufs["fa"], bufs["fscr"]]
            else:
                ini = init_buf.ap[:, c:c + 1]
                rd = [bufs["fa"], bufs["fscr"], init_buf]
            dve(lambda v, c=c, ini=ini: v.tensor_tensor_scan(
                chunk(fst, c), chunk(fa, c), chunk(fscr, c), ini, ALU.mult, ALU.add),
                reads=rd, writes=[bufs["fst"]], tag=f"{tag}_sc{c}")
        dve(lambda v: v.tensor_copy(
            car_buf.ap[:, :], fst[:, :].rearrange("p (c t) -> p c t", c=C)[:, :, NT - 1]),
            reads=[bufs["fst"]], writes=[car_buf], tag=f"{tag}_car")

    for k in range(n_p):
        tag = f"P{k}"
        front(k, tag)
        fr_low_high(False, f"{tag}_fr")
        dp_bp_scan(carPB, k == 0, carPB, tag)

    dve(lambda v: v.tensor_scalar(minit[:, :], carP[:, :], masks[:, 0:1], None, ALU.mult),
        reads=[carPB, bufs["masks"]], writes=[bufs["minit"]], tag="minit")

    if n_m > 0:
        front(8, "M0")
    for k in range(n_m):
        tag = f"M{k}"
        x0b = bufs["fx0"]
        fr_low_high(True, f"{tag}_fr")
        init_b = bufs["minit"] if k == 0 else bufs["carM"]
        dp_bp_scan(init_b, False, bufs["carM"], tag)
        if dbg and k == 0:
            dma("act", dbg_out["d_xn"], fxn[:, :], reads=[bufs["fxn"]], tag="dbg_xn")

        def epi_c(m, pb_, g):
            act(lambda s, m=m, pb_=pb_: s.activation(
                chunk(fscr, m), pb_.ap[:, :], AF.Tanh, bias=pv["cp_b"][:, m:m + 1]),
                reads=[pb_, pvB["cp_b"]], writes=[bufs["fscr"]], extra=[g], tag=f"{tag}_ec{m}")
        proj_1024("cp_w", bhigh, bufs["bhigh"], epi_c, f"{tag}_cp", bank=mid_bank)
        dve(lambda v: v.tensor_tensor(fst[:, :], fscr[:, :], fst[:, :], ALU.mult),
            reads=[bufs["fscr"], bufs["fst"]], writes=[bufs["fst"]], tag=f"{tag}_s1")
        for c in range(C):
            dve(lambda v, c=c: v.scalar_tensor_tensor(
                chunk(fst, c), chunk(fxn, c), pv["skip"][:, c:c + 1], chunk(fst, c),
                ALU.mult, ALU.add),
                reads=[bufs["fxn"], pvB["skip"], bufs["fst"]], writes=[bufs["fst"]],
                tag=f"{tag}_ssd{c}")
        dve(lambda v: v.tensor_copy(bssd[:, :], fst[:, :]),
            reads=[bufs["fst"]], writes=[bufs["bssd"]], tag=f"{tag}_ssdbf")
        if dbg and k == 0:
            dma("act", dbg_out["d_ssd"], fst[:, :], reads=[bufs["fst"]], tag="dbg_ssd")

        for mg in range(4):
            slab = stream_mg_slab("m1_w", mg)
            b0 = next_quad()

            def em(t, slab=slab, b0=b0):
                last = None
                for j in range(4):
                    for kk in range(C):
                        last = t.matmul(pbank[b0 + j][:, :],
                                        slab.ap[:, kk * 512 + j * P:kk * 512 + j * P + P],
                                        chunk(blow, kk),
                                        start=(kk == 0), stop=(kk == C - 1),
                                        skip_group_check=True)
                return last
            g = pe(em, reads=[slab, bufs["blow"]],
                   writes=[pB[b0], pB[b0 + 1], pB[b0 + 2], pB[b0 + 3]],
                   tag=f"{tag}_m1_{mg}")
            for j in range(4):
                m = mg * 4 + j
                act(lambda s, m=m, j=j, b0=b0: s.activation(
                    hbuf[:, m * 512:(m + 1) * 512], pbank[b0 + j][:, :], AF.Gelu,
                    bias=pv["m1_b"][:, m:m + 1]),
                    reads=[pB[b0 + j], pvB["m1_b"]], writes=[bufs["hbuf"]], extra=[g],
                    tag=f"{tag}_m1e{m}")

        def half_m_kslab(nm, nk, rhs_fn, rhs_bufs, epi, tag2, rhs_bufs_fn=None):
            for mg in range(2):
                g = None
                for kq in range(nk // 4):
                    slab = stream_kfat(nm, kq)
                    rbufs = rhs_bufs_fn(kq) if rhs_bufs_fn else rhs_bufs

                    def em(t, kq=kq, mg=mg, slab=slab):
                        last = None
                        for ki in range(4):
                            kk = kq * 4 + ki
                            for j in range(4):
                                m = mg * 4 + j
                                last = t.matmul(pbank[j][:, :],
                                                slab.ap[:, ki * D + m * P:ki * D + (m + 1) * P],
                                                rhs_fn(kk), start=(kk == 0), stop=(kk == nk - 1),
                                                skip_group_check=True)
                        return last
                    g = pe(em, reads=[slab] + rbufs,
                           writes=[pB[0], pB[1], pB[2], pB[3]], tag=f"{tag2}_{mg}_{kq}")
                for j in range(4):
                    epi(mg * 4 + j, pB[j], g)

        def m2_epi(m, pb_, g):
            dve(lambda v, m=m, pb_=pb_: v.tensor_scalar(
                chunk(fa, m), pb_.ap[:, :], pv["m2_b"][:, m:m + 1], None, ALU.add),
                reads=[pb_, pvB["m2_b"]], writes=[bufs["fa"]], extra=[g], tag=f"{tag}_m2e{m}")
        half_m_kslab("m2_w", 16, lambda kk: hbuf[:, kk * 512:(kk + 1) * 512],
                     [bufs["hbuf"]], m2_epi, f"{tag}_m2")
        dve(lambda v: v.tensor_copy(bmam[:, 0:4 * NT], fa[:, 0:4 * NT]),
            reads=[bufs["fa"]], writes=[bufs["bmam0"]], tag=f"{tag}_mambf0")
        dve(lambda v: v.tensor_copy(bmam[:, 4 * NT:8 * NT], fa[:, 4 * NT:8 * NT]),
            reads=[bufs["fa"]], writes=[bufs["bmam1"]], tag=f"{tag}_mambf1")
        if dbg and k == 0:
            dma("act", dbg_out["d_mam"], fa[:, :], reads=[bufs["fa"]], tag="dbg_mam")

        def gate_rhs(kk):
            return chunk(bssd, kk) if kk < 8 else chunk(bmam, kk - 8)

        def gate_epi(m, pb_, g):
            act(lambda s, m=m, pb_=pb_: s.activation(
                chunk(fscr, m), pb_.ap[:, :], AF.Tanh,
                bias=pv["gate_b"][:, m:m + 1], scale=0.5),
                reads=[pb_, pvB["gate_b"]], writes=[bufs["fscr"]], extra=[g],
                tag=f"{tag}_ge{m}")
        half_m_kslab("gate_w", 16, gate_rhs, None, gate_epi, f"{tag}_gate",
                     rhs_bufs_fn=lambda kq: [bufs["bssd"]] if kq < 2 else
                     [bufs["bmam0"] if kq == 2 else bufs["bmam1"]])

        dve(lambda v: v.tensor_tensor(fst[:, :], fst[:, :], fa[:, :], ALU.subtract),
            reads=[bufs["fst"], bufs["fa"]], writes=[bufs["fst"]], tag=f"{tag}_d")
        dve(lambda v: v.tensor_tensor(fscr[:, :], fscr[:, :], fst[:, :], ALU.mult),
            reads=[bufs["fscr"], bufs["fst"]], writes=[bufs["fscr"]], tag=f"{tag}_e")
        dve(lambda v: v.tensor_tensor(fst[:, :], fst[:, :], fscr[:, :], ALU.add),
            reads=[bufs["fst"], bufs["fscr"]], writes=[bufs["fst"]], tag=f"{tag}_f")
        dve(lambda v: v.scalar_tensor_tensor(fst[:, :], fst[:, :], 0.5, fa[:, :],
                                             ALU.mult, ALU.add),
            reads=[bufs["fst"], bufs["fa"]], writes=[bufs["fst"]], tag=f"{tag}_u")
        hi_lo(2, 0, 1, 4, f"{tag}_hr1")
        hi_lo(3, 2, 3, 4, f"{tag}_hs2")
        bc1 = bcast(0, 1, pB[4], f"{tag}_br1")
        bc2 = bcast(2, 3, pB[5], f"{tag}_bs2", neg=True)
        for c in range(C):
            dve(lambda v, c=c: v.tensor_tensor(chunk(fxn, c), chunk(fx0, c),
                                               pbank[4][:, :], ALU.mult),
                reads=[x0b, pB[4]], writes=[bufs["fxn"]], extra=[bc1], tag=f"{tag}_y1{c}")
            dve(lambda v, c=c: v.tensor_tensor(chunk(fxn, c), chunk(fxn, c),
                                               pbank[5][:, :], ALU.add),
                reads=[bufs["fxn"], pB[5]], writes=[bufs["fxn"]], extra=[bc2], tag=f"{tag}_y2{c}")
        dve(lambda v: v.tensor_tensor(fxn[:, :], fxn[:, :], fst[:, :], ALU.add),
            reads=[bufs["fxn"], bufs["fst"]], writes=[bufs["fxn"]], tag=f"{tag}_y3")
        if dbg and k == 0:
            dma("act", dbg_out["d_y"], fxn[:, :], reads=[bufs["fxn"]], tag="dbg_y")

        act(lambda s: s.activation(fst[:, :], fxn[:, :], AF.Square),
            reads=[bufs["fxn"]], writes=[bufs["fst"]], tag=f"{tag}_ysq")
        g1, g2 = ln_stats(bufs["fxn"], fxn, bufs["fst"], fst, f"{tag}_l3")
        ln_rows_mv(g1, g2, f"{tag}_l3")
        rsqrt_row(5, f"{tag}_l3r", src_j=1)
        rmul(0, 0, 5, f"{tag}_l3sa")
        hi_lo(5, 0, 1, 1, f"{tag}_l3h1")
        hi_lo(0, 2, 3, 1, f"{tag}_l3h2")
        bc1 = bcast(0, 1, pB[4], f"{tag}_l3b1")
        bc2 = bcast(2, 3, pB[5], f"{tag}_l3b2", neg=True)
        for c in range(C):
            dve(lambda v, c=c: v.tensor_tensor(chunk(fst, c), chunk(fxn, c),
                                               pbank[4][:, :], ALU.mult),
                reads=[bufs["fxn"], pB[4]], writes=[bufs["fst"]], extra=[bc1], tag=f"{tag}_l3u{c}")
            dve(lambda v, c=c: v.tensor_tensor(chunk(fst, c), chunk(fst, c),
                                               pbank[5][:, :], ALU.add),
                reads=[bufs["fst"], pB[5]], writes=[bufs["fst"]], extra=[bc2], tag=f"{tag}_l3a{c}")
            dve(lambda v, c=c: v.tensor_scalar(chunk(bxn, c), chunk(fst, c),
                                               pv["ffn_g"][:, c:c + 1], pv["ffn_bn"][:, c:c + 1],
                                               ALU.mult, ALU.add),
                reads=[bufs["fst"], pvB["ffn_g"], pvB["ffn_bn"]], writes=[bufs["bxn"]],
                tag=f"{tag}_l3g{c}")

        for mg in range(8):
            slab = stream_mg_slab("ffn1_w", mg)
            b0 = next_quad()

            def em(t, slab=slab, b0=b0):
                last = None
                for j in range(4):
                    for kk in range(C):
                        last = t.matmul(pbank[b0 + j][:, :],
                                        slab.ap[:, kk * 512 + j * P:kk * 512 + j * P + P],
                                        chunk(bxn, kk),
                                        start=(kk == 0), stop=(kk == C - 1),
                                        skip_group_check=True)
                return last
            g = pe(em, reads=[slab, bufs["bxn"]],
                   writes=[pB[b0], pB[b0 + 1], pB[b0 + 2], pB[b0 + 3]],
                   tag=f"{tag}_f1_{mg}")
            for j in range(4):
                m = mg * 4 + j
                act(lambda s, m=m, j=j, b0=b0: s.activation(
                    hbuf[:, m * 512:(m + 1) * 512], pbank[b0 + j][:, :], AF.Gelu,
                    bias=pv["ffn1_b"][:, m:m + 1]),
                    reads=[pB[b0 + j], pvB["ffn1_b"]], writes=[bufs["hbuf"]], extra=[g],
                    tag=f"{tag}_f1e{m}")

        def f2_epi(m, pb_, g):
            dve(lambda v, m=m, pb_=pb_: v.scalar_tensor_tensor(
                chunk(fa, m), pb_.ap[:, :], pv["ffn2_b"][:, m:m + 1], chunk(fxn, m),
                ALU.add, ALU.add),
                reads=[pb_, pvB["ffn2_b"], bufs["fxn"]], writes=[bufs["fa"]], extra=[g],
                tag=f"{tag}_f2e{m}")
        half_m_kslab("ffn2_w", 32, lambda kk: hbuf[:, kk * 512:(kk + 1) * 512],
                     [bufs["hbuf"]], f2_epi, f"{tag}_f2")
        dve(lambda v: v.tensor_tensor(fa[:, :], fa[:, :], fx0[:, :], ALU.add),
            reads=[bufs["fa"], x0b], writes=[bufs["fa"]], tag=f"{tag}_x1")
        if dbg and k == 0:
            dma("act", dbg_out["d_x1"], fa[:, :], reads=[bufs["fa"]], tag="dbg_x1")

        act(lambda s: s.activation(fst[:, :], fa[:, :], AF.Square),
            reads=[bufs["fa"]], writes=[bufs["fst"]], tag=f"{tag}_x1sq")
        g1, g2 = ln_stats(bufs["fa"], fa, bufs["fst"], fst, f"{tag}_l4")
        ln_rows_mv(g1, g2, f"{tag}_l4")
        rsqrt_row(5, f"{tag}_l4r", src_j=1)
        rmul(0, 0, 5, f"{tag}_l4sa")
        hi_lo(5, 0, 1, 1, f"{tag}_l4h1")
        hi_lo(0, 2, 3, 1, f"{tag}_l4h2")
        bc1 = bcast(0, 1, pB[4], f"{tag}_l4b1")
        bc2 = bcast(2, 3, pB[5], f"{tag}_l4b2", neg=True)
        for c in range(C):
            dve(lambda v, c=c: v.tensor_tensor(chunk(fst, c), chunk(fa, c),
                                               pbank[4][:, :], ALU.mult),
                reads=[bufs["fa"], pB[4]], writes=[bufs["fst"]], extra=[bc1], tag=f"{tag}_l4u{c}")
            dve(lambda v, c=c: v.tensor_tensor(chunk(blow, c), chunk(fst, c),
                                               pbank[5][:, :], ALU.add),
                reads=[bufs["fst"], pB[5]], writes=[bufs["blow"]], extra=[bc2], tag=f"{tag}_l4a{c}")

        for tc in range(4):
            def em_t(t, tc=tc):
                last = None
                for cc in range(C):
                    pbk = pbank[4 + cc // 4]
                    last = t.transpose(pbk[:, (cc % 4) * P:(cc % 4 + 1) * P],
                                       fa[:, cc * 512 + tc * P:cc * 512 + tc * P + P],
                                       ident[:, :])
                return last
            g = pe(em_t, reads=[bufs["fa"], bufs["ident"]], writes=[pB[4], pB[5]],
                   tag=f"{tag}_tp{tc}")
            act(lambda s, tc=tc: s.activation(acc[:, tc * D:tc * D + 512],
                                              pbank[4][:, :], AF.Copy),
                reads=[pB[4]], writes=[bufs["acc"]], extra=[g], tag=f"{tag}_tpa{tc}")
            act(lambda s, tc=tc: s.activation(acc[:, tc * D + 512:(tc + 1) * D],
                                              pbank[5][:, :], AF.Copy),
                reads=[pB[5]], writes=[bufs["acc"]], extra=[g], tag=f"{tag}_tpb{tc}")

        def em_lg(t):
            last = None
            for tc in range(4):
                for kk in range(C):
                    last = t.matmul(pbank[6][:, tc * 8:(tc + 1) * 8],
                                    blow[:, kk * 512 + tc * P:kk * 512 + tc * P + P],
                                    wrbf[:, kk * 8:(kk + 1) * 8],
                                    start=(kk == 0), stop=(kk == C - 1),
                                    skip_group_check=True)
            return last
        glg = pe(em_lg, reads=[bufs["blow"], wrB], writes=[pB[6]], tag=f"{tag}_lg")
        for tc in range(4):
            ls = pbank[6][:, tc * 8:(tc + 1) * 8]
            dve(lambda v, ls=ls: v.tensor_reduce(tkt[:, 0:1], ls, AX.X, ALU.max),
                reads=[pB[6]], writes=[bufs["tkt"]], extra=[glg], tag=f"{tag}_tk0{tc}")
            dve(lambda v, ls=ls: v.tensor_scalar(eqt[:, 0:8], ls, tkt[:, 0:1], None, ALU.is_equal),
                reads=[pB[6], bufs["tkt"]], writes=[bufs["eqt"]], tag=f"{tag}_tk1{tc}")
            dve(lambda v, ls=ls: v.scalar_tensor_tensor(mskt[:, :], eqt[:, 0:8], -1e9, ls,
                                                        ALU.mult, ALU.add),
                reads=[bufs["eqt"], pB[6]], writes=[bufs["mskt"]], tag=f"{tag}_tk2{tc}")
            dve(lambda v: v.tensor_reduce(tkt[:, 1:2], mskt[:, :], AX.X, ALU.max),
                reads=[bufs["mskt"]], writes=[bufs["tkt"]], tag=f"{tag}_tk3{tc}")
            dve(lambda v: v.tensor_tensor(tkt[:, 4:5], tkt[:, 0:1], tkt[:, 1:2], ALU.subtract),
                reads=[bufs["tkt"]], writes=[bufs["tkt"]], tag=f"{tag}_tk4{tc}")
            act(lambda s: s.activation(tkt[:, 5:6], tkt[:, 4:5], AF.Tanh, scale=0.5),
                reads=[bufs["tkt"]], writes=[bufs["tkt"]], tag=f"{tag}_tk5{tc}")
            dve(lambda v: v.tensor_scalar(tkt[:, 2:3], tkt[:, 5:6], 0.5, 0.5, ALU.mult, ALU.add),
                reads=[bufs["tkt"]], writes=[bufs["tkt"]], tag=f"{tag}_tk6{tc}")
            dve(lambda v: v.tensor_scalar(tkt[:, 3:4], tkt[:, 2:3], -1.0, 1.0, ALU.mult, ALU.add),
                reads=[bufs["tkt"]], writes=[bufs["tkt"]], tag=f"{tag}_tk7{tc}")
            dve(lambda v: v.tensor_scalar(eqt[:, 8:16], mskt[:, :], tkt[:, 1:2], None, ALU.is_equal),
                reads=[bufs["mskt"], bufs["tkt"]], writes=[bufs["eqt"]], tag=f"{tag}_tk8{tc}")
            dve(lambda v, tc=tc: v.tensor_scalar(comb[:, tc * 8:(tc + 1) * 8], eqt[:, 0:8],
                                                 tkt[:, 2:3], None, ALU.mult),
                reads=[bufs["eqt"], bufs["tkt"]], writes=[bufs["comb"]], tag=f"{tag}_tk9{tc}")
            dve(lambda v, tc=tc: v.scalar_tensor_tensor(
                comb[:, tc * 8:(tc + 1) * 8], eqt[:, 8:16], tkt[:, 3:4],
                comb[:, tc * 8:(tc + 1) * 8], ALU.mult, ALU.add),
                reads=[bufs["eqt"], bufs["tkt"], bufs["comb"]], writes=[bufs["comb"]],
                tag=f"{tag}_tkA{tc}")

        def em_ct(t):
            last = None
            for tc in range(4):
                last = t.transpose(pbank[7][0:NE, tc * P:(tc + 1) * P],
                                   comb[:, tc * 8:(tc + 1) * 8], ident[:, :])
            return last
        gct = pe(em_ct, reads=[bufs["comb"], bufs["ident"]], writes=[pB[7]], tag=f"{tag}_ct")
        act(lambda s: s.activation(combT[:, :], pbank[7][0:NE, :], AF.Copy),
            reads=[pB[7]], writes=[bufs["combT"]], extra=[gct], tag=f"{tag}_ctc")
        for tc in range(4):
            for fh in range(2):
                def em_b2(t, tc=tc, fh=fh):
                    return t.matmul(pbank[6][:, :], combT[0:NE, tc * P:(tc + 1) * P],
                                    b2bf[0:NE, fh * 512:(fh + 1) * 512],
                                    start=True, stop=True)
                g = pe(em_b2, reads=[bufs["combT"], bufs["b2bf"]], writes=[pB[6]],
                       tag=f"{tag}_b2_{tc}_{fh}")
                dve(lambda v, tc=tc, fh=fh: v.tensor_tensor(
                    acc[:, tc * D + fh * 512:tc * D + (fh + 1) * 512],
                    acc[:, tc * D + fh * 512:tc * D + (fh + 1) * 512],
                    pbank[6][:, :], ALU.add),
                    reads=[bufs["acc"], pB[6]], writes=[bufs["acc"]], extra=[g],
                    tag=f"{tag}_b2a_{tc}_{fh}")

        def emit_w2_quad(e, fh, hg2, half):
            hap = half[0].ap
            def em_w2(t, hg2=hg2, hap=hap):
                last = None
                for j in range(4):
                    mk = hg2 * 4 + j
                    for tc in range(4):
                        last = t.matmul(
                            pbank[tc][:, :],
                            hbuf[:, mk * 512 + tc * P:mk * 512 + tc * P + P],
                            hap[:, (mk % 8) * 512:(mk % 8 + 1) * 512],
                            start=(mk == 0), stop=(mk == 31),
                            skip_group_check=True)
                return last
            pe(em_w2, reads=[bufs["hbuf"]] + list(half),
               writes=[pB[0], pB[1], pB[2], pB[3]], tag=f"{tag}_w2_{e}_{fh}_{hg2}")

        for e in range(n_e):
            if e == 2 and k + 1 < n_m:
                front(8 + k + 1, f"M{k + 1}")
            for fh in range(2):
                halves = {}
                for hg in range(8):
                    if hg % 2 == 0:
                        halves[hg // 2] = stream_w2half(e, fh, hg // 2)
                    if fh == 0:
                        slab = stream_mg_slab(f"W1_{e}", hg)
                        for pr in range(2):
                            pb0 = 4 if (hg * 2 + pr) % 2 == 0 else 6
                            def em_w1(t, slab=slab, hg=hg, pr=pr, pb0=pb0):
                                last = None
                                for jj in range(2):
                                    j = pr * 2 + jj
                                    for kk in range(C):
                                        last = t.matmul(
                                            pbank[pb0 + jj][:, :],
                                            slab.ap[:, kk * 512 + j * P:kk * 512 + j * P + P],
                                            chunk(blow, kk),
                                            start=(kk == 0), stop=(kk == C - 1),
                                            skip_group_check=True)
                                return last
                            g = pe(em_w1, reads=[slab, bufs["blow"]],
                                   writes=[pB[pb0], pB[pb0 + 1]], tag=f"{tag}_w1_{e}_{hg}_{pr}")
                            for jj in range(2):
                                mk = hg * 4 + pr * 2 + jj
                                act(lambda s, mk=mk, jj=jj, e=e, pb0=pb0: s.activation(
                                    hbuf[:, mk * 512:(mk + 1) * 512], pbank[pb0 + jj][:, :],
                                    AF.Gelu,
                                    bias=pv["b1"][:, e * 32 + mk:e * 32 + mk + 1]),
                                    reads=[pB[pb0 + jj], pvB["b1"]], writes=[bufs["hbuf"]],
                                    extra=[g], tag=f"{tag}_w1e_{e}_{mk}")
                        if hg >= 1:
                            emit_w2_quad(e, fh, hg - 1, halves[(hg - 1) // 2])
                    else:
                        emit_w2_quad(e, fh, hg, halves[hg // 2])
                if fh == 0:
                    emit_w2_quad(e, fh, 7, halves[3])
                for tc in range(4):
                    dve(lambda v, tc=tc, fh=fh, e=e: v.scalar_tensor_tensor(
                        acc[:, tc * D + fh * 512:tc * D + (fh + 1) * 512],
                        pbank[tc][:, :], comb[:, tc * 8 + e:tc * 8 + e + 1],
                        acc[:, tc * D + fh * 512:tc * D + (fh + 1) * 512],
                        ALU.mult, ALU.add),
                        reads=[pB[tc], bufs["comb"], bufs["acc"]], writes=[bufs["acc"]],
                        tag=f"{tag}_ye_{e}_{fh}_{tc}")

        dma("act", x2[k], acc[:, :], reads=[bufs["acc"]], writes=[x2B], tag=f"{tag}_store")

    S.add("sync", lambda e: e.nop(), extra=[x2B.w] + x2B.rs, tag="drain")

    S.finalize()
    sem_ctx = ExitStack()
    sems = {q: sem_ctx.enter_context(nc.semaphore(f"sem_{q}")) for q in QUEUES}

    def runner(qname):
        def _(eng):
            clock = {q: 0 for q in QUEUES}
            for st in S.plan[qname]:
                needed = {}
                for d in st.deps:
                    if d.q == qname == "pe":
                        continue
                    if d.cum is None:
                        continue
                    if d.cum > needed.get(d.q, 0):
                        needed[d.q] = d.cum
                for qd, v in needed.items():
                    if clock[qd] < v:
                        eng.wait_ge(sems[qd], v)
                        clock[qd] = v
                last = st.emit(eng)
                if st.cum is not None:
                    if last is None:
                        raise RuntimeError(f"step {st.tag} returned no instruction")
                    last.then_inc(sems[qname], st.inc)
        return _

    with nc.Block() as block:
        block.tensor(runner("pe"))
        block.scalar(runner("act"))
        block.vector(runner("dve"))
        block.gpsimd(runner("pool"))
        block.sync(runner("sync"))

    sbuf_ctx.close()
    psum_ctx.close()
    sem_ctx.close()
    return nc


_IN_NAMES = {
    "fr_w": "L0_attn_fr_w", "dp_w": "L0_attn_dp_w", "bp_w": "L0_attn_bp_w",
    "cp_w": "L0_attn_cp_w", "m1_w": "L0_attn_m1_w", "m2_w": "L0_attn_m2_w",
    "gate_w": "L0_attn_gate_w", "ffn1_w": "L0_attn_ffn1_w", "ffn2_w": "L0_attn_ffn2_w",
    "W1": "L0_ffn_W1", "W2": "L0_ffn_W2", "Wr": "L0_ffn_Wr",
    "norm_g": "L0_attn_norm_g", "norm_b": "L0_attn_norm_b",
    "fr_b": "L0_attn_fr_b", "dp_b": "L0_attn_dp_b", "bp_b": "L0_attn_bp_b",
    "cp_b": "L0_attn_cp_b", "skip": "L0_attn_skip", "m1_b": "L0_attn_m1_b",
    "m2_b": "L0_attn_m2_b", "gate_b": "L0_attn_gate_b",
    "ffn_g": "L0_attn_ffn_g", "ffn_bn": "L0_attn_ffn_bn",
    "ffn1_b": "L0_attn_ffn1_b", "ffn2_b": "L0_attn_ffn2_b",
    "b1": "L0_ffn_b1", "b2": "L0_ffn_b2",
}


def kernel(**inputs):
    x = np.ascontiguousarray(np.asarray(inputs["L0_attn_x"], dtype=np.float32))
    w = {k: np.ascontiguousarray(np.asarray(inputs[v], dtype=np.float32))
         for k, v in _IN_NAMES.items()}
    nc = build()
    in_maps = []
    for c in range(8):
        b, h = c // 2, c % 2
        xb = x[b]
        pref = xb[0:4096].T
        own = xb[h * 4096:(h + 1) * 4096].T
        full = np.concatenate([pref, own], axis=1)
        tiled = full.reshape(8, P, 16, NT).transpose(2, 1, 0, 3).reshape(16, P, C * NT)
        m = dict(w)
        m["xfm"] = np.ascontiguousarray(tiled)
        m["mask"] = np.full((P, 1), float(h), np.float32)
        in_maps.append(m)
    res = run_bass_kernel_spmd(nc, in_maps, core_ids=list(range(8)))
    out = np.empty((4, 8192, 1024), np.float32)
    for c in range(8):
        b, h = c // 2, c % 2
        z = np.asarray(res.results[c]["x2"])
        rows = z.reshape(8, P, 4, D).transpose(0, 2, 1, 3).reshape(4096, D)
        out[b, h * 4096:(h + 1) * 4096, :] = rows
    return out


# version 15
# speedup vs baseline: 1.2088x; 1.1039x over previous
import numpy as np
import concourse.bass as bass
import concourse.mybir as mybir
from concourse.bass_utils import run_bass_kernel_spmd
from contextlib import ExitStack

F32 = mybir.dt.float32
BF16 = mybir.dt.bfloat16
AF = mybir.ActivationFunctionType
ALU = mybir.AluOpType
AX = mybir.AxisListType

P = 128
NT = 512
C = 8
D = 1024
H2 = 2048
H4 = 4096
NE = 8
EPS = 1e-5
QUEUES = ("pe", "act", "dve", "pool", "sync")


class Step:
    __slots__ = ("q", "emit", "deps", "inc", "cum", "needs", "tag")

    def __init__(self, q, emit, inc, tag):
        self.q = q
        self.emit = emit
        self.deps = []
        self.inc = inc
        self.cum = None
        self.needs = False
        self.tag = tag


class Buf:
    __slots__ = ("ap", "w", "rs", "name")

    def __init__(self, ap, name=""):
        self.ap = ap
        self.w = None
        self.rs = []
        self.name = name


class Sched:
    def __init__(self):
        self.plan = {q: [] for q in QUEUES}

    def add(self, q, emit, reads=(), writes=(), extra=(), inc=1, dma=False, tag=""):
        st = Step(q, emit, 16 if dma else inc, tag)
        deps = []
        for d in extra:
            if d is not None:
                deps.append(d)
        for b in reads:
            if b.w is not None:
                deps.append(b.w)
        for b in writes:
            if b.w is not None:
                deps.append(b.w)
            deps.extend(b.rs)
        seen = set()
        for d in deps:
            if id(d) not in seen:
                seen.add(id(d))
                st.deps.append(d)
                d.needs = True
        if dma:
            st.needs = True
        for b in reads:
            b.rs.append(st)
        for b in writes:
            b.w = st
            b.rs = []
        self.plan[q].append(st)
        return st

    def finalize(self):
        for q in QUEUES:
            cum = 0
            for st in self.plan[q]:
                if st.needs:
                    cum += st.inc
                    st.cum = cum
                else:
                    st.cum = None


def build(n_p=8, n_m=8, n_e=NE, dbg=False, pump=True, router_in=False):
    nc = bass.Bass("TRN2", target_bir_lowering=False, debug=False)
    S = Sched()

    xfm = nc.dram_tensor("xfm", [16, P, C * NT], F32, kind="ExternalInput").ap()
    maskd = nc.dram_tensor("mask", [P, 1], F32, kind="ExternalInput").ap()
    wD = {}
    for nm in ("fr_w", "dp_w", "bp_w", "cp_w"):
        wD[nm] = nc.dram_tensor(nm, [D, D], F32, kind="ExternalInput").ap()
    wD["m1_w"] = nc.dram_tensor("m1_w", [D, H2], F32, kind="ExternalInput").ap()
    wD["m2_w"] = nc.dram_tensor("m2_w", [H2, D], F32, kind="ExternalInput").ap()
    wD["gate_w"] = nc.dram_tensor("gate_w", [H2, D], F32, kind="ExternalInput").ap()
    wD["ffn1_w"] = nc.dram_tensor("ffn1_w", [D, H4], F32, kind="ExternalInput").ap()
    wD["ffn2_w"] = nc.dram_tensor("ffn2_w", [H4, D], F32, kind="ExternalInput").ap()
    wD["W1"] = nc.dram_tensor("W1", [NE, D, H4], F32, kind="ExternalInput").ap()
    wD["W2"] = nc.dram_tensor("W2", [NE, H4, D], F32, kind="ExternalInput").ap()
    wD["Wr"] = nc.dram_tensor("Wr", [D, NE], F32, kind="ExternalInput").ap()
    pD = {}
    for nm, sz in (("norm_g", D), ("norm_b", D), ("fr_b", D), ("dp_b", D),
                   ("bp_b", D), ("cp_b", D), ("skip", D), ("m1_b", H2),
                   ("m2_b", D), ("gate_b", D), ("ffn_g", D), ("ffn_bn", D),
                   ("ffn1_b", H4), ("ffn2_b", D)):
        pD[nm] = nc.dram_tensor(nm, [sz], F32, kind="ExternalInput").ap()
    pD["b1"] = nc.dram_tensor("b1", [NE, H4], F32, kind="ExternalInput").ap()
    pD["b2"] = nc.dram_tensor("b2", [NE, D], F32, kind="ExternalInput").ap()
    x2 = nc.dram_tensor("x2", [8, P, 4 * D], F32, kind="ExternalOutput").ap()
    dbg_out = {}
    if dbg:
        for nm in ("d_xn", "d_ssd", "d_mam", "d_y", "d_x1"):
            dbg_out[nm] = nc.dram_tensor(nm, [P, C * NT], F32, kind="ExternalOutput").ap()

    scr = {}
    for nm, mg in (("fr_w", 2), ("dp_w", 2), ("bp_w", 2), ("cp_w", 2),
                   ("m1_w", 4), ("ffn1_w", 8)):
        scr[nm] = nc.dram_tensor("s_" + nm, [mg, P, 8 * 512], BF16, kind="Internal").ap()
    for nm, nq in (("m2_w", 4), ("gate_w", 4), ("ffn2_w", 8)):
        scr[nm] = nc.dram_tensor("s_" + nm, [nq, P, 4 * D], BF16, kind="Internal").ap()
    scr["W1"] = nc.dram_tensor("s_W1", [NE, 8, P, 8 * 512], BF16, kind="Internal").ap()
    scr["W2"] = nc.dram_tensor("s_W2", [NE, 2, P, 32 * 512], BF16, kind="Internal").ap()
    sbuf_ctx = ExitStack()

    def sb(name, shape, dt):
        return sbuf_ctx.enter_context(nc.sbuf_tensor(name, shape, dt))

    fx0 = sb("fx0", [P, C * NT], F32)
    fxn = sb("fxn", [P, C * NT], F32)
    fscr = sb("fscr", [P, C * NT], F32)
    fa = sb("fa", [P, C * NT], F32)
    fst = sb("fst", [P, C * NT], F32)
    bxn = sb("bxn", [P, C * NT], BF16)
    blow = sb("blow", [P, C * NT], BF16)
    bhigh = sb("bhigh", [P, C * NT], BF16)
    bssd = sb("bssd", [P, C * NT], BF16)
    bmam = sb("bmam", [P, C * NT], BF16)
    hbuf = sb("hbuf", [P, 32 * NT], BF16)
    wsA = [sb(f"wsA{i}", [P, 8 * 512], BF16) for i in range(4)]
    pv = {}
    for nm in ("norm_g", "norm_b", "fr_b", "dp_b", "bp_b", "cp_b", "skip",
               "m2_b", "gate_b", "ffn_g", "ffn_bn", "ffn2_b"):
        pv[nm] = sb("p_" + nm, [P, C], F32)
    pv["m1_b"] = sb("p_m1_b", [P, 16], F32)
    pv["ffn1_b"] = sb("p_ffn1_b", [P, 32], F32)
    pv["b1"] = sb("p_b1", [P, NE * 32], F32)
    b2bf = sb("b2bf", [NE, D], BF16)
    wrbf = sb("wrbf", [P, C * NE], BF16)
    masks = sb("masks", [P, 1], F32)
    epst = sb("epst", [P, 1], F32)
    ident = sb("ident", [P, P], F32)
    ones_f = sb("ones_f", [P, 1], F32)
    ones_row = sb("ones_row", [1, P], BF16)
    nones_row = sb("nones_row", [1, P], BF16)
    RW = sb("RW", [P, 6 * 512], F32)
    RWB = sb("RWB", [P, 4 * 512], BF16)
    carP = sb("carP", [P, C], F32)
    carM = sb("carM", [P, C], F32)
    minit = sb("minit", [P, C], F32)
    comb = sb("comb", [P, 4 * NE], F32)
    combT = sb("combT", [NE, 4 * P], BF16)
    tkt = sb("tkt", [P, 8], F32)
    eqt = sb("eqt", [P, 16], F32)
    mskt = sb("mskt", [P, 8], F32)
    acc = fst

    psum_ctx = ExitStack()
    pbank = [psum_ctx.enter_context(nc.psum_tensor(f"pb{i}", [P, NT], F32)) for i in range(8)]

    B = lambda t, nm: Buf(t, nm)
    bufs = {}
    for nm, t in [("fx0", fx0), ("fxn", fxn), ("fscr", fscr), ("fa", fa),
                  ("fst", fst), ("bxn", bxn),
                  ("blow", blow), ("bhigh", bhigh),
                  ("bssd", bssd), ("bmam", bmam), ("hbuf", hbuf),
                  ("RW", RW), ("RWB", RWB),
                  ("carP", carP), ("carM", carM), ("minit", minit),
                  ("comb", comb), ("combT", combT), ("tkt", tkt), ("eqt", eqt),
                  ("mskt", mskt), ("b2bf", b2bf),
                  ("masks", masks), ("ident", ident)]:
        bufs[nm] = B(t, nm)
    bufs["acc"] = bufs["fst"]
    bufs["bmam0"] = Buf(bmam, "bmam0")
    bufs["bmam1"] = Buf(bmam, "bmam1")
    hQ = [Buf(hbuf, f"hQ{i}") for i in range(8)]
    bxnH = [Buf(bxn, "bxnH0"), Buf(bxn, "bxnH1")]
    pB = [B(pbank[i], f"pb{i}") for i in range(8)]
    wsB = [B(t, f"wsA{i}") for i, t in enumerate(wsA)]
    pvB = {nm: B(t, "p_" + nm) for nm, t in pv.items()}
    scrB = {nm: B(a, "s_" + nm) for nm, a in scr.items()}
    x2B = B(x2, "x2")

    ring = {"ws": 0, "wv2": 0}
    wv2B = [(bhigh, [bufs["bhigh"]]),
            (bmam, [bufs["bmam"], bufs["bmam0"], bufs["bmam1"]])]

    def ring_next(kind, bl):
        i = ring[kind] % len(bl)
        ring[kind] += 1
        return bl[i]

    def dma(q, out_ap, in_ap, reads=(), writes=(), extra=(), tag="", slow=False):
        def em(e):
            return e.dma_start(out=out_ap, in_=in_ap,
                               **({"allow_slow_non_contiguous": True} if slow else {}))
        return S.add(q, em, reads, writes, extra, dma=True, tag=tag)

    def act(fn, reads=(), writes=(), extra=(), tag=""):
        return S.add("act", fn, reads, writes, extra, tag=tag)

    def dve(fn, reads=(), writes=(), extra=(), tag=""):
        return S.add("dve", fn, reads, writes, extra, tag=tag)

    def pe(fn, reads=(), writes=(), extra=(), tag=""):
        return S.add("pe", fn, reads, writes, extra, tag=tag)

    def chunk(t, c, n=NT):
        return t[:, c * n:(c + 1) * n]

    for nm in pv:
        src = pD[nm]
        w = pv[nm].shape[1]
        if nm == "b1":
            ap = src.rearrange("e (mk p) -> p e mk", p=P)
            dst = pv[nm][:, :].rearrange("p (e mk) -> p e mk", e=NE)
        else:
            ap = src.rearrange("(c p) -> p c", p=P)
            dst = pv[nm][:, :]
        dma("sync", dst, ap, writes=[pvB[nm]], tag=f"ld_{nm}", slow=True)
    dma("pool", b2bf[:, :], pD["b2"], writes=[bufs["b2bf"]], tag="ld_b2")
    dma("sync", masks[:, :], maskd, writes=[bufs["masks"]], tag="ld_mask")
    dma("pool", wrbf[:, :].rearrange("p (k e) -> p k e", k=C),
        wD["Wr"].rearrange("(k p) e -> p k e", p=P),
        writes=[Buf(wrbf, "wrbf")], tag="ld_wr", slow=True)
    wrB = Buf(wrbf, "wrbf")
    wrB.w = S.plan["pool"][-1]
    for nm in ("fr_b", "dp_b", "gate_b"):
        def em(v, t=pv[nm]):
            return v.tensor_scalar(t[:, :], t[:, :], 0.5, None, ALU.mult)
        dve(em, reads=[pvB[nm]], writes=[pvB[nm]], tag=f"half_{nm}")
    S.add("pool", lambda g: g.memset(ident[:, :], 0.0), writes=[bufs["ident"]], tag="id0")
    S.add("pool", lambda g: g.affine_select(
        out=ident[:, :], in_=ident[:, :], compare_op=ALU.not_equal, fill=1.0,
        base=0, pattern=[[-1, P]], channel_multiplier=1),
        reads=[bufs["ident"]], writes=[bufs["ident"]], tag="id1")
    onesFB = Buf(ones_f, "onesf")
    dve(lambda v: v.memset(ones_f[:, :], 1.0), writes=[onesFB], tag="ones_f")
    onesRB = Buf(ones_row, "onesr")
    dve(lambda v: v.memset(ones_row[:, :], 1.0), writes=[onesRB], tag="ones_r")
    dve(lambda v: v.memset(nones_row[:, :], -1.0), writes=[onesRB], tag="nones_r")
    epsB = Buf(epst, "eps")
    dve(lambda v: v.memset(epst[:, :], EPS), writes=[epsB], tag="eps")
    carPB = bufs["carP"]
    dve(lambda v: v.memset(carP[:, :], 0.0), writes=[carPB], tag="carP0")
    dve(lambda v: v.memset(carM[:, :], 0.0), writes=[bufs["carM"]], tag="carM0")

    converted = set()

    def stream_mg_slab(nm, mg, nk=8):
        sbuf = ring_next("ws", wsB)
        if nm.startswith("W1_"):
            e = int(nm[3:])
            sc, scb, wsrc = scr["W1"][e, mg], scrB["W1"], wD["W1"][e]
        else:
            sc, scb, wsrc = scr[nm][mg], scrB[nm], wD[nm]
        key = ("mg", nm, mg)
        if key not in converted:
            converted.add(key)
            fsrc = wsrc[:, mg * 512:(mg + 1) * 512].rearrange("(k p) f -> p k f", p=P)
            dma("pool", sbuf.ap[:, 0:nk * 512].rearrange("p (k f) -> p k f", k=nk),
                fsrc, writes=[sbuf], tag=f"cv_{nm}_{mg}")
            dma("sync", sc[:, 0:nk * 512], sbuf.ap[:, 0:nk * 512],
                reads=[sbuf], writes=[scb], tag=f"wb_{nm}_{mg}")
        else:
            dma("sync", sbuf.ap[:, 0:nk * 512], sc[:, 0:nk * 512],
                reads=[scb], writes=[sbuf], tag=f"st_{nm}_{mg}")
        return sbuf

    def stream_kfat(nm, kq):
        sbuf = ring_next("ws", wsB)
        key = ("ks", nm, kq)
        if key not in converted:
            converted.add(key)
            fsrc = wD[nm][kq * 512:(kq + 1) * 512, :].rearrange("(k p) f -> p k f", p=P)
            dma("pool", sbuf.ap[:, :].rearrange("p (k f) -> p k f", k=4),
                fsrc, writes=[sbuf], tag=f"cv_{nm}_{kq}")
            dma("sync", scr[nm][kq], sbuf.ap[:, :],
                reads=[sbuf], writes=[scrB[nm]], tag=f"wb_{nm}_{kq}")
        else:
            dma("sync", sbuf.ap[:, :], scr[nm][kq],
                reads=[scrB[nm]], writes=[sbuf], tag=f"st_{nm}_{kq}")
        return sbuf

    def stream_w2half(e, fh, hq):
        tens, wbl = wv2B[ring["wv2"] % 2]
        ring["wv2"] += 1
        key = ("w2", e, fh, hq)
        if key not in converted:
            converted.add(key)
            fsrc = wD["W2"][e][hq * 1024:(hq + 1) * 1024, fh * 512:(fh + 1) * 512] \
                .rearrange("(k p) f -> p k f", p=P)
            dma("pool", tens[:, 0:4096].rearrange("p (k f) -> p k f", k=8),
                fsrc, writes=wbl, tag=f"cv_W2_{e}_{fh}_{hq}")
            dma("sync", scr["W2"][e, fh][:, hq * 4096:(hq + 1) * 4096], tens[:, 0:4096],
                reads=[wbl[0]], writes=[scrB["W2"]], tag=f"wb_W2_{e}_{fh}_{hq}")
        else:
            dma("sync", tens[:, 0:4096], scr["W2"][e, fh][:, hq * 4096:(hq + 1) * 4096],
                reads=[scrB["W2"]], writes=wbl, tag=f"st_W2_{e}_{fh}_{hq}")
        return wbl

    def mm_group(pbuf, pairs, reads, extra=(), tag=""):
        def em(t):
            last = None
            n = len(pairs)
            for i, (l, r) in enumerate(pairs):
                last = t.matmul(pbuf.ap[:, :], l, r,
                                start=(i == 0), stop=(i == n - 1))
            return last
        return pe(em, reads=reads, writes=[pbuf], extra=extra, tag=tag)

    rot_idx = [0]
    mid_idx = [0]

    def rot_bank():
        i = 6 + (rot_idx[0] % 2)
        rot_idx[0] += 1
        return pB[i]

    def mid_bank():
        i = 4 + (mid_idx[0] % 4)
        mid_idx[0] += 1
        return pB[i]

    quad_idx = [0]

    def next_quad():
        b0 = 4 if quad_idx[0] % 2 == 0 else 0
        quad_idx[0] += 1
        return b0

    def proj_1024(nm, rhs_bf, rhs_buf, epi, tag, bank=None):
        for mg in range(2):
            slab = stream_mg_slab(nm, mg)
            b0 = next_quad()

            def em(t, slab=slab, b0=b0):
                last = None
                for j in range(4):
                    for k in range(C):
                        last = t.matmul(pbank[b0 + j][:, :],
                                        slab.ap[:, k * 512 + j * P:k * 512 + j * P + P],
                                        chunk(rhs_bf, k),
                                        start=(k == 0), stop=(k == C - 1),
                                        skip_group_check=True)
                return last
            g = pe(em, reads=[slab, rhs_buf],
                   writes=[pB[b0], pB[b0 + 1], pB[b0 + 2], pB[b0 + 3]],
                   tag=f"{tag}_mg{mg}")
            for j in range(4):
                epi(mg * 4 + j, pB[b0 + j], g)

    def ln_stats(src_buf, src_t, sq_buf, sq_t, tag):
        g1 = pe(lambda t: [t.matmul(pbank[6][0:1, :], ones_f[:, 0:1], chunk(src_t, k),
                                    start=(k == 0), stop=(k == C - 1)) for k in range(C)][-1],
                reads=[src_buf, onesFB], writes=[pB[6]], tag=f"{tag}_sum")
        g2 = pe(lambda t: [t.matmul(pbank[7][0:1, :], ones_f[:, 0:1], chunk(sq_t, k),
                                    start=(k == 0), stop=(k == C - 1)) for k in range(C)][-1],
                reads=[sq_buf, onesFB], writes=[pB[7]], tag=f"{tag}_sq")
        return g1, g2

    RWb, RWBb = bufs["RW"], bufs["RWB"]

    def Rr(j):
        return RW[0:1, j * 512:(j + 1) * 512]

    def Br(j):
        return RWB[0:1, j * 512:(j + 1) * 512]

    def rmul(dj, aj, bj, tag):
        dve(lambda v: v.tensor_tensor(Rr(dj), Rr(aj), Rr(bj), ALU.mult),
            reads=[RWb], writes=[RWb], tag=tag)

    def rneg(j, tag):
        dve(lambda v: v.tensor_scalar(Rr(j), Rr(j), -1.0, None, ALU.mult),
            reads=[RWb], writes=[RWb], tag=tag)

    def rsqrt_row(j, tag, src_j=None):
        sj = j if src_j is None else src_j
        act(lambda s: s.activation(Rr(j), Rr(sj), AF.Sqrt, bias=epst[0:1, 0:1]),
            reads=[RWb, epsB], writes=[RWb], tag=f"{tag}_sd")
        dve(lambda v: v.reciprocal(Rr(j), Rr(j)),
            reads=[RWb], writes=[RWb], tag=f"{tag}_rc")

    def hi_lo(src_j, hi_j, lo_j, scr_j, tag):
        dve(lambda v: v.tensor_copy(Br(hi_j), Rr(src_j)),
            reads=[RWb], writes=[RWBb], tag=f"{tag}_hi")
        dve(lambda v: v.tensor_copy(Rr(scr_j), Br(hi_j)),
            reads=[RWBb], writes=[RWb], tag=f"{tag}_hf")
        dve(lambda v: v.tensor_tensor(Rr(scr_j), Rr(src_j), Rr(scr_j), ALU.subtract),
            reads=[RWb], writes=[RWb], tag=f"{tag}_lf")
        dve(lambda v: v.tensor_copy(Br(lo_j), Rr(scr_j)),
            reads=[RWb], writes=[RWBb], tag=f"{tag}_lo")

    def bcast(hi_j, lo_j, pbuf, tag, neg=False):
        orow = nones_row if neg else ones_row
        def em(t):
            t.matmul(pbuf.ap[:, :], orow[0:1, :], Br(hi_j),
                     start=True, stop=False, skip_group_check=True)
            return t.matmul(pbuf.ap[:, :], orow[0:1, :], Br(lo_j),
                            start=False, stop=True, skip_group_check=True)
        return pe(em, reads=[RWBb, onesRB], writes=[pbuf], tag=tag)

    def ln_rows_mv(g1, g2, tag):
        dve(lambda v: v.tensor_scalar(Rr(0), pbank[6][0:1, :], 1.0 / D, None, ALU.mult),
            reads=[pB[6]], writes=[RWb], extra=[g1], tag=f"{tag}_m")
        dve(lambda v: v.tensor_scalar(Rr(1), pbank[7][0:1, :], 1.0 / D, None, ALU.mult),
            reads=[pB[7]], writes=[RWb], extra=[g2], tag=f"{tag}_ss")
        rmul(4, 0, 0, f"{tag}_m2")
        dve(lambda v: v.tensor_tensor(Rr(1), Rr(1), Rr(4), ALU.subtract),
            reads=[RWb], writes=[RWb], tag=f"{tag}_v")

    def ln12(x0b, tag):
        act(lambda s: s.activation(fxn[:, :], x0b.ap[:, :], AF.Square),
            reads=[x0b], writes=[bufs["fxn"]], tag=f"{tag}_sq")
        g1, g2 = ln_stats(x0b, fx0, bufs["fxn"], fxn, tag)
        ln_rows_mv(g1, g2, tag)
        rsqrt_row(2, f"{tag}_r1", src_j=1)
        rmul(4, 2, 2, f"{tag}_rq")
        rmul(4, 1, 4, f"{tag}_v2")
        rsqrt_row(4, f"{tag}_r2")
        rmul(3, 0, 2, f"{tag}_s2a")
        rmul(5, 2, 4, f"{tag}_rr")
        rmul(0, 0, 5, f"{tag}_mm")
        hi_lo(5, 0, 1, 4, f"{tag}_hrr")
        hi_lo(0, 2, 3, 4, f"{tag}_hmm")

    def ln_apply_xn(x0b, g_nm, b_nm, tag):
        p_rr = pB[4 + (rot_idx[0] % 2)]; rot_idx[0] += 1
        p_mm = pB[4 + (rot_idx[0] % 2)]; rot_idx[0] += 1
        bc1 = bcast(0, 1, p_rr, f"{tag}_bc1")
        bc2 = bcast(2, 3, p_mm, f"{tag}_bc2", neg=True)
        for c in range(C):
            dve(lambda v, c=c: v.tensor_tensor(chunk(fxn, c), chunk(x0b.ap, c),
                                               p_rr.ap[:, :], ALU.mult),
                reads=[x0b, p_rr], writes=[bufs["fxn"]], extra=[bc1], tag=f"{tag}_u{c}")
            dve(lambda v, c=c: v.tensor_tensor(chunk(fxn, c), chunk(fxn, c),
                                               p_mm.ap[:, :], ALU.add),
                reads=[bufs["fxn"], p_mm], writes=[bufs["fxn"]], extra=[bc2], tag=f"{tag}_a{c}")
            dve(lambda v, c=c: v.tensor_scalar(chunk(fxn, c), chunk(fxn, c),
                                               pv[g_nm][:, c:c + 1], pv[b_nm][:, c:c + 1],
                                               ALU.mult, ALU.add),
                reads=[bufs["fxn"], pvB[g_nm], pvB[b_nm]], writes=[bufs["fxn"]], tag=f"{tag}_g{c}")
        dve(lambda v: v.tensor_copy(bxn[:, :], fxn[:, :]),
            reads=[bufs["fxn"]], writes=[bufs["bxn"], bxnH[0], bxnH[1]], tag=f"{tag}_bf")

    def front(tile_idx, tag):
        x0b = bufs["fx0"]
        dma("sync", fx0[:, :], xfm[tile_idx], writes=[x0b], tag=f"{tag}_x0")
        ln12(x0b, f"{tag}_ln")
        ln_apply_xn(x0b, "norm_g", "norm_b", f"{tag}_xn")
        return x0b

    def fr_low_high(with_high, tag):
        xnbfb = bufs["bxn"]
        if with_high:
            hi_lo(2, 0, 1, 4, f"{tag}_hr1")
            hi_lo(3, 2, 3, 4, f"{tag}_hs2")

        def epi(m, pb_, g):
            act(lambda s, m=m, pb_=pb_: s.activation(
                chunk(fscr, m), pb_.ap[:, :], AF.Tanh,
                bias=pv["fr_b"][:, m:m + 1], scale=0.5),
                reads=[pb_, pvB["fr_b"]], writes=[bufs["fscr"]], extra=[g], tag=f"{tag}_e{m}")
        proj_1024("fr_w", bxn, xnbfb, epi, tag)
        dve(lambda v: v.tensor_scalar(fscr[:, :], fscr[:, :], 0.5, 0.5, ALU.mult, ALU.add),
            reads=[bufs["fscr"]], writes=[bufs["fscr"]], tag=f"{tag}_lg")
        dve(lambda v: v.tensor_tensor(blow[:, :], fxn[:, :], fscr[:, :], ALU.mult),
            reads=[bufs["fxn"], bufs["fscr"]], writes=[bufs["blow"]], tag=f"{tag}_low")
        if with_high:
            dve(lambda v: v.tensor_scalar(fscr[:, :], fscr[:, :], -1.0, 1.0, ALU.mult, ALU.add),
                reads=[bufs["fscr"]], writes=[bufs["fscr"]], tag=f"{tag}_ilg")
            dve(lambda v: v.tensor_tensor(bhigh[:, :], fxn[:, :], fscr[:, :], ALU.mult),
                reads=[bufs["fxn"], bufs["fscr"]], writes=[bufs["bhigh"]], tag=f"{tag}_high")

    def dp_bp_scan(init_buf, init_zero, car_buf, tag):
        lowb = bufs["blow"]

        def epi_a(m, pb_, g):
            act(lambda s, m=m, pb_=pb_: s.activation(
                chunk(fa, m), pb_.ap[:, :], AF.Tanh,
                bias=pv["dp_b"][:, m:m + 1], scale=0.5),
                reads=[pb_, pvB["dp_b"]], writes=[bufs["fa"]], extra=[g], tag=f"{tag}_ea{m}")
        proj_1024("dp_w", blow, lowb, epi_a, f"{tag}_dp")
        dve(lambda v: v.tensor_scalar(fa[:, :], fa[:, :], 0.5, 0.5, ALU.mult, ALU.add),
            reads=[bufs["fa"]], writes=[bufs["fa"]], tag=f"{tag}_a")

        def epi_b(m, pb_, g):
            act(lambda s, m=m, pb_=pb_: s.activation(
                chunk(fscr, m), pb_.ap[:, :], AF.Tanh,
                bias=pv["bp_b"][:, m:m + 1]),
                reads=[pb_, pvB["bp_b"]], writes=[bufs["fscr"]], extra=[g], tag=f"{tag}_eb{m}")
        proj_1024("bp_w", blow, lowb, epi_b, f"{tag}_bp")
        dve(lambda v: v.tensor_tensor(fscr[:, :], fscr[:, :], fxn[:, :], ALU.mult),
            reads=[bufs["fscr"], bufs["fxn"]], writes=[bufs["fscr"]], tag=f"{tag}_b")
        for c in range(C):
            if init_zero:
                ini = 0.0
                rd = [bufs["fa"], bufs["fscr"]]
            else:
                ini = init_buf.ap[:, c:c + 1]
                rd = [bufs["fa"], bufs["fscr"], init_buf]
            dve(lambda v, c=c, ini=ini: v.tensor_tensor_scan(
                chunk(fst, c), chunk(fa, c), chunk(fscr, c), ini, ALU.mult, ALU.add),
                reads=rd, writes=[bufs["fst"]], tag=f"{tag}_sc{c}")
        dve(lambda v: v.tensor_copy(
            car_buf.ap[:, :], fst[:, :].rearrange("p (c t) -> p c t", c=C)[:, :, NT - 1]),
            reads=[bufs["fst"]], writes=[car_buf], tag=f"{tag}_car")

    for k in range(n_p):
        tag = f"P{k}"
        front(k, tag)
        fr_low_high(False, f"{tag}_fr")
        dp_bp_scan(carPB, k == 0, carPB, tag)

    dve(lambda v: v.tensor_scalar(minit[:, :], carP[:, :], masks[:, 0:1], None, ALU.mult),
        reads=[carPB, bufs["masks"]], writes=[bufs["minit"]], tag="minit")

    if n_m > 0:
        front(8, "M0")
    for k in range(n_m):
        tag = f"M{k}"
        x0b = bufs["fx0"]
        fr_low_high(True, f"{tag}_fr")
        init_b = bufs["minit"] if k == 0 else bufs["carM"]
        dp_bp_scan(init_b, False, bufs["carM"], tag)
        if dbg and k == 0:
            dma("act", dbg_out["d_xn"], fxn[:, :], reads=[bufs["fxn"]], tag="dbg_xn")

        def epi_c(m, pb_, g):
            act(lambda s, m=m, pb_=pb_: s.activation(
                chunk(fscr, m), pb_.ap[:, :], AF.Tanh, bias=pv["cp_b"][:, m:m + 1]),
                reads=[pb_, pvB["cp_b"]], writes=[bufs["fscr"]], extra=[g], tag=f"{tag}_ec{m}")
        proj_1024("cp_w", bhigh, bufs["bhigh"], epi_c, f"{tag}_cp", bank=mid_bank)
        dve(lambda v: v.tensor_tensor(fst[:, :], fscr[:, :], fst[:, :], ALU.mult),
            reads=[bufs["fscr"], bufs["fst"]], writes=[bufs["fst"]], tag=f"{tag}_s1")
        for c in range(C):
            dve(lambda v, c=c: v.scalar_tensor_tensor(
                chunk(fst, c), chunk(fxn, c), pv["skip"][:, c:c + 1], chunk(fst, c),
                ALU.mult, ALU.add),
                reads=[bufs["fxn"], pvB["skip"], bufs["fst"]], writes=[bufs["fst"]],
                tag=f"{tag}_ssd{c}")
        dve(lambda v: v.tensor_copy(bssd[:, :], fst[:, :]),
            reads=[bufs["fst"]], writes=[bufs["bssd"]], tag=f"{tag}_ssdbf")
        if dbg and k == 0:
            dma("act", dbg_out["d_ssd"], fst[:, :], reads=[bufs["fst"]], tag="dbg_ssd")

        for mg in range(4):
            slab = stream_mg_slab("m1_w", mg)
            b0 = next_quad()

            def em(t, slab=slab, b0=b0):
                last = None
                for j in range(4):
                    for kk in range(C):
                        last = t.matmul(pbank[b0 + j][:, :],
                                        slab.ap[:, kk * 512 + j * P:kk * 512 + j * P + P],
                                        chunk(blow, kk),
                                        start=(kk == 0), stop=(kk == C - 1),
                                        skip_group_check=True)
                return last
            g = pe(em, reads=[slab, bufs["blow"]],
                   writes=[pB[b0], pB[b0 + 1], pB[b0 + 2], pB[b0 + 3]],
                   tag=f"{tag}_m1_{mg}")
            for j in range(4):
                m = mg * 4 + j
                act(lambda s, m=m, j=j, b0=b0: s.activation(
                    hbuf[:, m * 512:(m + 1) * 512], pbank[b0 + j][:, :], AF.Gelu,
                    bias=pv["m1_b"][:, m:m + 1]),
                    reads=[pB[b0 + j], pvB["m1_b"]], writes=[hQ[m // 4]], extra=[g],
                    tag=f"{tag}_m1e{m}")

        def half_m_kslab(nm, nk, rhs_fn, rhs_bufs, epi, tag2, rhs_bufs_fn=None):
            for mg in range(2):
                g = None
                for kq in range(nk // 4):
                    slab = stream_kfat(nm, kq)
                    rbufs = rhs_bufs_fn(kq) if rhs_bufs_fn else rhs_bufs

                    def em(t, kq=kq, mg=mg, slab=slab):
                        last = None
                        for ki in range(4):
                            kk = kq * 4 + ki
                            for j in range(4):
                                m = mg * 4 + j
                                last = t.matmul(pbank[j][:, :],
                                                slab.ap[:, ki * D + m * P:ki * D + (m + 1) * P],
                                                rhs_fn(kk), start=(kk == 0), stop=(kk == nk - 1),
                                                skip_group_check=True)
                        return last
                    g = pe(em, reads=[slab] + rbufs,
                           writes=[pB[0], pB[1], pB[2], pB[3]], tag=f"{tag2}_{mg}_{kq}")
                for j in range(4):
                    epi(mg * 4 + j, pB[j], g)

        def m2_epi(m, pb_, g):
            dve(lambda v, m=m, pb_=pb_: v.tensor_scalar(
                chunk(fa, m), pb_.ap[:, :], pv["m2_b"][:, m:m + 1], None, ALU.add),
                reads=[pb_, pvB["m2_b"]], writes=[bufs["fa"]], extra=[g], tag=f"{tag}_m2e{m}")
        half_m_kslab("m2_w", 16, lambda kk: hbuf[:, kk * 512:(kk + 1) * 512],
                     None, m2_epi, f"{tag}_m2",
                     rhs_bufs_fn=lambda kq: [hQ[kq]])
        dve(lambda v: v.tensor_copy(bmam[:, 0:4 * NT], fa[:, 0:4 * NT]),
            reads=[bufs["fa"]], writes=[bufs["bmam0"]], tag=f"{tag}_mambf0")
        dve(lambda v: v.tensor_copy(bmam[:, 4 * NT:8 * NT], fa[:, 4 * NT:8 * NT]),
            reads=[bufs["fa"]], writes=[bufs["bmam1"]], tag=f"{tag}_mambf1")
        if dbg and k == 0:
            dma("act", dbg_out["d_mam"], fa[:, :], reads=[bufs["fa"]], tag="dbg_mam")

        def gate_rhs(kk):
            return chunk(bssd, kk) if kk < 8 else chunk(bmam, kk - 8)

        def gate_epi(m, pb_, g):
            act(lambda s, m=m, pb_=pb_: s.activation(
                chunk(fscr, m), pb_.ap[:, :], AF.Tanh,
                bias=pv["gate_b"][:, m:m + 1], scale=0.5),
                reads=[pb_, pvB["gate_b"]], writes=[bufs["fscr"]], extra=[g],
                tag=f"{tag}_ge{m}")
        half_m_kslab("gate_w", 16, gate_rhs, None, gate_epi, f"{tag}_gate",
                     rhs_bufs_fn=lambda kq: [bufs["bssd"]] if kq < 2 else
                     [bufs["bmam0"] if kq == 2 else bufs["bmam1"]])

        dve(lambda v: v.tensor_tensor(fst[:, :], fst[:, :], fa[:, :], ALU.subtract),
            reads=[bufs["fst"], bufs["fa"]], writes=[bufs["fst"]], tag=f"{tag}_d")
        dve(lambda v: v.tensor_tensor(fscr[:, :], fscr[:, :], fst[:, :], ALU.mult),
            reads=[bufs["fscr"], bufs["fst"]], writes=[bufs["fscr"]], tag=f"{tag}_e")
        dve(lambda v: v.tensor_tensor(fst[:, :], fst[:, :], fscr[:, :], ALU.add),
            reads=[bufs["fst"], bufs["fscr"]], writes=[bufs["fst"]], tag=f"{tag}_f")
        dve(lambda v: v.scalar_tensor_tensor(fst[:, :], fst[:, :], 0.5, fa[:, :],
                                             ALU.mult, ALU.add),
            reads=[bufs["fst"], bufs["fa"]], writes=[bufs["fst"]], tag=f"{tag}_u")
        bc1 = bcast(0, 1, pB[4], f"{tag}_br1")
        bc2 = bcast(2, 3, pB[5], f"{tag}_bs2", neg=True)
        for c in range(C):
            dve(lambda v, c=c: v.tensor_tensor(chunk(fxn, c), chunk(fx0, c),
                                               pbank[4][:, :], ALU.mult),
                reads=[x0b, pB[4]], writes=[bufs["fxn"]], extra=[bc1], tag=f"{tag}_y1{c}")
            dve(lambda v, c=c: v.tensor_tensor(chunk(fxn, c), chunk(fxn, c),
                                               pbank[5][:, :], ALU.add),
                reads=[bufs["fxn"], pB[5]], writes=[bufs["fxn"]], extra=[bc2], tag=f"{tag}_y2{c}")
        dve(lambda v: v.tensor_tensor(fxn[:, :], fxn[:, :], fst[:, :], ALU.add),
            reads=[bufs["fxn"], bufs["fst"]], writes=[bufs["fxn"]], tag=f"{tag}_y3")
        if dbg and k == 0:
            dma("act", dbg_out["d_y"], fxn[:, :], reads=[bufs["fxn"]], tag="dbg_y")

        act(lambda s: s.activation(fst[:, :], fxn[:, :], AF.Square),
            reads=[bufs["fxn"]], writes=[bufs["fst"]], tag=f"{tag}_ysq")
        g1, g2 = ln_stats(bufs["fxn"], fxn, bufs["fst"], fst, f"{tag}_l3")
        ln_rows_mv(g1, g2, f"{tag}_l3")
        rsqrt_row(5, f"{tag}_l3r", src_j=1)
        rmul(0, 0, 5, f"{tag}_l3sa")
        hi_lo(5, 0, 1, 1, f"{tag}_l3h1")
        hi_lo(0, 2, 3, 1, f"{tag}_l3h2")
        bc1 = bcast(0, 1, pB[4], f"{tag}_l3b1")
        bc2 = bcast(2, 3, pB[5], f"{tag}_l3b2", neg=True)
        for c in range(C):
            dve(lambda v, c=c: v.tensor_tensor(chunk(fst, c), chunk(fxn, c),
                                               pbank[4][:, :], ALU.mult),
                reads=[bufs["fxn"], pB[4]], writes=[bufs["fst"]], extra=[bc1], tag=f"{tag}_l3u{c}")
            dve(lambda v, c=c: v.tensor_tensor(chunk(fst, c), chunk(fst, c),
                                               pbank[5][:, :], ALU.add),
                reads=[bufs["fst"], pB[5]], writes=[bufs["fst"]], extra=[bc2], tag=f"{tag}_l3a{c}")
            dve(lambda v, c=c: v.tensor_scalar(chunk(bxn, c), chunk(fst, c),
                                               pv["ffn_g"][:, c:c + 1], pv["ffn_bn"][:, c:c + 1],
                                               ALU.mult, ALU.add),
                reads=[bufs["fst"], pvB["ffn_g"], pvB["ffn_bn"]],
                writes=[bufs["bxn"], bxnH[c // 4]],
                tag=f"{tag}_l3g{c}")

        for mg in range(8):
            slab = stream_mg_slab("ffn1_w", mg)
            b0 = next_quad()
            if mg == 0:
                def em_a(t, slab=slab, b0=b0):
                    last = None
                    for j in range(4):
                        for kk in range(4):
                            last = t.matmul(pbank[b0 + j][:, :],
                                            slab.ap[:, kk * 512 + j * P:kk * 512 + j * P + P],
                                            chunk(bxn, kk),
                                            start=(kk == 0), stop=False,
                                            skip_group_check=True)
                    return last
                pe(em_a, reads=[slab, bxnH[0]],
                   writes=[pB[b0], pB[b0 + 1], pB[b0 + 2], pB[b0 + 3]],
                   tag=f"{tag}_f1_{mg}a")

                def em_b(t, slab=slab, b0=b0):
                    last = None
                    for j in range(4):
                        for kk in range(4, C):
                            last = t.matmul(pbank[b0 + j][:, :],
                                            slab.ap[:, kk * 512 + j * P:kk * 512 + j * P + P],
                                            chunk(bxn, kk),
                                            start=False, stop=(kk == C - 1),
                                            skip_group_check=True)
                    return last
                g = pe(em_b, reads=[slab, bxnH[1]],
                       writes=[pB[b0], pB[b0 + 1], pB[b0 + 2], pB[b0 + 3]],
                       tag=f"{tag}_f1_{mg}b")
            else:
                def em(t, slab=slab, b0=b0):
                    last = None
                    for j in range(4):
                        for kk in range(C):
                            last = t.matmul(pbank[b0 + j][:, :],
                                            slab.ap[:, kk * 512 + j * P:kk * 512 + j * P + P],
                                            chunk(bxn, kk),
                                            start=(kk == 0), stop=(kk == C - 1),
                                            skip_group_check=True)
                    return last
                g = pe(em, reads=[slab, bxnH[0], bxnH[1]],
                       writes=[pB[b0], pB[b0 + 1], pB[b0 + 2], pB[b0 + 3]],
                       tag=f"{tag}_f1_{mg}")
            for j in range(4):
                m = mg * 4 + j
                act(lambda s, m=m, j=j, b0=b0: s.activation(
                    hbuf[:, m * 512:(m + 1) * 512], pbank[b0 + j][:, :], AF.Gelu,
                    bias=pv["ffn1_b"][:, m:m + 1]),
                    reads=[pB[b0 + j], pvB["ffn1_b"]], writes=[hQ[m // 4]], extra=[g],
                    tag=f"{tag}_f1e{m}")

        def f2_epi(m, pb_, g):
            dve(lambda v, m=m, pb_=pb_: v.scalar_tensor_tensor(
                chunk(fa, m), pb_.ap[:, :], pv["ffn2_b"][:, m:m + 1], chunk(fxn, m),
                ALU.add, ALU.add),
                reads=[pb_, pvB["ffn2_b"], bufs["fxn"]], writes=[bufs["fa"]], extra=[g],
                tag=f"{tag}_f2e{m}")
        half_m_kslab("ffn2_w", 32, lambda kk: hbuf[:, kk * 512:(kk + 1) * 512],
                     None, f2_epi, f"{tag}_f2",
                     rhs_bufs_fn=lambda kq: [hQ[kq]])
        dve(lambda v: v.tensor_tensor(fa[:, :], fa[:, :], fx0[:, :], ALU.add),
            reads=[bufs["fa"], x0b], writes=[bufs["fa"]], tag=f"{tag}_x1")
        if dbg and k == 0:
            dma("act", dbg_out["d_x1"], fa[:, :], reads=[bufs["fa"]], tag="dbg_x1")

        act(lambda s: s.activation(fst[:, :], fa[:, :], AF.Square),
            reads=[bufs["fa"]], writes=[bufs["fst"]], tag=f"{tag}_x1sq")
        g1, g2 = ln_stats(bufs["fa"], fa, bufs["fst"], fst, f"{tag}_l4")
        ln_rows_mv(g1, g2, f"{tag}_l4")
        rsqrt_row(5, f"{tag}_l4r", src_j=1)
        rmul(0, 0, 5, f"{tag}_l4sa")
        hi_lo(5, 0, 1, 1, f"{tag}_l4h1")
        hi_lo(0, 2, 3, 1, f"{tag}_l4h2")
        bc1 = bcast(0, 1, pB[4], f"{tag}_l4b1")
        bc2 = bcast(2, 3, pB[5], f"{tag}_l4b2", neg=True)
        for c in range(C):
            dve(lambda v, c=c: v.tensor_tensor(chunk(fst, c), chunk(fa, c),
                                               pbank[4][:, :], ALU.mult),
                reads=[bufs["fa"], pB[4]], writes=[bufs["fst"]], extra=[bc1], tag=f"{tag}_l4u{c}")
            dve(lambda v, c=c: v.tensor_tensor(chunk(blow, c), chunk(fst, c),
                                               pbank[5][:, :], ALU.add),
                reads=[bufs["fst"], pB[5]], writes=[bufs["blow"]], extra=[bc2], tag=f"{tag}_l4a{c}")

        def acc_router():
            for tc in range(4):
                def em_t(t, tc=tc):
                    last = None
                    for cc in range(C):
                        pbk = pbank[4 + cc // 4]
                        last = t.transpose(pbk[:, (cc % 4) * P:(cc % 4 + 1) * P],
                                           fa[:, cc * 512 + tc * P:cc * 512 + tc * P + P],
                                           ident[:, :])
                    return last
                g = pe(em_t, reads=[bufs["fa"], bufs["ident"]], writes=[pB[4], pB[5]],
                       tag=f"{tag}_tp{tc}")
                act(lambda s, tc=tc: s.activation(acc[:, tc * D:tc * D + 512],
                                                  pbank[4][:, :], AF.Copy),
                    reads=[pB[4]], writes=[bufs["acc"]], extra=[g], tag=f"{tag}_tpa{tc}")
                act(lambda s, tc=tc: s.activation(acc[:, tc * D + 512:(tc + 1) * D],
                                                  pbank[5][:, :], AF.Copy),
                    reads=[pB[5]], writes=[bufs["acc"]], extra=[g], tag=f"{tag}_tpb{tc}")

            def em_lg(t):
                last = None
                for tc in range(4):
                    for kk in range(C):
                        last = t.matmul(pbank[6][:, tc * 8:(tc + 1) * 8],
                                        blow[:, kk * 512 + tc * P:kk * 512 + tc * P + P],
                                        wrbf[:, kk * 8:(kk + 1) * 8],
                                        start=(kk == 0), stop=(kk == C - 1),
                                        skip_group_check=True)
                return last
            glg = pe(em_lg, reads=[bufs["blow"], wrB], writes=[pB[6]], tag=f"{tag}_lg")
            for tc in range(4):
                ls = pbank[6][:, tc * 8:(tc + 1) * 8]
                dve(lambda v, ls=ls: v.tensor_reduce(tkt[:, 0:1], ls, AX.X, ALU.max),
                    reads=[pB[6]], writes=[bufs["tkt"]], extra=[glg], tag=f"{tag}_tk0{tc}")
                dve(lambda v, ls=ls: v.tensor_scalar(eqt[:, 0:8], ls, tkt[:, 0:1], None, ALU.is_equal),
                    reads=[pB[6], bufs["tkt"]], writes=[bufs["eqt"]], tag=f"{tag}_tk1{tc}")
                dve(lambda v, ls=ls: v.scalar_tensor_tensor(mskt[:, :], eqt[:, 0:8], -1e9, ls,
                                                            ALU.mult, ALU.add),
                    reads=[bufs["eqt"], pB[6]], writes=[bufs["mskt"]], tag=f"{tag}_tk2{tc}")
                dve(lambda v: v.tensor_reduce(tkt[:, 1:2], mskt[:, :], AX.X, ALU.max),
                    reads=[bufs["mskt"]], writes=[bufs["tkt"]], tag=f"{tag}_tk3{tc}")
                dve(lambda v: v.tensor_tensor(tkt[:, 4:5], tkt[:, 0:1], tkt[:, 1:2], ALU.subtract),
                    reads=[bufs["tkt"]], writes=[bufs["tkt"]], tag=f"{tag}_tk4{tc}")
                act(lambda s: s.activation(tkt[:, 5:6], tkt[:, 4:5], AF.Tanh, scale=0.5),
                    reads=[bufs["tkt"]], writes=[bufs["tkt"]], tag=f"{tag}_tk5{tc}")
                dve(lambda v: v.tensor_scalar(tkt[:, 2:3], tkt[:, 5:6], 0.5, 0.5, ALU.mult, ALU.add),
                    reads=[bufs["tkt"]], writes=[bufs["tkt"]], tag=f"{tag}_tk6{tc}")
                dve(lambda v: v.tensor_scalar(tkt[:, 3:4], tkt[:, 2:3], -1.0, 1.0, ALU.mult, ALU.add),
                    reads=[bufs["tkt"]], writes=[bufs["tkt"]], tag=f"{tag}_tk7{tc}")
                dve(lambda v: v.tensor_scalar(eqt[:, 8:16], mskt[:, :], tkt[:, 1:2], None, ALU.is_equal),
                    reads=[bufs["mskt"], bufs["tkt"]], writes=[bufs["eqt"]], tag=f"{tag}_tk8{tc}")
                dve(lambda v, tc=tc: v.tensor_scalar(comb[:, tc * 8:(tc + 1) * 8], eqt[:, 0:8],
                                                     tkt[:, 2:3], None, ALU.mult),
                    reads=[bufs["eqt"], bufs["tkt"]], writes=[bufs["comb"]], tag=f"{tag}_tk9{tc}")
                dve(lambda v, tc=tc: v.scalar_tensor_tensor(
                    comb[:, tc * 8:(tc + 1) * 8], eqt[:, 8:16], tkt[:, 3:4],
                    comb[:, tc * 8:(tc + 1) * 8], ALU.mult, ALU.add),
                    reads=[bufs["eqt"], bufs["tkt"], bufs["comb"]], writes=[bufs["comb"]],
                    tag=f"{tag}_tkA{tc}")

            def em_ct(t):
                last = None
                for tc in range(4):
                    last = t.transpose(pbank[7][0:NE, tc * P:(tc + 1) * P],
                                       comb[:, tc * 8:(tc + 1) * 8], ident[:, :])
                return last
            gct = pe(em_ct, reads=[bufs["comb"], bufs["ident"]], writes=[pB[7]], tag=f"{tag}_ct")
            act(lambda s: s.activation(combT[:, :], pbank[7][0:NE, :], AF.Copy),
                reads=[pB[7]], writes=[bufs["combT"]], extra=[gct], tag=f"{tag}_ctc")
            for i2, (tc, fh) in enumerate([(a, b) for a in range(4) for b in range(2)]):
                pbk = 6 + (i2 % 2)
                def em_b2(t, tc=tc, fh=fh, pbk=pbk):
                    return t.matmul(pbank[pbk][:, :], combT[0:NE, tc * P:(tc + 1) * P],
                                    b2bf[0:NE, fh * 512:(fh + 1) * 512],
                                    start=True, stop=True)
                g = pe(em_b2, reads=[bufs["combT"], bufs["b2bf"]], writes=[pB[pbk]],
                       tag=f"{tag}_b2_{tc}_{fh}")
                dve(lambda v, tc=tc, fh=fh, pbk=pbk: v.tensor_tensor(
                    acc[:, tc * D + fh * 512:tc * D + (fh + 1) * 512],
                    acc[:, tc * D + fh * 512:tc * D + (fh + 1) * 512],
                    pbank[pbk][:, :], ALU.add),
                    reads=[bufs["acc"], pB[pbk]], writes=[bufs["acc"]], extra=[g],
                    tag=f"{tag}_b2a_{tc}_{fh}")


        if not router_in:
            acc_router()

        def emit_w2_quad(e, fh, hg2, half):
            hap = half[0].ap
            def em_w2(t, hg2=hg2, hap=hap):
                last = None
                for j in range(4):
                    mk = hg2 * 4 + j
                    for tc in range(4):
                        last = t.matmul(
                            pbank[tc][:, :],
                            hbuf[:, mk * 512 + tc * P:mk * 512 + tc * P + P],
                            hap[:, (mk % 8) * 512:(mk % 8 + 1) * 512],
                            start=(mk == 0), stop=(mk == 31),
                            skip_group_check=True)
                return last
            pe(em_w2, reads=[hQ[hg2]] + list(half),
               writes=[pB[0], pB[1], pB[2], pB[3]], tag=f"{tag}_w2_{e}_{fh}_{hg2}")

        for e in range(n_e):
            if pump and e == 2 and k + 1 < n_m:
                front(8 + k + 1, f"M{k + 1}")
            for fh in range(2):
                halves = {}
                for hg in range(8):
                    if hg % 2 == 0:
                        halves[hg // 2] = stream_w2half(e, fh, hg // 2)
                    if fh == 0:
                        slab = stream_mg_slab(f"W1_{e}", hg)
                        for pr in range(2):
                            pb0 = 4 if (hg * 2 + pr) % 2 == 0 else 6
                            def em_w1(t, slab=slab, hg=hg, pr=pr, pb0=pb0):
                                last = None
                                for jj in range(2):
                                    j = pr * 2 + jj
                                    for kk in range(C):
                                        last = t.matmul(
                                            pbank[pb0 + jj][:, :],
                                            slab.ap[:, kk * 512 + j * P:kk * 512 + j * P + P],
                                            chunk(blow, kk),
                                            start=(kk == 0), stop=(kk == C - 1),
                                            skip_group_check=True)
                                return last
                            g = pe(em_w1, reads=[slab, bufs["blow"]],
                                   writes=[pB[pb0], pB[pb0 + 1]], tag=f"{tag}_w1_{e}_{hg}_{pr}")
                            for jj in range(2):
                                mk = hg * 4 + pr * 2 + jj
                                act(lambda s, mk=mk, jj=jj, e=e, pb0=pb0: s.activation(
                                    hbuf[:, mk * 512:(mk + 1) * 512], pbank[pb0 + jj][:, :],
                                    AF.Gelu,
                                    bias=pv["b1"][:, e * 32 + mk:e * 32 + mk + 1]),
                                    reads=[pB[pb0 + jj], pvB["b1"]], writes=[hQ[mk // 4]],
                                    extra=[g], tag=f"{tag}_w1e_{e}_{mk}")
                        if hg >= 1:
                            emit_w2_quad(e, fh, hg - 1, halves[(hg - 1) // 2])
                    else:
                        emit_w2_quad(e, fh, hg, halves[hg // 2])
                if fh == 0:
                    emit_w2_quad(e, fh, 7, halves[3])
                if router_in and e == 0 and fh == 0:
                    acc_router()
                for tc in range(4):
                    dve(lambda v, tc=tc, fh=fh, e=e: v.scalar_tensor_tensor(
                        acc[:, tc * D + fh * 512:tc * D + (fh + 1) * 512],
                        pbank[tc][:, :], comb[:, tc * 8 + e:tc * 8 + e + 1],
                        acc[:, tc * D + fh * 512:tc * D + (fh + 1) * 512],
                        ALU.mult, ALU.add),
                        reads=[pB[tc], bufs["comb"], bufs["acc"]], writes=[bufs["acc"]],
                        tag=f"{tag}_ye_{e}_{fh}_{tc}")

        dma("act", x2[k], acc[:, :], reads=[bufs["acc"]], writes=[x2B], tag=f"{tag}_store")

    S.add("sync", lambda e: e.nop(), extra=[x2B.w] + x2B.rs, tag="drain")

    S.finalize()
    sem_ctx = ExitStack()
    sems = {q: sem_ctx.enter_context(nc.semaphore(f"sem_{q}")) for q in QUEUES}

    def runner(qname):
        def _(eng):
            clock = {q: 0 for q in QUEUES}
            for st in S.plan[qname]:
                needed = {}
                for d in st.deps:
                    if d.q == qname == "pe":
                        continue
                    if d.cum is None:
                        continue
                    if d.cum > needed.get(d.q, 0):
                        needed[d.q] = d.cum
                for qd, v in needed.items():
                    if clock[qd] < v:
                        eng.wait_ge(sems[qd], v)
                        clock[qd] = v
                last = st.emit(eng)
                if st.cum is not None:
                    if last is None:
                        raise RuntimeError(f"step {st.tag} returned no instruction")
                    last.then_inc(sems[qname], st.inc)
        return _

    with nc.Block() as block:
        block.tensor(runner("pe"))
        block.scalar(runner("act"))
        block.vector(runner("dve"))
        block.gpsimd(runner("pool"))
        block.sync(runner("sync"))

    sbuf_ctx.close()
    psum_ctx.close()
    sem_ctx.close()
    return nc


_IN_NAMES = {
    "fr_w": "L0_attn_fr_w", "dp_w": "L0_attn_dp_w", "bp_w": "L0_attn_bp_w",
    "cp_w": "L0_attn_cp_w", "m1_w": "L0_attn_m1_w", "m2_w": "L0_attn_m2_w",
    "gate_w": "L0_attn_gate_w", "ffn1_w": "L0_attn_ffn1_w", "ffn2_w": "L0_attn_ffn2_w",
    "W1": "L0_ffn_W1", "W2": "L0_ffn_W2", "Wr": "L0_ffn_Wr",
    "norm_g": "L0_attn_norm_g", "norm_b": "L0_attn_norm_b",
    "fr_b": "L0_attn_fr_b", "dp_b": "L0_attn_dp_b", "bp_b": "L0_attn_bp_b",
    "cp_b": "L0_attn_cp_b", "skip": "L0_attn_skip", "m1_b": "L0_attn_m1_b",
    "m2_b": "L0_attn_m2_b", "gate_b": "L0_attn_gate_b",
    "ffn_g": "L0_attn_ffn_g", "ffn_bn": "L0_attn_ffn_bn",
    "ffn1_b": "L0_attn_ffn1_b", "ffn2_b": "L0_attn_ffn2_b",
    "b1": "L0_ffn_b1", "b2": "L0_ffn_b2",
}


def kernel(**inputs):
    x = np.ascontiguousarray(np.asarray(inputs["L0_attn_x"], dtype=np.float32))
    w = {k: np.ascontiguousarray(np.asarray(inputs[v], dtype=np.float32))
         for k, v in _IN_NAMES.items()}
    nc = build()
    in_maps = []
    for c in range(8):
        b, h = c // 2, c % 2
        xb = x[b]
        pref = xb[0:4096].T
        own = xb[h * 4096:(h + 1) * 4096].T
        full = np.concatenate([pref, own], axis=1)
        tiled = full.reshape(8, P, 16, NT).transpose(2, 1, 0, 3).reshape(16, P, C * NT)
        m = dict(w)
        m["xfm"] = np.ascontiguousarray(tiled)
        m["mask"] = np.full((P, 1), float(h), np.float32)
        in_maps.append(m)
    res = run_bass_kernel_spmd(nc, in_maps, core_ids=list(range(8)))
    out = np.empty((4, 8192, 1024), np.float32)
    for c in range(8):
        b, h = c // 2, c % 2
        z = np.asarray(res.results[c]["x2"])
        rows = z.reshape(8, P, 4, D).transpose(0, 2, 1, 3).reshape(4096, D)
        out[b, h * 4096:(h + 1) * 4096, :] = rows
    return out


# version 16
# speedup vs baseline: 1.2379x; 1.0241x over previous
import numpy as np
import concourse.bass as bass
import concourse.mybir as mybir
from concourse.bass_utils import run_bass_kernel_spmd
from contextlib import ExitStack

F32 = mybir.dt.float32
BF16 = mybir.dt.bfloat16
AF = mybir.ActivationFunctionType
ALU = mybir.AluOpType
AX = mybir.AxisListType

P = 128
NT = 512
C = 8
D = 1024
H2 = 2048
H4 = 4096
NE = 8
EPS = 1e-5
QUEUES = ("pe", "act", "dve", "pool", "sync")


class Step:
    __slots__ = ("q", "emit", "deps", "inc", "cum", "needs", "tag", "dma")

    def __init__(self, q, emit, inc, tag, dma):
        self.q = q
        self.emit = emit
        self.deps = []
        self.inc = inc
        self.cum = None
        self.needs = False
        self.tag = tag
        self.dma = dma


class Buf:
    __slots__ = ("ap", "w", "rs", "name")

    def __init__(self, ap, name=""):
        self.ap = ap
        self.w = None
        self.rs = []
        self.name = name


class Sched:
    def __init__(self):
        self.plan = {q: [] for q in QUEUES}

    def add(self, q, emit, reads=(), writes=(), extra=(), inc=1, dma=False, tag=""):
        st = Step(q, emit, 16 if dma else inc, tag, dma)
        deps = []
        for d in extra:
            if d is not None:
                deps.append(d)
        for b in reads:
            if b.w is not None:
                deps.append(b.w)
        for b in writes:
            if b.w is not None:
                deps.append(b.w)
            deps.extend(b.rs)
        seen = set()
        for d in deps:
            if id(d) not in seen:
                seen.add(id(d))
                st.deps.append(d)
                d.needs = True
        if dma:
            st.needs = True
        for b in reads:
            b.rs.append(st)
        for b in writes:
            b.w = st
            b.rs = []
        self.plan[q].append(st)
        return st

    def finalize(self):
        for q in QUEUES:
            cum = {False: 0, True: 0}
            for st in self.plan[q]:
                if st.needs:
                    cum[st.dma] += st.inc
                    st.cum = cum[st.dma]
                else:
                    st.cum = None


def build(n_p=8, n_m=8, n_e=NE, dbg=False, pump=True, router_in=False):
    nc = bass.Bass("TRN2", target_bir_lowering=False, debug=False)
    S = Sched()

    xfm = nc.dram_tensor("xfm", [16, P, C * NT], F32, kind="ExternalInput").ap()
    maskd = nc.dram_tensor("mask", [P, 1], F32, kind="ExternalInput").ap()
    wD = {}
    for nm in ("fr_w", "dp_w", "bp_w", "cp_w"):
        wD[nm] = nc.dram_tensor(nm, [D, D], F32, kind="ExternalInput").ap()
    wD["m1_w"] = nc.dram_tensor("m1_w", [D, H2], F32, kind="ExternalInput").ap()
    wD["m2_w"] = nc.dram_tensor("m2_w", [H2, D], F32, kind="ExternalInput").ap()
    wD["gate_w"] = nc.dram_tensor("gate_w", [H2, D], F32, kind="ExternalInput").ap()
    wD["ffn1_w"] = nc.dram_tensor("ffn1_w", [D, H4], F32, kind="ExternalInput").ap()
    wD["ffn2_w"] = nc.dram_tensor("ffn2_w", [H4, D], F32, kind="ExternalInput").ap()
    wD["W1"] = nc.dram_tensor("W1", [NE, D, H4], F32, kind="ExternalInput").ap()
    wD["W2"] = nc.dram_tensor("W2", [NE, H4, D], F32, kind="ExternalInput").ap()
    wD["Wr"] = nc.dram_tensor("Wr", [D, NE], F32, kind="ExternalInput").ap()
    pD = {}
    for nm, sz in (("norm_g", D), ("norm_b", D), ("fr_b", D), ("dp_b", D),
                   ("bp_b", D), ("cp_b", D), ("skip", D), ("m1_b", H2),
                   ("m2_b", D), ("gate_b", D), ("ffn_g", D), ("ffn_bn", D),
                   ("ffn1_b", H4), ("ffn2_b", D)):
        pD[nm] = nc.dram_tensor(nm, [sz], F32, kind="ExternalInput").ap()
    pD["b1"] = nc.dram_tensor("b1", [NE, H4], F32, kind="ExternalInput").ap()
    pD["b2"] = nc.dram_tensor("b2", [NE, D], F32, kind="ExternalInput").ap()
    x2 = nc.dram_tensor("x2", [8, P, 4 * D], F32, kind="ExternalOutput").ap()
    dbg_out = {}
    if dbg:
        for nm in ("d_xn", "d_ssd", "d_mam", "d_y", "d_x1"):
            dbg_out[nm] = nc.dram_tensor(nm, [P, C * NT], F32, kind="ExternalOutput").ap()

    scr = {}
    for nm, mg in (("fr_w", 2), ("dp_w", 2), ("bp_w", 2), ("cp_w", 2),
                   ("m1_w", 4), ("ffn1_w", 8)):
        scr[nm] = nc.dram_tensor("s_" + nm, [mg, P, 8 * 512], BF16, kind="Internal").ap()
    for nm, nq in (("m2_w", 4), ("gate_w", 4), ("ffn2_w", 8)):
        scr[nm] = nc.dram_tensor("s_" + nm, [nq, P, 4 * D], BF16, kind="Internal").ap()
    scr["W1"] = nc.dram_tensor("s_W1", [NE, 8, P, 8 * 512], BF16, kind="Internal").ap()
    scr["W2"] = nc.dram_tensor("s_W2", [NE, 2, P, 32 * 512], BF16, kind="Internal").ap()
    sbuf_ctx = ExitStack()

    def sb(name, shape, dt):
        return sbuf_ctx.enter_context(nc.sbuf_tensor(name, shape, dt))

    fx0 = sb("fx0", [P, C * NT], F32)
    fxn = sb("fxn", [P, C * NT], F32)
    fscr = sb("fscr", [P, C * NT], F32)
    fa = sb("fa", [P, C * NT], F32)
    fst = sb("fst", [P, C * NT], F32)
    bxn = sb("bxn", [P, C * NT], BF16)
    blow = sb("blow", [P, C * NT], BF16)
    bhigh = sb("bhigh", [P, C * NT], BF16)
    bssd = sb("bssd", [P, C * NT], BF16)
    bmam = sb("bmam", [P, C * NT], BF16)
    hbuf = sb("hbuf", [P, 32 * NT], BF16)
    wsA = [sb(f"wsA{i}", [P, 8 * 512], BF16) for i in range(4)]
    pv = {}
    for nm in ("norm_g", "norm_b", "fr_b", "dp_b", "bp_b", "cp_b", "skip",
               "m2_b", "gate_b", "ffn_g", "ffn_bn", "ffn2_b"):
        pv[nm] = sb("p_" + nm, [P, C], F32)
    pv["m1_b"] = sb("p_m1_b", [P, 16], F32)
    pv["ffn1_b"] = sb("p_ffn1_b", [P, 32], F32)
    pv["b1"] = sb("p_b1", [P, NE * 32], F32)
    b2bf = sb("b2bf", [NE, D], BF16)
    wrbf = sb("wrbf", [P, C * NE], BF16)
    masks = sb("masks", [P, 1], F32)
    epst = sb("epst", [P, 1], F32)
    ident = sb("ident", [P, P], F32)
    ones_f = sb("ones_f", [P, 1], F32)
    ones_row = sb("ones_row", [1, P], BF16)
    nones_row = sb("nones_row", [1, P], BF16)
    RW = sb("RW", [P, 6 * 512], F32)
    RWB = sb("RWB", [P, 4 * 512], BF16)
    carP = sb("carP", [P, C], F32)
    carM = sb("carM", [P, C], F32)
    minit = sb("minit", [P, C], F32)
    comb = sb("comb", [P, 4 * NE], F32)
    combT = sb("combT", [NE, 4 * P], BF16)
    tkt = sb("tkt", [P, 8], F32)
    eqt = sb("eqt", [P, 16], F32)
    mskt = sb("mskt", [P, 8], F32)
    acc = fst

    psum_ctx = ExitStack()
    pbank = [psum_ctx.enter_context(nc.psum_tensor(f"pb{i}", [P, NT], F32)) for i in range(8)]

    B = lambda t, nm: Buf(t, nm)
    bufs = {}
    for nm, t in [("fx0", fx0), ("fxn", fxn), ("fscr", fscr), ("fa", fa),
                  ("fst", fst), ("bxn", bxn),
                  ("blow", blow), ("bhigh", bhigh),
                  ("bssd", bssd), ("bmam", bmam), ("hbuf", hbuf),
                  ("RW", RW), ("RWB", RWB),
                  ("carP", carP), ("carM", carM), ("minit", minit),
                  ("comb", comb), ("combT", combT), ("tkt", tkt), ("eqt", eqt),
                  ("mskt", mskt), ("b2bf", b2bf),
                  ("masks", masks), ("ident", ident)]:
        bufs[nm] = B(t, nm)
    bufs["acc"] = bufs["fst"]
    bufs["bmam0"] = Buf(bmam, "bmam0")
    bufs["bmam1"] = Buf(bmam, "bmam1")
    hQ = [Buf(hbuf, f"hQ{i}") for i in range(8)]
    bxnH = [Buf(bxn, "bxnH0"), Buf(bxn, "bxnH1")]
    pB = [B(pbank[i], f"pb{i}") for i in range(8)]
    wsB = [B(t, f"wsA{i}") for i, t in enumerate(wsA)]
    pvB = {nm: B(t, "p_" + nm) for nm, t in pv.items()}
    scrB = {nm: B(a, "s_" + nm) for nm, a in scr.items()}
    x2B = B(x2, "x2")

    ring = {"ws": 0, "wv2": 0}
    wv2B = [(bhigh, [bufs["bhigh"]]),
            (bmam, [bufs["bmam"], bufs["bmam0"], bufs["bmam1"]])]

    def ring_next(kind, bl):
        i = ring[kind] % len(bl)
        ring[kind] += 1
        return bl[i]

    def dma(q, out_ap, in_ap, reads=(), writes=(), extra=(), tag="", slow=False):
        def em(e):
            return e.dma_start(out=out_ap, in_=in_ap,
                               **({"allow_slow_non_contiguous": True} if slow else {}))
        return S.add(q, em, reads, writes, extra, dma=True, tag=tag)

    def act(fn, reads=(), writes=(), extra=(), tag=""):
        return S.add("act", fn, reads, writes, extra, tag=tag)

    def dve(fn, reads=(), writes=(), extra=(), tag=""):
        return S.add("dve", fn, reads, writes, extra, tag=tag)

    def pe(fn, reads=(), writes=(), extra=(), tag=""):
        return S.add("pe", fn, reads, writes, extra, tag=tag)

    def chunk(t, c, n=NT):
        return t[:, c * n:(c + 1) * n]

    for nm in pv:
        src = pD[nm]
        w = pv[nm].shape[1]
        if nm == "b1":
            ap = src.rearrange("e (mk p) -> p e mk", p=P)
            dst = pv[nm][:, :].rearrange("p (e mk) -> p e mk", e=NE)
        else:
            ap = src.rearrange("(c p) -> p c", p=P)
            dst = pv[nm][:, :]
        dma("sync", dst, ap, writes=[pvB[nm]], tag=f"ld_{nm}", slow=True)
    dma("pool", b2bf[:, :], pD["b2"], writes=[bufs["b2bf"]], tag="ld_b2")
    dma("sync", masks[:, :], maskd, writes=[bufs["masks"]], tag="ld_mask")
    dma("pool", wrbf[:, :].rearrange("p (k e) -> p k e", k=C),
        wD["Wr"].rearrange("(k p) e -> p k e", p=P),
        writes=[Buf(wrbf, "wrbf")], tag="ld_wr", slow=True)
    wrB = Buf(wrbf, "wrbf")
    wrB.w = S.plan["pool"][-1]
    for nm in ("fr_b", "dp_b", "gate_b"):
        def em(v, t=pv[nm]):
            return v.tensor_scalar(t[:, :], t[:, :], 0.5, None, ALU.mult)
        dve(em, reads=[pvB[nm]], writes=[pvB[nm]], tag=f"half_{nm}")
    S.add("pool", lambda g: g.memset(ident[:, :], 0.0), writes=[bufs["ident"]], tag="id0")
    S.add("pool", lambda g: g.affine_select(
        out=ident[:, :], in_=ident[:, :], compare_op=ALU.not_equal, fill=1.0,
        base=0, pattern=[[-1, P]], channel_multiplier=1),
        reads=[bufs["ident"]], writes=[bufs["ident"]], tag="id1")
    onesFB = Buf(ones_f, "onesf")
    dve(lambda v: v.memset(ones_f[:, :], 1.0), writes=[onesFB], tag="ones_f")
    onesRB = Buf(ones_row, "onesr")
    dve(lambda v: v.memset(ones_row[:, :], 1.0), writes=[onesRB], tag="ones_r")
    dve(lambda v: v.memset(nones_row[:, :], -1.0), writes=[onesRB], tag="nones_r")
    epsB = Buf(epst, "eps")
    dve(lambda v: v.memset(epst[:, :], EPS), writes=[epsB], tag="eps")
    carPB = bufs["carP"]
    dve(lambda v: v.memset(carP[:, :], 0.0), writes=[carPB], tag="carP0")
    dve(lambda v: v.memset(carM[:, :], 0.0), writes=[bufs["carM"]], tag="carM0")

    converted = set()

    def stream_mg_slab(nm, mg, nk=8):
        sbuf = ring_next("ws", wsB)
        if nm.startswith("W1_"):
            e = int(nm[3:])
            sc, scb, wsrc = scr["W1"][e, mg], scrB["W1"], wD["W1"][e]
        else:
            sc, scb, wsrc = scr[nm][mg], scrB[nm], wD[nm]
        key = ("mg", nm, mg)
        if key not in converted:
            converted.add(key)
            fsrc = wsrc[:, mg * 512:(mg + 1) * 512].rearrange("(k p) f -> p k f", p=P)
            dma("pool", sbuf.ap[:, 0:nk * 512].rearrange("p (k f) -> p k f", k=nk),
                fsrc, writes=[sbuf], tag=f"cv_{nm}_{mg}")
            dma("sync", sc[:, 0:nk * 512], sbuf.ap[:, 0:nk * 512],
                reads=[sbuf], writes=[scb], tag=f"wb_{nm}_{mg}")
        else:
            dma("sync", sbuf.ap[:, 0:nk * 512], sc[:, 0:nk * 512],
                reads=[scb], writes=[sbuf], tag=f"st_{nm}_{mg}")
        return sbuf

    def stream_kfat(nm, kq):
        sbuf = ring_next("ws", wsB)
        key = ("ks", nm, kq)
        if key not in converted:
            converted.add(key)
            fsrc = wD[nm][kq * 512:(kq + 1) * 512, :].rearrange("(k p) f -> p k f", p=P)
            dma("pool", sbuf.ap[:, :].rearrange("p (k f) -> p k f", k=4),
                fsrc, writes=[sbuf], tag=f"cv_{nm}_{kq}")
            dma("sync", scr[nm][kq], sbuf.ap[:, :],
                reads=[sbuf], writes=[scrB[nm]], tag=f"wb_{nm}_{kq}")
        else:
            dma("sync", sbuf.ap[:, :], scr[nm][kq],
                reads=[scrB[nm]], writes=[sbuf], tag=f"st_{nm}_{kq}")
        return sbuf

    def stream_w2half(e, fh, hq):
        tens, wbl = wv2B[ring["wv2"] % 2]
        ring["wv2"] += 1
        key = ("w2", e, fh, hq)
        if key not in converted:
            converted.add(key)
            fsrc = wD["W2"][e][hq * 1024:(hq + 1) * 1024, fh * 512:(fh + 1) * 512] \
                .rearrange("(k p) f -> p k f", p=P)
            dma("pool", tens[:, 0:4096].rearrange("p (k f) -> p k f", k=8),
                fsrc, writes=wbl, tag=f"cv_W2_{e}_{fh}_{hq}")
            dma("sync", scr["W2"][e, fh][:, hq * 4096:(hq + 1) * 4096], tens[:, 0:4096],
                reads=[wbl[0]], writes=[scrB["W2"]], tag=f"wb_W2_{e}_{fh}_{hq}")
        else:
            dma("sync", tens[:, 0:4096], scr["W2"][e, fh][:, hq * 4096:(hq + 1) * 4096],
                reads=[scrB["W2"]], writes=wbl, tag=f"st_W2_{e}_{fh}_{hq}")
        return wbl

    def mm_group(pbuf, pairs, reads, extra=(), tag=""):
        def em(t):
            last = None
            n = len(pairs)
            for i, (l, r) in enumerate(pairs):
                last = t.matmul(pbuf.ap[:, :], l, r,
                                start=(i == 0), stop=(i == n - 1))
            return last
        return pe(em, reads=reads, writes=[pbuf], extra=extra, tag=tag)

    rot_idx = [0]
    mid_idx = [0]

    def rot_bank():
        i = 6 + (rot_idx[0] % 2)
        rot_idx[0] += 1
        return pB[i]

    def mid_bank():
        i = 4 + (mid_idx[0] % 4)
        mid_idx[0] += 1
        return pB[i]

    quad_idx = [0]

    def next_quad():
        b0 = 4 if quad_idx[0] % 2 == 0 else 0
        quad_idx[0] += 1
        return b0

    def proj_1024(nm, rhs_bf, rhs_buf, epi, tag, bank=None):
        for mg in range(2):
            slab = stream_mg_slab(nm, mg)
            b0 = next_quad()

            def em(t, slab=slab, b0=b0):
                last = None
                for j in range(4):
                    for k in range(C):
                        last = t.matmul(pbank[b0 + j][:, :],
                                        slab.ap[:, k * 512 + j * P:k * 512 + j * P + P],
                                        chunk(rhs_bf, k),
                                        start=(k == 0), stop=(k == C - 1),
                                        skip_group_check=True)
                return last
            g = pe(em, reads=[slab, rhs_buf],
                   writes=[pB[b0], pB[b0 + 1], pB[b0 + 2], pB[b0 + 3]],
                   tag=f"{tag}_mg{mg}")
            for j in range(4):
                epi(mg * 4 + j, pB[b0 + j], g)

    def ln_stats(src_buf, src_t, sq_buf, sq_t, tag):
        g1 = pe(lambda t: [t.matmul(pbank[6][0:1, :], ones_f[:, 0:1], chunk(src_t, k),
                                    start=(k == 0), stop=(k == C - 1)) for k in range(C)][-1],
                reads=[src_buf, onesFB], writes=[pB[6]], tag=f"{tag}_sum")
        g2 = pe(lambda t: [t.matmul(pbank[7][0:1, :], ones_f[:, 0:1], chunk(sq_t, k),
                                    start=(k == 0), stop=(k == C - 1)) for k in range(C)][-1],
                reads=[sq_buf, onesFB], writes=[pB[7]], tag=f"{tag}_sq")
        return g1, g2

    RWb, RWBb = bufs["RW"], bufs["RWB"]

    def Rr(j):
        return RW[0:1, j * 512:(j + 1) * 512]

    def Br(j):
        return RWB[0:1, j * 512:(j + 1) * 512]

    def rmul(dj, aj, bj, tag):
        dve(lambda v: v.tensor_tensor(Rr(dj), Rr(aj), Rr(bj), ALU.mult),
            reads=[RWb], writes=[RWb], tag=tag)

    def rneg(j, tag):
        dve(lambda v: v.tensor_scalar(Rr(j), Rr(j), -1.0, None, ALU.mult),
            reads=[RWb], writes=[RWb], tag=tag)

    def rsqrt_row(j, tag, src_j=None):
        sj = j if src_j is None else src_j
        act(lambda s: s.activation(Rr(j), Rr(sj), AF.Sqrt, bias=epst[0:1, 0:1]),
            reads=[RWb, epsB], writes=[RWb], tag=f"{tag}_sd")
        dve(lambda v: v.reciprocal(Rr(j), Rr(j)),
            reads=[RWb], writes=[RWb], tag=f"{tag}_rc")

    def hi_lo(src_j, hi_j, lo_j, scr_j, tag):
        dve(lambda v: v.tensor_copy(Br(hi_j), Rr(src_j)),
            reads=[RWb], writes=[RWBb], tag=f"{tag}_hi")
        dve(lambda v: v.tensor_copy(Rr(scr_j), Br(hi_j)),
            reads=[RWBb], writes=[RWb], tag=f"{tag}_hf")
        dve(lambda v: v.tensor_tensor(Rr(scr_j), Rr(src_j), Rr(scr_j), ALU.subtract),
            reads=[RWb], writes=[RWb], tag=f"{tag}_lf")
        dve(lambda v: v.tensor_copy(Br(lo_j), Rr(scr_j)),
            reads=[RWb], writes=[RWBb], tag=f"{tag}_lo")

    def bcast(hi_j, lo_j, pbuf, tag, neg=False):
        orow = nones_row if neg else ones_row
        def em(t):
            t.matmul(pbuf.ap[:, :], orow[0:1, :], Br(hi_j),
                     start=True, stop=False, skip_group_check=True)
            return t.matmul(pbuf.ap[:, :], orow[0:1, :], Br(lo_j),
                            start=False, stop=True, skip_group_check=True)
        return pe(em, reads=[RWBb, onesRB], writes=[pbuf], tag=tag)

    def ln_rows_mv(g1, g2, tag):
        dve(lambda v: v.tensor_scalar(Rr(0), pbank[6][0:1, :], 1.0 / D, None, ALU.mult),
            reads=[pB[6]], writes=[RWb], extra=[g1], tag=f"{tag}_m")
        dve(lambda v: v.tensor_scalar(Rr(1), pbank[7][0:1, :], 1.0 / D, None, ALU.mult),
            reads=[pB[7]], writes=[RWb], extra=[g2], tag=f"{tag}_ss")
        rmul(4, 0, 0, f"{tag}_m2")
        dve(lambda v: v.tensor_tensor(Rr(1), Rr(1), Rr(4), ALU.subtract),
            reads=[RWb], writes=[RWb], tag=f"{tag}_v")

    def ln12(x0b, tag):
        act(lambda s: s.activation(fxn[:, :], x0b.ap[:, :], AF.Square),
            reads=[x0b], writes=[bufs["fxn"]], tag=f"{tag}_sq")
        g1, g2 = ln_stats(x0b, fx0, bufs["fxn"], fxn, tag)
        ln_rows_mv(g1, g2, tag)
        rsqrt_row(2, f"{tag}_r1", src_j=1)
        rmul(4, 2, 2, f"{tag}_rq")
        rmul(4, 1, 4, f"{tag}_v2")
        rsqrt_row(4, f"{tag}_r2")
        rmul(3, 0, 2, f"{tag}_s2a")
        rmul(5, 2, 4, f"{tag}_rr")
        rmul(0, 0, 5, f"{tag}_mm")
        hi_lo(5, 0, 1, 4, f"{tag}_hrr")
        hi_lo(0, 2, 3, 4, f"{tag}_hmm")

    def ln_apply_xn(x0b, g_nm, b_nm, tag):
        p_rr = pB[4 + (rot_idx[0] % 2)]; rot_idx[0] += 1
        p_mm = pB[4 + (rot_idx[0] % 2)]; rot_idx[0] += 1
        bc1 = bcast(0, 1, p_rr, f"{tag}_bc1")
        bc2 = bcast(2, 3, p_mm, f"{tag}_bc2", neg=True)
        for c in range(C):
            dve(lambda v, c=c: v.tensor_tensor(chunk(fxn, c), chunk(x0b.ap, c),
                                               p_rr.ap[:, :], ALU.mult),
                reads=[x0b, p_rr], writes=[bufs["fxn"]], extra=[bc1], tag=f"{tag}_u{c}")
            dve(lambda v, c=c: v.tensor_tensor(chunk(fxn, c), chunk(fxn, c),
                                               p_mm.ap[:, :], ALU.add),
                reads=[bufs["fxn"], p_mm], writes=[bufs["fxn"]], extra=[bc2], tag=f"{tag}_a{c}")
            dve(lambda v, c=c: v.tensor_scalar(chunk(fxn, c), chunk(fxn, c),
                                               pv[g_nm][:, c:c + 1], pv[b_nm][:, c:c + 1],
                                               ALU.mult, ALU.add),
                reads=[bufs["fxn"], pvB[g_nm], pvB[b_nm]], writes=[bufs["fxn"]], tag=f"{tag}_g{c}")
        dve(lambda v: v.tensor_copy(bxn[:, :], fxn[:, :]),
            reads=[bufs["fxn"]], writes=[bufs["bxn"], bxnH[0], bxnH[1]], tag=f"{tag}_bf")

    def front(tile_idx, tag):
        x0b = bufs["fx0"]
        dma("sync", fx0[:, :], xfm[tile_idx], writes=[x0b], tag=f"{tag}_x0")
        ln12(x0b, f"{tag}_ln")
        ln_apply_xn(x0b, "norm_g", "norm_b", f"{tag}_xn")
        return x0b

    def fr_low_high(with_high, tag):
        xnbfb = bufs["bxn"]
        if with_high:
            hi_lo(2, 0, 1, 4, f"{tag}_hr1")
            hi_lo(3, 2, 3, 4, f"{tag}_hs2")

        def epi(m, pb_, g):
            act(lambda s, m=m, pb_=pb_: s.activation(
                chunk(fscr, m), pb_.ap[:, :], AF.Tanh,
                bias=pv["fr_b"][:, m:m + 1], scale=0.5),
                reads=[pb_, pvB["fr_b"]], writes=[bufs["fscr"]], extra=[g], tag=f"{tag}_e{m}")
        proj_1024("fr_w", bxn, xnbfb, epi, tag)
        dve(lambda v: v.tensor_scalar(fscr[:, :], fscr[:, :], 0.5, 0.5, ALU.mult, ALU.add),
            reads=[bufs["fscr"]], writes=[bufs["fscr"]], tag=f"{tag}_lg")
        dve(lambda v: v.tensor_tensor(blow[:, :], fxn[:, :], fscr[:, :], ALU.mult),
            reads=[bufs["fxn"], bufs["fscr"]], writes=[bufs["blow"]], tag=f"{tag}_low")
        if with_high:
            dve(lambda v: v.tensor_scalar(fscr[:, :], fscr[:, :], -1.0, 1.0, ALU.mult, ALU.add),
                reads=[bufs["fscr"]], writes=[bufs["fscr"]], tag=f"{tag}_ilg")
            dve(lambda v: v.tensor_tensor(bhigh[:, :], fxn[:, :], fscr[:, :], ALU.mult),
                reads=[bufs["fxn"], bufs["fscr"]], writes=[bufs["bhigh"]], tag=f"{tag}_high")

    def dp_bp_scan(init_buf, init_zero, car_buf, tag):
        lowb = bufs["blow"]

        def epi_a(m, pb_, g):
            act(lambda s, m=m, pb_=pb_: s.activation(
                chunk(fa, m), pb_.ap[:, :], AF.Tanh,
                bias=pv["dp_b"][:, m:m + 1], scale=0.5),
                reads=[pb_, pvB["dp_b"]], writes=[bufs["fa"]], extra=[g], tag=f"{tag}_ea{m}")
        proj_1024("dp_w", blow, lowb, epi_a, f"{tag}_dp")
        dve(lambda v: v.tensor_scalar(fa[:, :], fa[:, :], 0.5, 0.5, ALU.mult, ALU.add),
            reads=[bufs["fa"]], writes=[bufs["fa"]], tag=f"{tag}_a")

        def epi_b(m, pb_, g):
            act(lambda s, m=m, pb_=pb_: s.activation(
                chunk(fscr, m), pb_.ap[:, :], AF.Tanh,
                bias=pv["bp_b"][:, m:m + 1]),
                reads=[pb_, pvB["bp_b"]], writes=[bufs["fscr"]], extra=[g], tag=f"{tag}_eb{m}")
        proj_1024("bp_w", blow, lowb, epi_b, f"{tag}_bp")
        dve(lambda v: v.tensor_tensor(fscr[:, :], fscr[:, :], fxn[:, :], ALU.mult),
            reads=[bufs["fscr"], bufs["fxn"]], writes=[bufs["fscr"]], tag=f"{tag}_b")
        for c in range(C):
            if init_zero:
                ini = 0.0
                rd = [bufs["fa"], bufs["fscr"]]
            else:
                ini = init_buf.ap[:, c:c + 1]
                rd = [bufs["fa"], bufs["fscr"], init_buf]
            dve(lambda v, c=c, ini=ini: v.tensor_tensor_scan(
                chunk(fst, c), chunk(fa, c), chunk(fscr, c), ini, ALU.mult, ALU.add),
                reads=rd, writes=[bufs["fst"]], tag=f"{tag}_sc{c}")
        dve(lambda v: v.tensor_copy(
            car_buf.ap[:, :], fst[:, :].rearrange("p (c t) -> p c t", c=C)[:, :, NT - 1]),
            reads=[bufs["fst"]], writes=[car_buf], tag=f"{tag}_car")

    for k in range(n_p):
        tag = f"P{k}"
        front(k, tag)
        fr_low_high(False, f"{tag}_fr")
        dp_bp_scan(carPB, k == 0, carPB, tag)

    dve(lambda v: v.tensor_scalar(minit[:, :], carP[:, :], masks[:, 0:1], None, ALU.mult),
        reads=[carPB, bufs["masks"]], writes=[bufs["minit"]], tag="minit")

    if n_m > 0:
        front(8, "M0")
    for k in range(n_m):
        tag = f"M{k}"
        x0b = bufs["fx0"]
        fr_low_high(True, f"{tag}_fr")
        init_b = bufs["minit"] if k == 0 else bufs["carM"]
        dp_bp_scan(init_b, False, bufs["carM"], tag)
        if dbg and k == 0:
            dma("act", dbg_out["d_xn"], fxn[:, :], reads=[bufs["fxn"]], tag="dbg_xn")

        def epi_c(m, pb_, g):
            act(lambda s, m=m, pb_=pb_: s.activation(
                chunk(fscr, m), pb_.ap[:, :], AF.Tanh, bias=pv["cp_b"][:, m:m + 1]),
                reads=[pb_, pvB["cp_b"]], writes=[bufs["fscr"]], extra=[g], tag=f"{tag}_ec{m}")
        proj_1024("cp_w", bhigh, bufs["bhigh"], epi_c, f"{tag}_cp", bank=mid_bank)
        dve(lambda v: v.tensor_tensor(fst[:, :], fscr[:, :], fst[:, :], ALU.mult),
            reads=[bufs["fscr"], bufs["fst"]], writes=[bufs["fst"]], tag=f"{tag}_s1")
        for c in range(C):
            dve(lambda v, c=c: v.scalar_tensor_tensor(
                chunk(fst, c), chunk(fxn, c), pv["skip"][:, c:c + 1], chunk(fst, c),
                ALU.mult, ALU.add),
                reads=[bufs["fxn"], pvB["skip"], bufs["fst"]], writes=[bufs["fst"]],
                tag=f"{tag}_ssd{c}")
        dve(lambda v: v.tensor_copy(bssd[:, :], fst[:, :]),
            reads=[bufs["fst"]], writes=[bufs["bssd"]], tag=f"{tag}_ssdbf")
        if dbg and k == 0:
            dma("act", dbg_out["d_ssd"], fst[:, :], reads=[bufs["fst"]], tag="dbg_ssd")

        for mg in range(4):
            slab = stream_mg_slab("m1_w", mg)
            b0 = next_quad()

            def em(t, slab=slab, b0=b0):
                last = None
                for j in range(4):
                    for kk in range(C):
                        last = t.matmul(pbank[b0 + j][:, :],
                                        slab.ap[:, kk * 512 + j * P:kk * 512 + j * P + P],
                                        chunk(blow, kk),
                                        start=(kk == 0), stop=(kk == C - 1),
                                        skip_group_check=True)
                return last
            g = pe(em, reads=[slab, bufs["blow"]],
                   writes=[pB[b0], pB[b0 + 1], pB[b0 + 2], pB[b0 + 3]],
                   tag=f"{tag}_m1_{mg}")
            for j in range(4):
                m = mg * 4 + j
                act(lambda s, m=m, j=j, b0=b0: s.activation(
                    hbuf[:, m * 512:(m + 1) * 512], pbank[b0 + j][:, :], AF.Gelu,
                    bias=pv["m1_b"][:, m:m + 1]),
                    reads=[pB[b0 + j], pvB["m1_b"]], writes=[hQ[m // 4]], extra=[g],
                    tag=f"{tag}_m1e{m}")

        def half_m_kslab(nm, nk, rhs_fn, rhs_bufs, epi, tag2, rhs_bufs_fn=None):
            for mg in range(2):
                g = None
                for kq in range(nk // 4):
                    slab = stream_kfat(nm, kq)
                    rbufs = rhs_bufs_fn(kq) if rhs_bufs_fn else rhs_bufs

                    def em(t, kq=kq, mg=mg, slab=slab):
                        last = None
                        for ki in range(4):
                            kk = kq * 4 + ki
                            for j in range(4):
                                m = mg * 4 + j
                                last = t.matmul(pbank[j][:, :],
                                                slab.ap[:, ki * D + m * P:ki * D + (m + 1) * P],
                                                rhs_fn(kk), start=(kk == 0), stop=(kk == nk - 1),
                                                skip_group_check=True)
                        return last
                    g = pe(em, reads=[slab] + rbufs,
                           writes=[pB[0], pB[1], pB[2], pB[3]], tag=f"{tag2}_{mg}_{kq}")
                for j in range(4):
                    epi(mg * 4 + j, pB[j], g)

        def m2_epi(m, pb_, g):
            dve(lambda v, m=m, pb_=pb_: v.tensor_scalar(
                chunk(fa, m), pb_.ap[:, :], pv["m2_b"][:, m:m + 1], None, ALU.add),
                reads=[pb_, pvB["m2_b"]], writes=[bufs["fa"]], extra=[g], tag=f"{tag}_m2e{m}")
        half_m_kslab("m2_w", 16, lambda kk: hbuf[:, kk * 512:(kk + 1) * 512],
                     None, m2_epi, f"{tag}_m2",
                     rhs_bufs_fn=lambda kq: [hQ[kq]])
        dve(lambda v: v.tensor_copy(bmam[:, 0:4 * NT], fa[:, 0:4 * NT]),
            reads=[bufs["fa"]], writes=[bufs["bmam0"]], tag=f"{tag}_mambf0")
        dve(lambda v: v.tensor_copy(bmam[:, 4 * NT:8 * NT], fa[:, 4 * NT:8 * NT]),
            reads=[bufs["fa"]], writes=[bufs["bmam1"]], tag=f"{tag}_mambf1")
        if dbg and k == 0:
            dma("act", dbg_out["d_mam"], fa[:, :], reads=[bufs["fa"]], tag="dbg_mam")

        def gate_rhs(kk):
            return chunk(bssd, kk) if kk < 8 else chunk(bmam, kk - 8)

        def gate_epi(m, pb_, g):
            act(lambda s, m=m, pb_=pb_: s.activation(
                chunk(fscr, m), pb_.ap[:, :], AF.Tanh,
                bias=pv["gate_b"][:, m:m + 1], scale=0.5),
                reads=[pb_, pvB["gate_b"]], writes=[bufs["fscr"]], extra=[g],
                tag=f"{tag}_ge{m}")
        half_m_kslab("gate_w", 16, gate_rhs, None, gate_epi, f"{tag}_gate",
                     rhs_bufs_fn=lambda kq: [bufs["bssd"]] if kq < 2 else
                     [bufs["bmam0"] if kq == 2 else bufs["bmam1"]])

        dve(lambda v: v.tensor_tensor(fst[:, :], fst[:, :], fa[:, :], ALU.subtract),
            reads=[bufs["fst"], bufs["fa"]], writes=[bufs["fst"]], tag=f"{tag}_d")
        dve(lambda v: v.tensor_tensor(fscr[:, :], fscr[:, :], fst[:, :], ALU.mult),
            reads=[bufs["fscr"], bufs["fst"]], writes=[bufs["fscr"]], tag=f"{tag}_e")
        dve(lambda v: v.tensor_tensor(fst[:, :], fst[:, :], fscr[:, :], ALU.add),
            reads=[bufs["fst"], bufs["fscr"]], writes=[bufs["fst"]], tag=f"{tag}_f")
        dve(lambda v: v.scalar_tensor_tensor(fst[:, :], fst[:, :], 0.5, fa[:, :],
                                             ALU.mult, ALU.add),
            reads=[bufs["fst"], bufs["fa"]], writes=[bufs["fst"]], tag=f"{tag}_u")
        bc1 = bcast(0, 1, pB[4], f"{tag}_br1")
        bc2 = bcast(2, 3, pB[5], f"{tag}_bs2", neg=True)
        for c in range(C):
            dve(lambda v, c=c: v.tensor_tensor(chunk(fxn, c), chunk(fx0, c),
                                               pbank[4][:, :], ALU.mult),
                reads=[x0b, pB[4]], writes=[bufs["fxn"]], extra=[bc1], tag=f"{tag}_y1{c}")
            dve(lambda v, c=c: v.tensor_tensor(chunk(fxn, c), chunk(fxn, c),
                                               pbank[5][:, :], ALU.add),
                reads=[bufs["fxn"], pB[5]], writes=[bufs["fxn"]], extra=[bc2], tag=f"{tag}_y2{c}")
        dve(lambda v: v.tensor_tensor(fxn[:, :], fxn[:, :], fst[:, :], ALU.add),
            reads=[bufs["fxn"], bufs["fst"]], writes=[bufs["fxn"]], tag=f"{tag}_y3")
        if dbg and k == 0:
            dma("act", dbg_out["d_y"], fxn[:, :], reads=[bufs["fxn"]], tag="dbg_y")

        act(lambda s: s.activation(fst[:, :], fxn[:, :], AF.Square),
            reads=[bufs["fxn"]], writes=[bufs["fst"]], tag=f"{tag}_ysq")
        g1, g2 = ln_stats(bufs["fxn"], fxn, bufs["fst"], fst, f"{tag}_l3")
        ln_rows_mv(g1, g2, f"{tag}_l3")
        rsqrt_row(5, f"{tag}_l3r", src_j=1)
        rmul(0, 0, 5, f"{tag}_l3sa")
        hi_lo(5, 0, 1, 1, f"{tag}_l3h1")
        hi_lo(0, 2, 3, 1, f"{tag}_l3h2")
        bc1 = bcast(0, 1, pB[4], f"{tag}_l3b1")
        bc2 = bcast(2, 3, pB[5], f"{tag}_l3b2", neg=True)
        for c in range(C):
            dve(lambda v, c=c: v.tensor_tensor(chunk(fst, c), chunk(fxn, c),
                                               pbank[4][:, :], ALU.mult),
                reads=[bufs["fxn"], pB[4]], writes=[bufs["fst"]], extra=[bc1], tag=f"{tag}_l3u{c}")
            dve(lambda v, c=c: v.tensor_tensor(chunk(fst, c), chunk(fst, c),
                                               pbank[5][:, :], ALU.add),
                reads=[bufs["fst"], pB[5]], writes=[bufs["fst"]], extra=[bc2], tag=f"{tag}_l3a{c}")
            dve(lambda v, c=c: v.tensor_scalar(chunk(bxn, c), chunk(fst, c),
                                               pv["ffn_g"][:, c:c + 1], pv["ffn_bn"][:, c:c + 1],
                                               ALU.mult, ALU.add),
                reads=[bufs["fst"], pvB["ffn_g"], pvB["ffn_bn"]],
                writes=[bufs["bxn"], bxnH[c // 4]],
                tag=f"{tag}_l3g{c}")

        for mg in range(8):
            slab = stream_mg_slab("ffn1_w", mg)
            b0 = next_quad()
            if mg == 0:
                def em_a(t, slab=slab, b0=b0):
                    last = None
                    for j in range(4):
                        for kk in range(4):
                            last = t.matmul(pbank[b0 + j][:, :],
                                            slab.ap[:, kk * 512 + j * P:kk * 512 + j * P + P],
                                            chunk(bxn, kk),
                                            start=(kk == 0), stop=False,
                                            skip_group_check=True)
                    return last
                pe(em_a, reads=[slab, bxnH[0]],
                   writes=[pB[b0], pB[b0 + 1], pB[b0 + 2], pB[b0 + 3]],
                   tag=f"{tag}_f1_{mg}a")

                def em_b(t, slab=slab, b0=b0):
                    last = None
                    for j in range(4):
                        for kk in range(4, C):
                            last = t.matmul(pbank[b0 + j][:, :],
                                            slab.ap[:, kk * 512 + j * P:kk * 512 + j * P + P],
                                            chunk(bxn, kk),
                                            start=False, stop=(kk == C - 1),
                                            skip_group_check=True)
                    return last
                g = pe(em_b, reads=[slab, bxnH[1]],
                       writes=[pB[b0], pB[b0 + 1], pB[b0 + 2], pB[b0 + 3]],
                       tag=f"{tag}_f1_{mg}b")
            else:
                def em(t, slab=slab, b0=b0):
                    last = None
                    for j in range(4):
                        for kk in range(C):
                            last = t.matmul(pbank[b0 + j][:, :],
                                            slab.ap[:, kk * 512 + j * P:kk * 512 + j * P + P],
                                            chunk(bxn, kk),
                                            start=(kk == 0), stop=(kk == C - 1),
                                            skip_group_check=True)
                    return last
                g = pe(em, reads=[slab, bxnH[0], bxnH[1]],
                       writes=[pB[b0], pB[b0 + 1], pB[b0 + 2], pB[b0 + 3]],
                       tag=f"{tag}_f1_{mg}")
            for j in range(4):
                m = mg * 4 + j
                act(lambda s, m=m, j=j, b0=b0: s.activation(
                    hbuf[:, m * 512:(m + 1) * 512], pbank[b0 + j][:, :], AF.Gelu,
                    bias=pv["ffn1_b"][:, m:m + 1]),
                    reads=[pB[b0 + j], pvB["ffn1_b"]], writes=[hQ[m // 4]], extra=[g],
                    tag=f"{tag}_f1e{m}")

        def f2_epi(m, pb_, g):
            dve(lambda v, m=m, pb_=pb_: v.scalar_tensor_tensor(
                chunk(fa, m), pb_.ap[:, :], pv["ffn2_b"][:, m:m + 1], chunk(fxn, m),
                ALU.add, ALU.add),
                reads=[pb_, pvB["ffn2_b"], bufs["fxn"]], writes=[bufs["fa"]], extra=[g],
                tag=f"{tag}_f2e{m}")
        half_m_kslab("ffn2_w", 32, lambda kk: hbuf[:, kk * 512:(kk + 1) * 512],
                     None, f2_epi, f"{tag}_f2",
                     rhs_bufs_fn=lambda kq: [hQ[kq]])
        dve(lambda v: v.tensor_tensor(fa[:, :], fa[:, :], fx0[:, :], ALU.add),
            reads=[bufs["fa"], x0b], writes=[bufs["fa"]], tag=f"{tag}_x1")
        if dbg and k == 0:
            dma("act", dbg_out["d_x1"], fa[:, :], reads=[bufs["fa"]], tag="dbg_x1")

        act(lambda s: s.activation(fst[:, :], fa[:, :], AF.Square),
            reads=[bufs["fa"]], writes=[bufs["fst"]], tag=f"{tag}_x1sq")
        g1, g2 = ln_stats(bufs["fa"], fa, bufs["fst"], fst, f"{tag}_l4")
        ln_rows_mv(g1, g2, f"{tag}_l4")
        rsqrt_row(5, f"{tag}_l4r", src_j=1)
        rmul(0, 0, 5, f"{tag}_l4sa")
        hi_lo(5, 0, 1, 1, f"{tag}_l4h1")
        hi_lo(0, 2, 3, 1, f"{tag}_l4h2")
        bc1 = bcast(0, 1, pB[4], f"{tag}_l4b1")
        bc2 = bcast(2, 3, pB[5], f"{tag}_l4b2", neg=True)
        for c in range(C):
            dve(lambda v, c=c: v.tensor_tensor(chunk(fst, c), chunk(fa, c),
                                               pbank[4][:, :], ALU.mult),
                reads=[bufs["fa"], pB[4]], writes=[bufs["fst"]], extra=[bc1], tag=f"{tag}_l4u{c}")
            dve(lambda v, c=c: v.tensor_tensor(chunk(blow, c), chunk(fst, c),
                                               pbank[5][:, :], ALU.add),
                reads=[bufs["fst"], pB[5]], writes=[bufs["blow"]], extra=[bc2], tag=f"{tag}_l4a{c}")

        def acc_router():
            for tc in range(4):
                def em_t(t, tc=tc):
                    last = None
                    for cc in range(C):
                        pbk = pbank[4 + cc // 4]
                        last = t.transpose(pbk[:, (cc % 4) * P:(cc % 4 + 1) * P],
                                           fa[:, cc * 512 + tc * P:cc * 512 + tc * P + P],
                                           ident[:, :])
                    return last
                g = pe(em_t, reads=[bufs["fa"], bufs["ident"]], writes=[pB[4], pB[5]],
                       tag=f"{tag}_tp{tc}")
                act(lambda s, tc=tc: s.activation(acc[:, tc * D:tc * D + 512],
                                                  pbank[4][:, :], AF.Copy),
                    reads=[pB[4]], writes=[bufs["acc"]], extra=[g], tag=f"{tag}_tpa{tc}")
                act(lambda s, tc=tc: s.activation(acc[:, tc * D + 512:(tc + 1) * D],
                                                  pbank[5][:, :], AF.Copy),
                    reads=[pB[5]], writes=[bufs["acc"]], extra=[g], tag=f"{tag}_tpb{tc}")

            def em_lg(t):
                last = None
                for tc in range(4):
                    for kk in range(C):
                        last = t.matmul(pbank[6][:, tc * 8:(tc + 1) * 8],
                                        blow[:, kk * 512 + tc * P:kk * 512 + tc * P + P],
                                        wrbf[:, kk * 8:(kk + 1) * 8],
                                        start=(kk == 0), stop=(kk == C - 1),
                                        skip_group_check=True)
                return last
            glg = pe(em_lg, reads=[bufs["blow"], wrB], writes=[pB[6]], tag=f"{tag}_lg")
            for tc in range(4):
                ls = pbank[6][:, tc * 8:(tc + 1) * 8]
                dve(lambda v, ls=ls: v.tensor_reduce(tkt[:, 0:1], ls, AX.X, ALU.max),
                    reads=[pB[6]], writes=[bufs["tkt"]], extra=[glg], tag=f"{tag}_tk0{tc}")
                dve(lambda v, ls=ls: v.tensor_scalar(eqt[:, 0:8], ls, tkt[:, 0:1], None, ALU.is_equal),
                    reads=[pB[6], bufs["tkt"]], writes=[bufs["eqt"]], tag=f"{tag}_tk1{tc}")
                dve(lambda v, ls=ls: v.scalar_tensor_tensor(mskt[:, :], eqt[:, 0:8], -1e9, ls,
                                                            ALU.mult, ALU.add),
                    reads=[bufs["eqt"], pB[6]], writes=[bufs["mskt"]], tag=f"{tag}_tk2{tc}")
                dve(lambda v: v.tensor_reduce(tkt[:, 1:2], mskt[:, :], AX.X, ALU.max),
                    reads=[bufs["mskt"]], writes=[bufs["tkt"]], tag=f"{tag}_tk3{tc}")
                dve(lambda v: v.tensor_tensor(tkt[:, 4:5], tkt[:, 0:1], tkt[:, 1:2], ALU.subtract),
                    reads=[bufs["tkt"]], writes=[bufs["tkt"]], tag=f"{tag}_tk4{tc}")
                act(lambda s: s.activation(tkt[:, 5:6], tkt[:, 4:5], AF.Tanh, scale=0.5),
                    reads=[bufs["tkt"]], writes=[bufs["tkt"]], tag=f"{tag}_tk5{tc}")
                dve(lambda v: v.tensor_scalar(tkt[:, 2:3], tkt[:, 5:6], 0.5, 0.5, ALU.mult, ALU.add),
                    reads=[bufs["tkt"]], writes=[bufs["tkt"]], tag=f"{tag}_tk6{tc}")
                dve(lambda v: v.tensor_scalar(tkt[:, 3:4], tkt[:, 2:3], -1.0, 1.0, ALU.mult, ALU.add),
                    reads=[bufs["tkt"]], writes=[bufs["tkt"]], tag=f"{tag}_tk7{tc}")
                dve(lambda v: v.tensor_scalar(eqt[:, 8:16], mskt[:, :], tkt[:, 1:2], None, ALU.is_equal),
                    reads=[bufs["mskt"], bufs["tkt"]], writes=[bufs["eqt"]], tag=f"{tag}_tk8{tc}")
                dve(lambda v, tc=tc: v.tensor_scalar(comb[:, tc * 8:(tc + 1) * 8], eqt[:, 0:8],
                                                     tkt[:, 2:3], None, ALU.mult),
                    reads=[bufs["eqt"], bufs["tkt"]], writes=[bufs["comb"]], tag=f"{tag}_tk9{tc}")
                dve(lambda v, tc=tc: v.scalar_tensor_tensor(
                    comb[:, tc * 8:(tc + 1) * 8], eqt[:, 8:16], tkt[:, 3:4],
                    comb[:, tc * 8:(tc + 1) * 8], ALU.mult, ALU.add),
                    reads=[bufs["eqt"], bufs["tkt"], bufs["comb"]], writes=[bufs["comb"]],
                    tag=f"{tag}_tkA{tc}")

            def em_ct(t):
                last = None
                for tc in range(4):
                    last = t.transpose(pbank[7][0:NE, tc * P:(tc + 1) * P],
                                       comb[:, tc * 8:(tc + 1) * 8], ident[:, :])
                return last
            gct = pe(em_ct, reads=[bufs["comb"], bufs["ident"]], writes=[pB[7]], tag=f"{tag}_ct")
            act(lambda s: s.activation(combT[:, :], pbank[7][0:NE, :], AF.Copy),
                reads=[pB[7]], writes=[bufs["combT"]], extra=[gct], tag=f"{tag}_ctc")
            for i2, (tc, fh) in enumerate([(a, b) for a in range(4) for b in range(2)]):
                pbk = 6 + (i2 % 2)
                def em_b2(t, tc=tc, fh=fh, pbk=pbk):
                    return t.matmul(pbank[pbk][:, :], combT[0:NE, tc * P:(tc + 1) * P],
                                    b2bf[0:NE, fh * 512:(fh + 1) * 512],
                                    start=True, stop=True)
                g = pe(em_b2, reads=[bufs["combT"], bufs["b2bf"]], writes=[pB[pbk]],
                       tag=f"{tag}_b2_{tc}_{fh}")
                dve(lambda v, tc=tc, fh=fh, pbk=pbk: v.tensor_tensor(
                    acc[:, tc * D + fh * 512:tc * D + (fh + 1) * 512],
                    acc[:, tc * D + fh * 512:tc * D + (fh + 1) * 512],
                    pbank[pbk][:, :], ALU.add),
                    reads=[bufs["acc"], pB[pbk]], writes=[bufs["acc"]], extra=[g],
                    tag=f"{tag}_b2a_{tc}_{fh}")


        if not router_in:
            acc_router()

        def emit_w2_quad(e, fh, hg2, half):
            hap = half[0].ap
            def em_w2(t, hg2=hg2, hap=hap):
                last = None
                for j in range(4):
                    mk = hg2 * 4 + j
                    for tc in range(4):
                        last = t.matmul(
                            pbank[tc][:, :],
                            hbuf[:, mk * 512 + tc * P:mk * 512 + tc * P + P],
                            hap[:, (mk % 8) * 512:(mk % 8 + 1) * 512],
                            start=(mk == 0), stop=(mk == 31),
                            skip_group_check=True)
                return last
            pe(em_w2, reads=[hQ[hg2]] + list(half),
               writes=[pB[0], pB[1], pB[2], pB[3]], tag=f"{tag}_w2_{e}_{fh}_{hg2}")

        for e in range(n_e):
            if pump and e == 2 and k + 1 < n_m:
                front(8 + k + 1, f"M{k + 1}")
            for fh in range(2):
                halves = {}
                for hg in range(8):
                    if hg % 2 == 0:
                        halves[hg // 2] = stream_w2half(e, fh, hg // 2)
                    if fh == 0:
                        slab = stream_mg_slab(f"W1_{e}", hg)
                        for pr in range(2):
                            pb0 = 4 if (hg * 2 + pr) % 2 == 0 else 6
                            def em_w1(t, slab=slab, hg=hg, pr=pr, pb0=pb0):
                                last = None
                                for jj in range(2):
                                    j = pr * 2 + jj
                                    for kk in range(C):
                                        last = t.matmul(
                                            pbank[pb0 + jj][:, :],
                                            slab.ap[:, kk * 512 + j * P:kk * 512 + j * P + P],
                                            chunk(blow, kk),
                                            start=(kk == 0), stop=(kk == C - 1),
                                            skip_group_check=True)
                                return last
                            g = pe(em_w1, reads=[slab, bufs["blow"]],
                                   writes=[pB[pb0], pB[pb0 + 1]], tag=f"{tag}_w1_{e}_{hg}_{pr}")
                            for jj in range(2):
                                mk = hg * 4 + pr * 2 + jj
                                act(lambda s, mk=mk, jj=jj, e=e, pb0=pb0: s.activation(
                                    hbuf[:, mk * 512:(mk + 1) * 512], pbank[pb0 + jj][:, :],
                                    AF.Gelu,
                                    bias=pv["b1"][:, e * 32 + mk:e * 32 + mk + 1]),
                                    reads=[pB[pb0 + jj], pvB["b1"]], writes=[hQ[mk // 4]],
                                    extra=[g], tag=f"{tag}_w1e_{e}_{mk}")
                        if hg >= 1:
                            emit_w2_quad(e, fh, hg - 1, halves[(hg - 1) // 2])
                    else:
                        emit_w2_quad(e, fh, hg, halves[hg // 2])
                if fh == 0:
                    emit_w2_quad(e, fh, 7, halves[3])
                if router_in and e == 0 and fh == 0:
                    acc_router()
                for tc in range(4):
                    dve(lambda v, tc=tc, fh=fh, e=e: v.scalar_tensor_tensor(
                        acc[:, tc * D + fh * 512:tc * D + (fh + 1) * 512],
                        pbank[tc][:, :], comb[:, tc * 8 + e:tc * 8 + e + 1],
                        acc[:, tc * D + fh * 512:tc * D + (fh + 1) * 512],
                        ALU.mult, ALU.add),
                        reads=[pB[tc], bufs["comb"], bufs["acc"]], writes=[bufs["acc"]],
                        tag=f"{tag}_ye_{e}_{fh}_{tc}")

        dma("act", x2[k], acc[:, :], reads=[bufs["acc"]], writes=[x2B], tag=f"{tag}_store")

    S.add("sync", lambda e: e.nop(), extra=[x2B.w] + x2B.rs, tag="drain")

    S.finalize()
    sem_ctx = ExitStack()
    sems = {(q, dm): sem_ctx.enter_context(nc.semaphore(f"sem_{q}{'_d' if dm else ''}"))
            for q in QUEUES for dm in (False, True)}

    def runner(qname):
        def _(eng):
            clock = {(q, dm): 0 for q in QUEUES for dm in (False, True)}
            for st in S.plan[qname]:
                needed = {}
                for d in st.deps:
                    if d.q == qname == "pe" and not d.dma:
                        continue
                    if d.cum is None:
                        continue
                    key = (d.q, d.dma)
                    if d.cum > needed.get(key, 0):
                        needed[key] = d.cum
                for key, v in needed.items():
                    if clock[key] < v:
                        eng.wait_ge(sems[key], v)
                        clock[key] = v
                last = st.emit(eng)
                if st.cum is not None:
                    if last is None:
                        raise RuntimeError(f"step {st.tag} returned no instruction")
                    last.then_inc(sems[(qname, st.dma)], st.inc)
        return _

    with nc.Block() as block:
        block.tensor(runner("pe"))
        block.scalar(runner("act"))
        block.vector(runner("dve"))
        block.gpsimd(runner("pool"))
        block.sync(runner("sync"))

    sbuf_ctx.close()
    psum_ctx.close()
    sem_ctx.close()
    return nc


_IN_NAMES = {
    "fr_w": "L0_attn_fr_w", "dp_w": "L0_attn_dp_w", "bp_w": "L0_attn_bp_w",
    "cp_w": "L0_attn_cp_w", "m1_w": "L0_attn_m1_w", "m2_w": "L0_attn_m2_w",
    "gate_w": "L0_attn_gate_w", "ffn1_w": "L0_attn_ffn1_w", "ffn2_w": "L0_attn_ffn2_w",
    "W1": "L0_ffn_W1", "W2": "L0_ffn_W2", "Wr": "L0_ffn_Wr",
    "norm_g": "L0_attn_norm_g", "norm_b": "L0_attn_norm_b",
    "fr_b": "L0_attn_fr_b", "dp_b": "L0_attn_dp_b", "bp_b": "L0_attn_bp_b",
    "cp_b": "L0_attn_cp_b", "skip": "L0_attn_skip", "m1_b": "L0_attn_m1_b",
    "m2_b": "L0_attn_m2_b", "gate_b": "L0_attn_gate_b",
    "ffn_g": "L0_attn_ffn_g", "ffn_bn": "L0_attn_ffn_bn",
    "ffn1_b": "L0_attn_ffn1_b", "ffn2_b": "L0_attn_ffn2_b",
    "b1": "L0_ffn_b1", "b2": "L0_ffn_b2",
}


def kernel(**inputs):
    x = np.ascontiguousarray(np.asarray(inputs["L0_attn_x"], dtype=np.float32))
    w = {k: np.ascontiguousarray(np.asarray(inputs[v], dtype=np.float32))
         for k, v in _IN_NAMES.items()}
    nc = build()
    in_maps = []
    for c in range(8):
        b, h = c // 2, c % 2
        xb = x[b]
        pref = xb[0:4096].T
        own = xb[h * 4096:(h + 1) * 4096].T
        full = np.concatenate([pref, own], axis=1)
        tiled = full.reshape(8, P, 16, NT).transpose(2, 1, 0, 3).reshape(16, P, C * NT)
        m = dict(w)
        m["xfm"] = np.ascontiguousarray(tiled)
        m["mask"] = np.full((P, 1), float(h), np.float32)
        in_maps.append(m)
    res = run_bass_kernel_spmd(nc, in_maps, core_ids=list(range(8)))
    out = np.empty((4, 8192, 1024), np.float32)
    for c in range(8):
        b, h = c // 2, c % 2
        z = np.asarray(res.results[c]["x2"])
        rows = z.reshape(8, P, 4, D).transpose(0, 2, 1, 3).reshape(4096, D)
        out[b, h * 4096:(h + 1) * 4096, :] = rows
    return out
